# version 16
# baseline (speedup 1.0000x reference)
import numpy as np
import concourse.bass as bass
import concourse.mybir as mybir
from concourse.bass_utils import run_bass_kernel_spmd
from contextlib import ExitStack

F32 = mybir.dt.float32
BF16 = mybir.dt.bfloat16
AF = mybir.ActivationFunctionType
ALU = mybir.AluOpType
AX = mybir.AxisListType

NCORES = 8
N_TOK = 16384
T = N_TOK // NCORES
D = 1024
H = 16
DH = 64
J = 64
NE = 8
DFF = 2048
NT = T // 128
NG = 4
GT = 4
NCH = D // 128

A_NORM = float(DH) ** -0.25
INV_A = float(DH) ** 0.25
COMB_C = 0.0625 * INV_A
EPS_FEAT = 1e-4
RATIO = float(J) ** -0.5
EPS_Q = EPS_FEAT / RATIO
DEN_CLIP = 0.01 / (RATIO * RATIO)
LN_EPS = 1e-5
ROUTER_EPS = 1e-8
CLAMP = 10000.0

DEBUG = False
ENGINES = ("pe", "act", "dve", "pool", "sp")
DMA_Q = {"pool": "dq_pool", "sp": "dq_sp", "act": "dq_act"}


class Prog:

    def __init__(self):
        self.ops = []
        self.last_write = {}
        self.readers = {}
        self.eng_ops = {e: [] for e in ENGINES}
        self.lane_count = {}
        self.lane_ids = []
        self.cc_count = 0

    def _edges_for(self, eng, reads, writes):
        deps = set()
        for b in reads:
            lw = self.last_write.get(id(b))
            if lw is not None:
                o = self.ops[lw]
                if not (o["eng"] == "pe" and eng == "pe" and o["kind"] == "compute"):
                    deps.add(lw)
        for b in writes:
            lw = self.last_write.get(id(b))
            if lw is not None:
                o = self.ops[lw]
                if not (o["eng"] == "pe" and eng == "pe" and o["kind"] == "compute"):
                    deps.add(lw)
            for r in self.readers.get(id(b), ()):
                o = self.ops[r]
                if o["eng"] == eng and o["kind"] == "compute" and eng != "pe":
                    continue
                if o["eng"] == eng and o["kind"] == "compute" and eng == "pe":
                    continue
                deps.add(r)
        return deps

    def barrier(self):
        last = {}
        for i, o in enumerate(self.ops):
            if o["kind"] == "dma":
                last["lane", o["lane"]] = i
            elif o["kind"] == "cc":
                last["cc"] = i
            else:
                last[o["eng"]] = i
        self.barrier_deps = set(last.values())

    def op(self, eng, fn, reads=(), writes=(), kind="compute", tag=""):
        idx = len(self.ops)
        deps = self._edges_for(eng, reads, writes)
        deps |= getattr(self, "barrier_deps", set())
        rec = dict(eng=eng, fn=fn, kind=kind, tag=tag, deps=deps, idx=idx,
                   need_inc=False)
        dep_vals = {}
        for d in deps:
            t = self.ops[d]
            if t["kind"] == "dma":
                ln = t["lane"]
                dep_vals[ln] = 16 * self.lane_count[ln]
        rec["dma_dep_vals"] = dep_vals
        if kind == "dma":
            assert len(writes) >= 1, f"dma op {tag} needs a dest buffer"
            lane = id(writes[0])
            if lane not in self.lane_count:
                self.lane_count[lane] = 0
                self.lane_ids.append(lane)
            self.lane_count[lane] += 1
            rec["lane"] = lane
        elif kind == "cc":
            self.cc_count += 1
            rec["cc_ord"] = self.cc_count
        self.ops.append(rec)
        self.eng_ops[eng].append(idx)
        for b in reads:
            self.readers.setdefault(id(b), []).append(idx)
        for b in writes:
            self.last_write[id(b)] = idx
            self.readers[id(b)] = []
        return idx

    def emit(self, block, sems, lane_sems, cc_sem):
        for o in self.ops:
            for d in o["deps"]:
                t = self.ops[d]
                if t["kind"] == "compute":
                    t["need_inc"] = True
        for e in ENGINES:
            v = 0
            for i in self.eng_ops[e]:
                o = self.ops[i]
                if o["kind"] == "compute" and o["need_inc"]:
                    v += 1
                o["ms"] = v

        def section(eng, handle):
            observed = {}
            for i in self.eng_ops[eng]:
                o = self.ops[i]
                waits = {}
                for ln, val in o["dma_dep_vals"].items():
                    key = ("lane", ln)
                    if val > waits.get(key, 0):
                        waits[key] = val
                for d in sorted(o["deps"]):
                    t = self.ops[d]
                    if t["kind"] == "dma":
                        continue
                    elif t["kind"] == "cc":
                        key, val = "cc", t["cc_ord"]
                    else:
                        key, val = t["eng"], t["ms"]
                    if val > waits.get(key, 0):
                        waits[key] = val
                for key in sorted(waits, key=str):
                    val = waits[key]
                    if observed.get(key, 0) >= val:
                        continue
                    if isinstance(key, tuple):
                        sem = lane_sems[key[1]]
                    elif key == "cc":
                        sem = cc_sem
                    else:
                        sem = sems[key]
                    handle.wait_ge(sem, val)
                    observed[key] = val
                inst = o["fn"](handle)
                if o["kind"] == "dma":
                    inst.then_inc(lane_sems[o["lane"]], 16)
                elif o["kind"] == "cc":
                    inst.then_inc(cc_sem, 1)
                elif o["need_inc"]:
                    inst.then_inc(sems[eng], 1)

        eng2dec = dict(pe=block.tensor, act=block.scalar, dve=block.vector,
                       pool=block.gpsimd, sp=block.sync)
        for e in ENGINES:
            def make(e=e):
                def _(handle):
                    section(e, handle)
                return _
            eng2dec[e](make())


def build_program():
    nc = bass.Bass("TRN2", target_bir_lowering=False)

    d_x = nc.dram_tensor("xq", [T, D], F32, kind="ExternalInput")
    d_wqT = nc.dram_tensor("wqT", [D, D], F32, kind="ExternalInput")
    d_wkT = nc.dram_tensor("wkT", [D, D], F32, kind="ExternalInput")
    d_wvT = nc.dram_tensor("wvT", [D, D], F32, kind="ExternalInput")
    d_woT = nc.dram_tensor("woT", [D, D], F32, kind="ExternalInput")
    d_bq = nc.dram_tensor("bq", [D], F32, kind="ExternalInput")
    d_bk = nc.dram_tensor("bk", [D], F32, kind="ExternalInput")
    d_bv = nc.dram_tensor("bv", [D], F32, kind="ExternalInput")
    d_bo = nc.dram_tensor("bo", [D], F32, kind="ExternalInput")
    d_projbd = nc.dram_tensor("projbd", [NCH, 128, 128], F32, kind="ExternalInput")
    d_gamma = nc.dram_tensor("gamma", [D], F32, kind="ExternalInput")
    d_beta = nc.dram_tensor("beta", [D], F32, kind="ExternalInput")
    d_rwT = nc.dram_tensor("rwT", [D, NE], F32, kind="ExternalInput")
    d_guT = nc.dram_tensor("guT", [NE, D, 16, 256], F32, kind="ExternalInput")
    d_dnT = nc.dram_tensor("dnT", [NE, DFF, D], F32, kind="ExternalInput")
    d_idf = nc.dram_tensor("idf", [128, 128], F32, kind="ExternalInput")
    d_riota = nc.dram_tensor("riota", [128, GT * NE], F32, kind="ExternalInput")
    d_e2 = nc.dram_tensor("e2", [NCH, 128, H], F32, kind="ExternalInput")
    d_ebc = nc.dram_tensor("ebc", [NCH, H, 128], F32, kind="ExternalInput")

    d_out = nc.dram_tensor("out", [T, D], F32, kind="ExternalOutput")

    d_y1 = nc.dram_tensor("y1d", [T, D], F32)
    d_ddk = nc.dram_tensor("ddkd", [NT, 128, D], BF16)
    d_xnT = nc.dram_tensor("xnTd", [NCH, 128, T], BF16)
    d_skd = nc.dram_tensor("skd", [128, H], F32)
    d_cc1i = nc.dram_tensor("cc1i", [1, H], F32)
    d_cc1o = nc.dram_tensor("cc1o", [1, H], F32)
    d_cc2i = nc.dram_tensor("cc2i", [J, H * 65], F32)
    d_cc2o = nc.dram_tensor("cc2o", [J, H * 65], F32)

    P = Prog()
    ctx = ExitStack()
    taps = {}

    def tap(name, buf, ap_in):
        if not DEBUG:
            return
        shp = list(ap_in.shape)
        dt_ = ap_in.dtype
        dtap = nc.dram_tensor("tap_" + name, shp, dt_, kind="ExternalOutput")
        taps[name] = dtap

        def fn(h):
            return h.dma_start(out=dtap.ap(), in_=ap_in)
        P.op("sp", fn, reads=[buf], writes=[dtap], kind="dma", tag="tap_" + name)

    def sbt(st, name, shape, dtype):
        return st.enter_context(nc.sbuf_tensor(name, shape, dtype))

    idf = sbt(ctx, "s_idf", [128, 128], F32)
    idb = sbt(ctx, "s_idb", [128, 128], BF16)
    riota = sbt(ctx, "s_riota", [128, GT * NE], F32)
    epst = sbt(ctx, "epst", [128, 1], F32)
    onesb = sbt(ctx, "onesb", [128, 128], F32)
    gateT = sbt(ctx, "gateT", [NE, T], F32)
    mu1 = sbt(ctx, "mu1", [128, NT], F32)
    rs1 = sbt(ctx, "rs1", [128, NT], F32)
    stabgb = sbt(ctx, "stabgb", [128, H], F32)
    pf = [ctx.enter_context(nc.psum_tensor(f"pf{i}", [128, 512], F32))
          for i in range(6)]
    pb = [ctx.enter_context(nc.psum_tensor(f"pb{i}", [128, 1024], BF16))
          for i in range(2)]

    stabR = sbt(ctx, "stabR", [1, H], F32)
    sA = ExitStack()
    sK = ExitStack()
    sB = ExitStack()
    qpT = [sbt(sA, f"qpT{c}", [128, T], BF16) for c in range(NCH)]
    kvbd = [sbt(sK, f"kvbd{c}", [128, 128], BF16) for c in range(NCH)]
    ksl = [sbt(sK, f"ksl{c}", [128, H], BF16) for c in range(NCH)]
    dgk = [sbt(sB, f"dgk{t}", [128, H], F32) for t in range(NT)]
    vtk = [sbt(sB, f"vtk{t}", [128, H * 65], BF16) for t in range(NT)]

    def pe_fence(bufs, tag):
        P.op("pe", lambda t: t.matmul(pf[5][0:1, 496:497], onesb[0:1, 0:1],
                                      onesb[0:1, 0:1], start=True, stop=True),
             reads=[onesb], writes=list(bufs), tag=f"fence_{tag}")

    def dma(eng, out_ap, in_ap, reads=(), writes=(), tag="", slow=False):
        def fn(h):
            if slow:
                with nc.allow_non_contiguous_dma(reason="small"):
                    return h.dma_start(out=out_ap, in_=in_ap)
            return h.dma_start(out=out_ap, in_=in_ap)
        return P.op(eng, fn, reads=reads, writes=writes, kind="dma", tag=tag)

    dma("sp", idf[:, :], d_idf.ap(), writes=[idf], tag="idf")
    dma("pool", idb[:, :], d_idf.ap(), writes=[idb], tag="idb")
    dma("sp", riota[:, :], d_riota.ap(), writes=[riota], tag="riota")
    P.op("dve", lambda v: v.memset(epst[:, :], LN_EPS), writes=[epst], tag="eps")
    P.op("dve", lambda v: v.memset(onesb[:, :], 1.0), writes=[onesb], tag="ones")

    def bcast_row(row, dst, tag):
        for hf in range(2):
            s = hf * 512
            P.op("pe", lambda t, s=s, row=row: t.matmul(
                pf[5][:, :], onesb[0:1, :], row[0:1, s:s + 512],
                start=True, stop=True), reads=[onesb, row], writes=[pf[5]],
                tag=f"bc_{tag}{hf}")
            P.op("dve", lambda v, s=s, dst=dst: v.tensor_copy(
                dst[:, s:s + 512], pf[5][:, :]), reads=[pf[5]], writes=[dst],
                tag=f"bce_{tag}{hf}")


    p1 = ExitStack()
    e2c = sbt(p1, "e2c", [128, NCH * H], F32)
    bq_sb = sbt(p1, "bq_sb", [128, NCH], F32)
    bk_sb = sbt(p1, "bk_sb", [128, NCH], F32)
    row_bv = sbt(p1, "row_bv", [1, D], F32)
    bvb = sbt(p1, "bvb", [128, D], F32)
    dma("sp", e2c.ap().rearrange("p (c h) -> p c h", c=NCH),
        d_e2.ap().rearrange("c p h -> p c h"), writes=[e2c], tag="e2")
    dma("sp", bq_sb[:, :], d_bq.ap().rearrange("(c p) -> p c", p=128),
        writes=[bq_sb], tag="bqc", slow=True)
    dma("sp", bk_sb[:, :], d_bk.ap().rearrange("(c p) -> p c", p=128),
        writes=[bk_sb], tag="bkc", slow=True)
    dma("sp", row_bv[:, :], d_bv.ap().unsqueeze(0), writes=[row_bv], tag="rbv")
    bcast_row(row_bv, bvb, "bv")
    wq = sbt(p1, "wq", [128, NCH * D], BF16)
    wk = sbt(p1, "wk", [128, NCH * D], BF16)
    wv = sbt(p1, "wv", [128, NCH * D], BF16)
    pjb = sbt(p1, "pjb", [128, NCH * 128], BF16)
    xinT = [sbt(p1, f"xinT{c}", [128, 512], BF16) for c in range(NCH)]
    xg = [sbt(p1, f"xg{i}", [128, D], F32) for i in range(2)]
    xtk = [sbt(p1, f"xtk{i}", [128, D], BF16) for i in range(2)]
    bn6 = [sbt(p1, f"bn6_{i}", [128, 2, 6], F32) for i in range(2)]
    bn2 = [sbt(p1, f"bn2_{i}", [128, 2], F32) for i in range(2)]
    rstd = [sbt(p1, f"rstd{i}", [128, 1], F32) for i in range(2)]
    pT = [sbt(p1, f"pT{c}", [128, 512], BF16) for c in range(NCH)]
    kT = [sbt(p1, f"kT{c}", [128, 512], BF16) for c in range(NCH)]
    sqt = [sbt(p1, f"sqt{i}", [128, 128], F32) for i in range(2)]
    dgq_tok = [sbt(p1, f"dgq_tok{i}", [128, H], F32) for i in range(2)]
    stq = [sbt(p1, f"stq{i}", [128, H], F32) for i in range(2)]
    comb = [sbt(p1, f"comb{i}", [128, H], F32) for i in range(2)]
    subf = [sbt(p1, f"subf{i}", [128, D], F32) for i in range(2)]
    qptk = [sbt(p1, f"qptk{i}", [128, D], BF16) for i in range(2)]
    qdd = [sbt(p1, f"qdd{i}", [128, D], BF16) for i in range(2)]
    stk_part = [sbt(p1, f"stk_part{i}", [128, H], F32) for i in range(2)]
    ddst = [sbt(p1, f"ddst{i}", [128, D], BF16) for i in range(2)]
    stk_run = sbt(p1, "stk_run", [128, H], F32)

    def w_load(dst, src, tag):
        dma("pool", dst.ap().rearrange("p (c n) -> p c n", c=NCH),
            src.ap().rearrange("(c p) n -> p c n", p=128),
            writes=[dst], tag=tag)

    w_load(wq, d_wqT, "wq")
    w_load(wk, d_wkT, "wk")
    w_load(wv, d_wvT, "wv")
    dma("pool", pjb.ap().rearrange("p (c n) -> p c n", c=NCH),
        d_projbd.ap().rearrange("c p n -> p c n"), writes=[pjb], tag="pjb")

    def ln_tile(ti, src_buf, dst_tok, dst_col, tag):
        par = ti % 2
        b6, b2, rs = bn6[par], bn2[par], rstd[par]
        P.op("dve", lambda v: v.bn_stats(b6[:, 0, :], src_buf[:, 0:512]),
             reads=[src_buf], writes=[b6], tag=f"{tag}s0_{ti}")
        P.op("dve", lambda v: v.bn_stats(b6[:, 1, :], src_buf[:, 512:1024]),
             reads=[src_buf], writes=[b6], tag=f"{tag}s1_{ti}")
        P.op("dve", lambda v: v.bn_aggr(b2[:, :], b6[:, :, :]),
             reads=[b6], writes=[b2], tag=f"{tag}ag_{ti}")
        P.op("act", lambda a: a.activation(rs[:, :], b2[:, 1:2], AF.Ln,
                                           bias=epst[:, 0:1]),
             reads=[b2, epst], writes=[rs], tag=f"{tag}sq_{ti}")
        P.op("act", lambda a: a.activation(rs[:, :], rs[:, 0:1], AF.Exp,
                                           scale=-0.5),
             reads=[rs], writes=[rs], tag=f"{tag}rc_{ti}")
        P.op("dve", lambda v: v.tensor_scalar(
            dst_tok[:, :], src_buf[:, :], b2[:, 0:1], rs[:, 0:1],
            ALU.subtract, ALU.mult),
            reads=[src_buf, b2, rs], writes=[dst_tok], tag=f"{tag}nm_{ti}")
        for c in range(NCH):
            P.op("pe", lambda t, c=c: t.transpose(
                pb[0][:, c * 128:(c + 1) * 128],
                dst_tok[:, c * 128:(c + 1) * 128], idb[:, :]),
                reads=[dst_tok, idb], writes=[pb[0]], tag=f"{tag}tr_{ti}_{c}")
        for c in range(NCH):
            P.op("act", lambda a, c=c: a.copy(
                xinT[c][:, dst_col:dst_col + 128],
                pb[0][:, c * 128:(c + 1) * 128]),
                reads=[pb[0]], writes=[xinT[c]], tag=f"{tag}te_{ti}_{c}")

    def proj_fmajor(wsb, bsb, dstT, tag):
        for m in range(NCH):
            bank = pf[m % 2]
            for k in range(NCH):
                P.op("pe", lambda t, m=m, k=k, bank=bank: t.matmul(
                    bank[:, :], wsb[:, k * D + m * 128: k * D + (m + 1) * 128],
                    xinT[k][:, :], start=(k == 0), stop=(k == NCH - 1)),
                    reads=[wsb, xinT[k]], writes=[bank], tag=f"{tag}m{m}_{k}")
            P.op("act", lambda a, m=m, bank=bank: a.activation(
                dstT[m][:, :], bank[:, :], AF.Identity, bias=bsb[:, m:m + 1]),
                reads=[bank, bsb], writes=[dstT[m]], tag=f"{tag}e{m}")

    def proj_v(g, tag):
        for ts in range(GT):
            ti = g * GT + ts
            for ob in range(2):
                bank = pf[(ts * 2 + ob) % 2]
                for k in range(NCH):
                    P.op("pe", lambda t, ts=ts, ob=ob, k=k, bank=bank: t.matmul(
                        bank[:, :], xinT[k][:, ts * 128:(ts + 1) * 128],
                        wv[:, k * D + ob * 512: k * D + (ob + 1) * 512],
                        start=(k == 0), stop=(k == NCH - 1)),
                        reads=[wv, xinT[k]], writes=[bank],
                        tag=f"{tag}m{ts}_{ob}_{k}")
                P.op("dve", lambda v, ti=ti, ob=ob, bank=bank: v.tensor_add(
                    vtk[ti].ap().rearrange("p (h j) -> p h j", j=65)
                    [:, ob * 8:(ob + 1) * 8, 0:64],
                    bank.ap().rearrange("p (h j) -> p h j", j=64),
                    bvb.ap().rearrange("p (h j) -> p h j", j=64)
                    [:, ob * 8:(ob + 1) * 8, :]),
                    reads=[bank, bvb], writes=[vtk[ti]], tag=f"{tag}e{ts}_{ob}")
            P.op("dve", lambda v, ti=ti: v.memset(
                vtk[ti].ap().rearrange("p (h j) -> p h j", j=65)[:, :, 64:65],
                1.0), writes=[vtk[ti]], tag=f"{tag}o{ti}")

    def features(side, g, tag):
        srcT = pT if side == "q" else kT
        for ts in range(GT):
            ti = g * GT + ts
            par = ts % 2
            bkA, bkB = (pf[2], pf[3]) if par == 0 else (pf[0], pf[1])
            bkD = pf[4] if par == 0 else pf[5]
            sq = sqt[par]
            ddsb = qdd[par] if side == "q" else ddst[par]
            for c in range(NCH):
                bank = bkA if c < 4 else bkB
                col = (c % 4) * 128
                P.op("pe", lambda t, c=c, ts=ts, bank=bank, col=col: t.matmul(
                    bank[:, col:col + 128], srcT[c][:, ts * 128:(ts + 1) * 128],
                    pjb[:, c * 128:(c + 1) * 128], start=True, stop=True),
                    reads=[srcT[c], pjb], writes=[bank], tag=f"{tag}d{ts}_{c}")
            for hf in range(2):
                bank = bkA if hf == 0 else bkB
                P.op("act", lambda a, hf=hf, bank=bank, ddsb=ddsb: a.copy(
                    ddsb[:, hf * 512:(hf + 1) * 512], bank[:, 0:512]),
                    reads=[bank], writes=[ddsb], tag=f"{tag}w{ts}_{hf}")
            for c in range(NCH):
                P.op("act", lambda a, c=c, ts=ts, sq=sq: a.activation(
                    sq[:, :], srcT[c][:, ts * 128:(ts + 1) * 128], AF.Square),
                    reads=[srcT[c]], writes=[sq], tag=f"{tag}q{ts}_{c}")
                P.op("pe", lambda t, c=c, sq=sq, bkD=bkD: t.matmul(
                    bkD[:, 0:H], sq[:, :], e2c[:, c * H:(c + 1) * H],
                    start=(c == 0), stop=(c == NCH - 1)),
                    reads=[e2c, sq], writes=[bkD], tag=f"{tag}g{ts}_{c}")
            dst_dg = dgq_tok[par] if side == "q" else dgk[ti]
            P.op("dve", lambda v, dst_dg=dst_dg, bkD=bkD: v.tensor_copy(
                dst_dg[:, :], bkD[:, 0:H]),
                reads=[bkD], writes=[dst_dg], tag=f"{tag}g2{ts}")
            stdst = stq[par] if side == "q" else stk_part[par]
            P.op("dve", lambda v, stdst=stdst, ddsb=ddsb: v.reduce_max(
                stdst[:, :], ddsb.ap().rearrange("p (h j) -> p h j", j=64),
                axis=AX.X),
                reads=[ddsb], writes=[stdst], tag=f"{tag}s{ts}")
            if side == "q":
                P.op("dve", lambda v, par=par: v.scalar_tensor_tensor(
                    comb[par][:, :], dgq_tok[par][:, :], COMB_C, stq[par][:, :],
                    ALU.mult, ALU.add),
                    reads=[dgq_tok[par], stq[par]], writes=[comb[par]],
                    tag=f"{tag}c{ts}")
                P.op("dve", lambda v, par=par, ddsb=ddsb: v.tensor_tensor(
                    subf[par].ap().rearrange("p (h j) -> p h j", j=64),
                    ddsb.ap().rearrange("p (h j) -> p h j", j=64),
                    comb[par].ap().unsqueeze(2).broadcast_to([128, H, 64]),
                    ALU.subtract),
                    reads=[ddsb, comb[par]], writes=[subf[par]],
                    tag=f"{tag}u{ts}")
                P.op("act", lambda a, par=par: a.activation(
                    qptk[par][:, :], subf[par][:, :], AF.Exp, scale=A_NORM),
                    reads=[subf[par]], writes=[qptk[par]], tag=f"{tag}x{ts}")
                P.op("dve", lambda v, par=par: v.tensor_scalar_add(
                    qptk[par][:, :], qptk[par][:, :], EPS_Q),
                    reads=[qptk[par]], writes=[qptk[par]], tag=f"{tag}p{ts}")
                pbt = pb[par]
                for c in range(NCH):
                    P.op("pe", lambda t, c=c, par=par, pbt=pbt: t.transpose(
                        pbt[:, c * 128:(c + 1) * 128],
                        qptk[par][:, c * 128:(c + 1) * 128], idb[:, :]),
                        reads=[qptk[par], idb], writes=[pbt],
                        tag=f"{tag}t{ts}_{c}")
                for c in range(NCH):
                    P.op("act", lambda a, c=c, ti=ti, pbt=pbt: a.copy(
                        qpT[c][:, ti * 128:(ti + 1) * 128],
                        pbt[:, c * 128:(c + 1) * 128]),
                        reads=[pbt], writes=[qpT[c]], tag=f"{tag}y{ts}_{c}")
            else:
                dma("sp", d_ddk[ti, :, :], ddsb[:, :], reads=[ddsb],
                    writes=[d_ddk], tag=f"{tag}wd{ts}")
                P.op("dve", lambda v, par=par: v.tensor_max(
                    stk_run[:, :], stk_run[:, :], stk_part[par][:, :]),
                    reads=[stk_run, stk_part[par]], writes=[stk_run],
                    tag=f"{tag}tm{ts}")

    P.op("dve", lambda v: v.memset(stk_run[:, :], -1e30), writes=[stk_run],
         tag="skz")

    def ln_group(g):
        for ts in range(GT):
            ti = g * GT + ts
            xb, xt = xg[ts % 2], xtk[ts % 2]
            dma("sp", xb[:, :], d_x[ti * 128:(ti + 1) * 128, :],
                writes=[xb], tag=f"ldx{ti}")
            ln_tile(ti, xb, xt, ts * 128, "ln")

    for g in range(NG):
        if g == 0:
            ln_group(0)
        proj_fmajor(wq, bq_sb, pT, f"pq{g}")
        proj_fmajor(wk, bk_sb, kT, f"pk{g}")
        proj_v(g, f"pv{g}")
        if g == 0:
            tap("qT0", pT[0], pT[0][:, :])
            tap("vtk0", vtk[0], vtk[0][:, :])
        if g + 1 < NG:
            ln_group(g + 1)
        features("q", g, f"fq{g}")
        if g == 0:
            tap("qpT0", qpT[0], qpT[0][:, 0:512])
        features("k", g, f"fk{g}")
        if g == 0:
            tap("dgk0", dgk[0], dgk[0][:, :])

    skrow = sbt(p1, "skrow", [1, 64 * H], F32)
    stab2 = sbt(p1, "stab2", [1, H], F32)
    dma("sp", d_skd.ap(), stk_run[:, :], reads=[stk_run], writes=[d_skd],
        tag="skst")
    for hf in range(2):
        dma("sp", skrow.ap(),
            d_skd[hf * 64:(hf + 1) * 64, :].rearrange("t h -> (t h)").unsqueeze(0),
            reads=[d_skd], writes=[skrow], tag=f"skfl{hf}")
        dst = stabR if hf == 0 else stab2
        P.op("dve", lambda v, dst=dst: v.reduce_max(
            dst[:, :], skrow.ap().rearrange("one (t h) -> one h t", t=64),
            axis=AX.X), reads=[skrow], writes=[dst], tag=f"sktr{hf}")
    P.op("dve", lambda v: v.tensor_max(stabR[:, :], stabR[:, :], stab2[:, :]),
         reads=[stabR, stab2], writes=[stabR], tag="sktm2")
    tap("stk_run", stk_run, stk_run[:, :])
    tap("stabL", stabR, stabR[:, :])
    dma("sp", d_cc1i.ap(), stabR[:, :], reads=[stabR], writes=[d_cc1i],
        tag="cc1w")
    P.op("pool", lambda gp: gp.collective_compute(
        "AllReduce", ALU.max, replica_groups=[list(range(NCORES))],
        ins=[d_cc1i.ap()], outs=[d_cc1o.ap()]),
        reads=[d_cc1i], writes=[d_cc1o], kind="cc", tag="AR1")
    row_st = sbt(p1, "row_st", [1, H], F32)
    dma("sp", row_st[:, :], d_cc1o.ap(), reads=[d_cc1o], writes=[row_st],
        tag="cc1r")
    P.op("pe", lambda t: t.matmul(pf[5][:, 0:H], onesb[0:1, :], row_st[0:1, :],
                                  start=True, stop=True),
         reads=[onesb, row_st], writes=[pf[5]], tag="stbm")
    P.op("dve", lambda v: v.tensor_copy(stabgb[:, :], pf[5][:, 0:H]),
         reads=[pf[5]], writes=[stabgb], tag="stbe")
    tap("stabgb", stabgb, stabgb[0:1, :])

    p1.close()
    P.barrier()

    p2 = ExitStack()
    combk = [sbt(p2, f"combk{i}", [128, H], F32) for i in range(2)]
    kddl = [sbt(p2, f"kddl{i}", [128, D], BF16) for i in range(2)]
    e1 = [sbt(p2, f"e1_{i}", [128, D], F32) for i in range(2)]
    e2t = [sbt(p2, f"e2t{i}", [128, D], BF16) for i in range(2)]
    kpt = [sbt(p2, f"kpt{t}", [128, D], BF16) for t in range(NT)]
    ar2st = sbt(p2, "ar2st", [J, H * 65], F32)
    kvs = sbt(p2, "kvs", [J, H * 65], F32)

    for ti in range(NT):
        par = ti % 2
        kdd = kddl[par]
        dma("sp", kdd[:, :], d_ddk[ti, :, :], reads=[d_ddk], writes=[kdd],
            tag=f"kld{ti}")
        P.op("dve", lambda v, ti=ti, par=par: v.scalar_tensor_tensor(
            combk[par][:, :], dgk[ti][:, :], COMB_C, stabgb[:, :],
            ALU.mult, ALU.add),
            reads=[dgk[ti], stabgb], writes=[combk[par]], tag=f"kc{ti}")
        P.op("dve", lambda v, par=par, kdd=kdd: v.tensor_tensor(
            e1[par].ap().rearrange("p (h j) -> p h j", j=64),
            kdd.ap().rearrange("p (h j) -> p h j", j=64),
            combk[par].ap().unsqueeze(2).broadcast_to([128, H, 64]),
            ALU.subtract),
            reads=[kdd, combk[par]], writes=[e1[par]], tag=f"ku{ti}")
        P.op("act", lambda a, par=par: a.activation(e2t[par][:, :],
                                                    e1[par][:, :], AF.Exp,
                                                    scale=A_NORM),
             reads=[e1[par]], writes=[e2t[par]], tag=f"kx{ti}")
        P.op("dve", lambda v, ti=ti, par=par: v.tensor_scalar_add(
            kpt[ti][:, :], e2t[par][:, :], EPS_Q),
            reads=[e2t[par]], writes=[kpt[ti]], tag=f"kp{ti}")

    def ubank(h):
        if h < 12:
            return pf[h // 2], (h % 2) * 130
        return pf[h - 12], 260

    for h in range(H):
        bank, col = ubank(h)
        for ti in range(NT):
            P.op("pe", lambda t, ti=ti, h=h, bank=bank, col=col: t.matmul(
                bank[0:J, col:col + 65], kpt[ti][:, h * 64:(h + 1) * 64],
                vtk[ti][:, h * 65:(h + 1) * 65],
                start=(ti == 0), stop=(ti == NT - 1)),
                reads=[kpt[ti], vtk[ti]], writes=[bank], tag=f"U{h}_{ti}")
    for h in range(H):
        bank, col = ubank(h)
        P.op("dve", lambda v, h=h, bank=bank, col=col: v.tensor_copy(
            ar2st[:, h * 65:(h + 1) * 65], bank[0:J, col:col + 65]),
            reads=[bank], writes=[ar2st], tag=f"Ue{h}")
    dma("sp", d_cc2i.ap(), ar2st[:, :], reads=[ar2st], writes=[d_cc2i],
        tag="cc2w")
    P.op("pool", lambda gp: gp.collective_compute(
        "AllReduce", ALU.add, replica_groups=[list(range(NCORES))],
        ins=[d_cc2i.ap()], outs=[d_cc2o.ap()]),
        reads=[d_cc2i], writes=[d_cc2o], kind="cc", tag="AR2")
    dma("sp", kvs[:, :], d_cc2o.ap(), reads=[d_cc2o], writes=[kvs], tag="cc2r")
    tap("kpt0", kpt[0], kpt[0][:, :])
    tap("ar2st", ar2st, ar2st[:, :])
    tap("kvs", kvs, kvs[:, :])

    for c in range(NCH):
        P.op("dve", lambda v, c=c: v.memset(kvbd[c][:, :], 0.0),
             writes=[kvbd[c]], tag=f"kz{c}")
        P.op("dve", lambda v, c=c: v.memset(ksl[c][:, :], 0.0),
             writes=[ksl[c]], tag=f"sz{c}")
        for s in range(2):
            h = 2 * c + s
            P.op("dve", lambda v, c=c, s=s, h=h: v.tensor_copy(
                kvbd[c][s * 64:(s + 1) * 64, s * 64:(s + 1) * 64],
                kvs[:, h * 65:h * 65 + 64]),
                reads=[kvs], writes=[kvbd[c]], tag=f"kc{c}_{s}")
            P.op("dve", lambda v, c=c, s=s, h=h: v.tensor_copy(
                ksl[c][s * 64:(s + 1) * 64, h:h + 1],
                kvs[:, h * 65 + 64:(h + 1) * 65]),
                reads=[kvs], writes=[ksl[c]], tag=f"sc{c}_{s}")

    p2.close()
    P.barrier()
    sB.close()
    P.barrier()

    p3 = ExitStack()
    ebcc = sbt(p3, "ebcc", [16, NCH * 128], F32)
    row_bo = sbt(p3, "row_bo", [1, D], F32)
    row_g = sbt(p3, "row_g", [1, D], F32)
    row_b = sbt(p3, "row_b", [1, D], F32)
    bob = sbt(p3, "bob", [128, D], F32)
    gmb = sbt(p3, "gmb", [128, D], F32)
    btb = sbt(p3, "btb", [128, D], F32)
    dma("sp", ebcc.ap().rearrange("p (c n) -> p c n", c=NCH),
        d_ebc.ap().rearrange("c p n -> p c n"), writes=[ebcc], tag="ebc")
    dma("sp", row_bo[:, :], d_bo.ap().unsqueeze(0), writes=[row_bo], tag="rbo")
    dma("sp", row_g[:, :], d_gamma.ap().unsqueeze(0), writes=[row_g], tag="rg")
    dma("sp", row_b[:, :], d_beta.ap().unsqueeze(0), writes=[row_b], tag="rb")
    bcast_row(row_bo, bob, "bo")
    bcast_row(row_g, gmb, "g")
    bcast_row(row_b, btb, "b")
    wo = sbt(p3, "wo", [128, NCH * D], BF16)
    w_load(wo, d_woT, "wo")
    rw = sbt(p3, "rw", [128, NCH * NE], BF16)
    dma("pool", rw.ap().rearrange("p (c n) -> p c n", c=NCH),
        d_rwT.ap().rearrange("(c p) n -> p c n", p=128), writes=[rw], tag="rw")

    for c in range(NCH):
        tap(f"ksl{c}", ksl[c], ksl[c][:, :])
    tap("kvbd0", kvbd[0], kvbd[0][:, :])
    tap("kvbd1", kvbd[1], kvbd[1][:, :])
    attnT = [sbt(p3, f"attnT{c}", [128, 512], BF16) for c in range(NCH)]
    xnTg = [sbt(p3, f"xnTg{c}", [128, 512], BF16) for c in range(NCH)]
    aog = [sbt(p3, f"aog{i}", [128, D], F32) for i in range(GT)]
    y1g = [sbt(p3, f"y1g{i}", [128, D], F32) for i in range(GT)]
    bn2g = sbt(p3, "bn2g", [128, GT, 2], F32)
    bn2h = sbt(p3, "bn2h", [128, GT, 2], F32)
    rstdg = sbt(p3, "rstdg", [128, GT], F32)
    rs1g = sbt(p3, "rs1g", [128, GT], F32)
    v2g = sbt(p3, "v2g", [128, GT], F32)
    rstd2g = sbt(p3, "rstd2g", [128, GT], F32)
    rp = sbt(p3, "rp", [128, GT * NE], F32)
    m1g = sbt(p3, "m1g", [128, GT], F32)
    sumg = sbt(p3, "sumg", [128, GT], F32)
    eqg = sbt(p3, "eqg", [128, GT * NE], F32)
    eqg2 = sbt(p3, "eqg2", [128, GT * NE], F32)
    sc1g = sbt(p3, "sc1g", [128, GT * NE], F32)
    s1mg = sbt(p3, "s1mg", [128, GT], F32)
    f1g = sbt(p3, "f1g", [128, GT * NE], F32)
    mskg = sbt(p3, "mskg", [128, GT * NE], F32)
    gvg = sbt(p3, "gvg", [128, GT * NE], F32)
    rnorm = sbt(p3, "rnorm", [16, 512], F32)
    rbf = sbt(p3, "rbf", [128, 512], F32)
    zt = [sbt(p3, f"zt{i}", [128, D], F32) for i in range(GT)]
    z2junk = sbt(p3, "z2junk", [128, D], F32)
    xntk = [sbt(p3, f"xntk{i}", [128, D], BF16) for i in range(2)]
    xtmp = [sbt(p3, f"xtmp{i}", [128, D], F32) for i in range(2)]
    bn6b = [sbt(p3, f"bn6b{i}", [128, 2, 6], F32) for i in range(2)]
    xg3 = [sbt(p3, f"xg3_{i}", [128, D], F32) for i in range(GT)]

    for g in range(NG):
        for c in range(NCH):
            P.op("pe", lambda t, c=c, g=g: t.matmul(
                pf[2][0:16, :], ksl[c][:, :], qpT[c][:, g * 512:(g + 1) * 512],
                start=(c == 0), stop=(c == NCH - 1)),
                reads=[ksl[c], qpT[c]], writes=[pf[2]], tag=f"n{g}_{c}")
        P.op("dve", lambda v: v.tensor_scalar_max(rnorm[:, :], pf[2][0:16, :],
                                                  DEN_CLIP),
             reads=[pf[2]], writes=[rnorm], tag=f"nx{g}")
        if g == 0:
            tap("normraw", rnorm, rnorm[:, :])
        P.op("dve", lambda v: v.reciprocal(rnorm[:, :], rnorm[:, :]),
             reads=[rnorm], writes=[rnorm], tag=f"nr{g}")
        for c in range(NCH):
            P.op("pe", lambda t, c=c, g=g: t.matmul(
                pf[3][:, :], kvbd[c][:, :], qpT[c][:, g * 512:(g + 1) * 512],
                start=True, stop=True),
                reads=[kvbd[c], qpT[c]], writes=[pf[3]], tag=f"qk{g}_{c}")
            P.op("pe", lambda t, c=c: t.matmul(
                pf[4][:, :], ebcc[:, c * 128:(c + 1) * 128], rnorm[0:16, :],
                start=True, stop=True),
                reads=[ebcc, rnorm], writes=[pf[4]], tag=f"rb{g}_{c}")
            P.op("dve", lambda v: v.tensor_copy(rbf[:, :], pf[4][:, :]),
                 reads=[pf[4]], writes=[rbf], tag=f"re{g}_{c}")
            P.op("dve", lambda v, c=c: v.tensor_mul(attnT[c][:, :], pf[3][:, :],
                                                    rbf[:, :]),
                 reads=[pf[3], rbf], writes=[attnT[c]], tag=f"at{g}_{c}")
        if g == 0:
            tap("rnorm", rnorm, rnorm[:, :])
            tap("attnT0", attnT[0], attnT[0][:, :])
            tap("attnT7", attnT[7], attnT[7][:, :])
        for ts in range(GT):
            ti = g * GT + ts
            for ob in range(2):
                bank = pf[(ts % 2) * 2 + ob]
                for k in range(NCH):
                    P.op("pe", lambda t, ts=ts, ob=ob, k=k, bank=bank: t.matmul(
                        bank[:, :], attnT[k][:, ts * 128:(ts + 1) * 128],
                        wo[:, k * D + ob * 512: k * D + (ob + 1) * 512],
                        start=(k == 0), stop=(k == NCH - 1)),
                        reads=[attnT[k], wo], writes=[bank],
                        tag=f"w{ti}_{ob}_{k}")
                P.op("dve", lambda v, ts=ts, ob=ob, bank=bank: v.tensor_add(
                    aog[ts][:, ob * 512:(ob + 1) * 512], bank[:, :],
                    bob[:, ob * 512:(ob + 1) * 512]),
                    reads=[bank, bob], writes=[aog[ts]], tag=f"we{ti}_{ob}")
            dma("sp", xg3[ts][:, :], d_x[ti * 128:(ti + 1) * 128, :],
                writes=[xg3[ts]], tag=f"lx3{ti}")
        for ts in range(GT):
            b6 = bn6b[ts % 2]
            P.op("dve", lambda v, ts=ts, b6=b6: v.bn_stats(
                b6[:, 0, :], aog[ts][:, 0:512]),
                reads=[aog[ts]], writes=[b6], tag=f"l1a{g}_{ts}")
            P.op("dve", lambda v, ts=ts, b6=b6: v.bn_stats(
                b6[:, 1, :], aog[ts][:, 512:1024]),
                reads=[aog[ts]], writes=[b6], tag=f"l1b{g}_{ts}")
            P.op("dve", lambda v, ts=ts, b6=b6: v.bn_aggr(
                bn2g[:, ts, :], b6[:, :, :]),
                reads=[b6], writes=[bn2g], tag=f"l1c{g}_{ts}")
        P.op("act", lambda a: a.activation(
            rstdg[:, :], bn2g[:, :, 1], AF.Ln, bias=epst[:, 0:1]),
            reads=[bn2g, epst], writes=[rstdg], tag=f"l1d{g}")
        P.op("act", lambda a: a.activation(rstdg[:, :], rstdg[:, :], AF.Exp,
                                           scale=-0.5),
             reads=[rstdg], writes=[rstdg], tag=f"l1e{g}")
        for ts in range(GT):
            ti = g * GT + ts
            P.op("dve", lambda v, ts=ts: v.tensor_scalar(
                y1g[ts][:, :], aog[ts][:, :], bn2g[:, ts, 0:1],
                rstdg[:, ts:ts + 1], ALU.subtract, ALU.mult),
                reads=[aog[ts], bn2g, rstdg], writes=[y1g[ts]],
                tag=f"l1n{ti}")
            P.op("dve", lambda v, ts=ts: v.tensor_add(
                y1g[ts][:, :], y1g[ts][:, :], xg3[ts][:, :]),
                reads=[y1g[ts], xg3[ts]], writes=[y1g[ts]], tag=f"y1{ti}")
            dma("sp", d_y1[ti * 128:(ti + 1) * 128, :], y1g[ts][:, :],
                reads=[y1g[ts]], writes=[d_y1], tag=f"sy1{ti}")
        for ts in range(GT):
            b6 = bn6b[ts % 2]
            P.op("dve", lambda v, ts=ts, b6=b6: v.bn_stats(
                b6[:, 0, :], y1g[ts][:, 0:512]),
                reads=[y1g[ts]], writes=[b6], tag=f"l2a{g}_{ts}")
            P.op("dve", lambda v, ts=ts, b6=b6: v.bn_stats(
                b6[:, 1, :], y1g[ts][:, 512:1024]),
                reads=[y1g[ts]], writes=[b6], tag=f"l2b{g}_{ts}")
            P.op("dve", lambda v, ts=ts, b6=b6: v.bn_aggr(
                bn2h[:, ts, :], b6[:, :, :]),
                reads=[b6], writes=[bn2h], tag=f"l2c{g}_{ts}")
        P.op("act", lambda a: a.activation(
            rs1g[:, :], bn2h[:, :, 1], AF.Ln, bias=epst[:, 0:1]),
            reads=[bn2h, epst], writes=[rs1g], tag=f"l2d{g}")
        P.op("act", lambda a: a.activation(rs1g[:, :], rs1g[:, :], AF.Exp,
                                           scale=-0.5),
             reads=[rs1g], writes=[rs1g], tag=f"l2e{g}")
        P.op("dve", lambda v, g=g: v.tensor_copy(
            mu1[:, g * GT:(g + 1) * GT], bn2h[:, :, 0]),
            reads=[bn2h], writes=[mu1], tag=f"m1s{g}")
        P.op("dve", lambda v, g=g: v.tensor_copy(
            rs1[:, g * GT:(g + 1) * GT], rs1g[:, :]),
            reads=[rs1g], writes=[rs1], tag=f"r1s{g}")
        for ts in range(GT):
            ti = g * GT + ts
            par = ts % 2
            P.op("dve", lambda v, ts=ts: v.tensor_scalar(
                zt[ts][:, :], y1g[ts][:, :], bn2h[:, ts, 0:1],
                rs1g[:, ts:ts + 1], ALU.subtract, ALU.mult),
                reads=[y1g[ts], bn2h, rs1g], writes=[zt[ts]], tag=f"z{ti}")
            P.op("act", lambda a, ts=ts: a.activation(
                z2junk[:, :], zt[ts][:, :], AF.Square,
                accum_out=v2g[:, ts:ts + 1]),
                reads=[zt[ts]], writes=[z2junk, v2g], tag=f"v2{ti}")
        P.op("act", lambda a: a.activation(rstd2g[:, :], v2g[:, :], AF.Ln,
                                           bias=epst[:, 0:1], scale=1.0 / D),
             reads=[v2g, epst], writes=[rstd2g], tag=f"r2{g}")
        P.op("act", lambda a: a.activation(rstd2g[:, :], rstd2g[:, :], AF.Exp,
                                           scale=-0.5),
             reads=[rstd2g], writes=[rstd2g], tag=f"r2r{g}")
        for ts in range(GT):
            ti = g * GT + ts
            par = ts % 2
            P.op("dve", lambda v, ts=ts, par=par: v.scalar_tensor_tensor(
                xtmp[par][:, :], zt[ts][:, :], rstd2g[:, ts:ts + 1], gmb[:, :],
                ALU.mult, ALU.mult),
                reads=[zt[ts], rstd2g, gmb], writes=[xtmp[par]], tag=f"xa{ti}")
            P.op("dve", lambda v, par=par: v.tensor_add(
                xntk[par][:, :], xtmp[par][:, :], btb[:, :]),
                reads=[xtmp[par], btb], writes=[xntk[par]], tag=f"xb{ti}")
            pbt = pb[par]
            for c in range(NCH):
                P.op("pe", lambda t, c=c, par=par, pbt=pbt: t.transpose(
                    pbt[:, c * 128:(c + 1) * 128],
                    xntk[par][:, c * 128:(c + 1) * 128], idb[:, :]),
                    reads=[xntk[par], idb], writes=[pbt], tag=f"xt{ti}_{c}")
            for c in range(NCH):
                P.op("act", lambda a, c=c, ts=ts, pbt=pbt: a.copy(
                    xnTg[c][:, ts * 128:(ts + 1) * 128],
                    pbt[:, c * 128:(c + 1) * 128]),
                    reads=[pbt], writes=[xnTg[c]], tag=f"xe{ti}_{c}")
            for c in range(NCH):
                P.op("pe", lambda t, c=c, ts=ts: t.matmul(
                    pf[5][:, ts * NE:(ts + 1) * NE],
                    xnTg[c][:, ts * 128:(ts + 1) * 128],
                    rw[:, c * NE:(c + 1) * NE],
                    start=(c == 0), stop=(c == NCH - 1)),
                    reads=[xnTg[c], rw], writes=[pf[5]], tag=f"rt{ti}_{c}")
        rp3 = rp.ap().rearrange("p (t e) -> p t e", e=NE)
        P.op("dve", lambda v: v.tensor_scalar(
            rp[:, :], pf[5][:, 0:GT * NE], CLAMP, -CLAMP, ALU.min, ALU.max),
            reads=[pf[5]], writes=[rp], tag=f"lg{g}")
        P.op("dve", lambda v: v.reduce_max(m1g[:, :], rp3, axis=AX.X),
             reads=[rp], writes=[m1g], tag=f"lm{g}")
        P.op("dve", lambda v: v.tensor_tensor(
            rp3, rp3, m1g.ap().unsqueeze(2).broadcast_to([128, GT, NE]),
            ALU.subtract),
            reads=[rp, m1g], writes=[rp], tag=f"ls{g}")
        P.op("act", lambda a: a.activation(rp[:, :], rp[:, :], AF.Exp),
             reads=[rp], writes=[rp], tag=f"px{g}")
        P.op("dve", lambda v: v.reduce_sum(sumg[:, :], rp3, axis=AX.X),
             reads=[rp], writes=[sumg], tag=f"p0{g}")
        P.op("dve", lambda v: v.tensor_scalar_add(sumg[:, :], sumg[:, :],
                                                  ROUTER_EPS),
             reads=[sumg], writes=[sumg], tag=f"p1{g}")
        P.op("dve", lambda v: v.reciprocal(sumg[:, :], sumg[:, :]),
             reads=[sumg], writes=[sumg], tag=f"p2{g}")
        P.op("dve", lambda v: v.tensor_tensor(
            rp3, rp3, sumg.ap().unsqueeze(2).broadcast_to([128, GT, NE]),
            ALU.mult),
            reads=[rp, sumg], writes=[rp], tag=f"p3{g}")
        P.op("dve", lambda v: v.tensor_scalar(
            rp[:, :], rp[:, :], 1.0, ROUTER_EPS, ALU.min, ALU.max),
            reads=[rp], writes=[rp], tag=f"p4{g}")

        def top2_sel(dst_mask, srcbuf, tag2):
            s3 = srcbuf.ap().rearrange("p (t e) -> p t e", e=NE)
            P.op("dve", lambda v: v.reduce_max(m1g[:, :], s3, axis=AX.X),
                 reads=[srcbuf], writes=[m1g], tag=f"{tag2}a")
            P.op("dve", lambda v: v.tensor_tensor(
                eqg.ap().rearrange("p (t e) -> p t e", e=NE), s3,
                m1g.ap().unsqueeze(2).broadcast_to([128, GT, NE]),
                ALU.is_equal),
                reads=[srcbuf, m1g], writes=[eqg], tag=f"{tag2}b")
            P.op("dve", lambda v: v.tensor_mul(sc1g[:, :], eqg[:, :],
                                               riota[:, :]),
                 reads=[eqg, riota], writes=[sc1g], tag=f"{tag2}c")
            P.op("dve", lambda v: v.reduce_max(
                s1mg[:, :], sc1g.ap().rearrange("p (t e) -> p t e", e=NE),
                axis=AX.X),
                reads=[sc1g], writes=[s1mg], tag=f"{tag2}d")
            P.op("dve", lambda v: v.tensor_tensor(
                dst_mask.ap().rearrange("p (t e) -> p t e", e=NE),
                sc1g.ap().rearrange("p (t e) -> p t e", e=NE),
                s1mg.ap().unsqueeze(2).broadcast_to([128, GT, NE]),
                ALU.is_equal),
                reads=[sc1g, s1mg], writes=[dst_mask], tag=f"{tag2}e")

        top2_sel(f1g, rp, f"a{g}_")
        P.op("dve", lambda v: v.scalar_tensor_tensor(
            mskg[:, :], f1g[:, :], -1e30, rp[:, :], ALU.mult, ALU.add),
            reads=[f1g, rp], writes=[mskg], tag=f"mk{g}")
        top2_sel(eqg2, mskg, f"b{g}_")
        P.op("dve", lambda v: v.tensor_add(f1g[:, :], f1g[:, :], eqg2[:, :]),
             reads=[f1g, eqg2], writes=[f1g], tag=f"fm{g}")
        P.op("dve", lambda v: v.tensor_mul(gvg[:, :], rp[:, :], f1g[:, :]),
             reads=[rp, f1g], writes=[gvg], tag=f"gv{g}")
        if g == 0:
            tap("gvals0", gvg, gvg[:, 0:NE])
            tap("aog0", aog[0], aog[0][:, :])
            tap("y1g0", y1g[0], y1g[0][:, :])
            tap("bn2g", bn2g, bn2g[:, :, :])
            tap("rstdg", rstdg, rstdg[:, :])
            tap("v2g", v2g, v2g[:, :])
            tap("rp", rp, rp[:, :])
            tap("xnTg0", xnTg[0], xnTg[0][:, :])
        for ts in range(GT):
            ti = g * GT + ts
            P.op("pe", lambda t, ts=ts: t.transpose(
                pf[5][0:NE, 256:256 + 128], gvg[:, ts * NE:(ts + 1) * NE],
                idf[:, :]),
                reads=[gvg, idf], writes=[pf[5]], tag=f"gt{ti}")
            P.op("act", lambda a, ti=ti: a.copy(
                gateT[:, ti * 128:(ti + 1) * 128], pf[5][0:NE, 256:256 + 128]),
                reads=[pf[5]], writes=[gateT], tag=f"ge{ti}")
        for c in range(NCH):
            dma("sp", d_xnT[c, :, g * 512:(g + 1) * 512], xnTg[c][:, :],
                reads=[xnTg[c]], writes=[d_xnT], tag=f"sxn{g}_{c}")

    p3.close()
    sK.close()
    sA.close()
    P.barrier()

    p4 = ExitStack()
    TH = T // 2
    guw = [sbt(p4, f"guw{i}", [128, NCH * 256], BF16) for i in range(4)]
    xnTh = [sbt(p4, f"xnTh{c}", [128, T // 2], BF16) for c in range(NCH)]
    dnw = [sbt(p4, f"dnw{i}", [128, 16 * 512], BF16) for i in range(2)]
    grow = [sbt(p4, f"grow{i}", [1, TH], F32) for i in range(2)]
    gbt = sbt(p4, "gbt", [128, TH], F32)
    xet = [sbt(p4, f"xet{c}", [128, TH], BF16) for c in range(NCH)]
    hid = [sbt(p4, f"hid{c}", [128, TH], BF16) for c in range(16)]
    acc = [sbt(p4, f"acc{i}", [128, D], F32) for i in range(8)]
    s1t = [sbt(p4, f"s1t{i}", [128, 512], BF16) for i in range(2)]
    y1l = sbt(p4, "y1l", [128, D], F32)
    zt4 = sbt(p4, "zt4", [128, D], F32)
    fr4 = sbt(p4, "fr4", [128, D], F32)
    bn64 = sbt(p4, "bn64", [128, 2, 6], F32)
    bn24 = sbt(p4, "bn24", [128, 2], F32)
    rstd4 = sbt(p4, "rstd4", [128, 1], F32)
    y2t = sbt(p4, "y2t", [128, D], F32)

    def fin_tile(half, tt):
        ti = half * 8 + tt
        dma("sp", y1l[:, :], d_y1[ti * 128:(ti + 1) * 128, :],
            reads=[d_y1], writes=[y1l], tag=f"ly{ti}")
        P.op("dve", lambda v, ti=ti: v.tensor_scalar(
            zt4[:, :], y1l[:, :], mu1[:, ti:ti + 1], rs1[:, ti:ti + 1],
            ALU.subtract, ALU.mult),
            reads=[y1l, mu1, rs1], writes=[zt4], tag=f"fz{ti}")
        P.op("dve", lambda v, tt=tt: v.tensor_add(fr4[:, :], zt4[:, :],
                                                  acc[tt][:, :]),
             reads=[zt4, acc[tt]], writes=[fr4], tag=f"ff{ti}")
        P.op("dve", lambda v: v.bn_stats(bn64[:, 0, :], fr4[:, 0:512]),
             reads=[fr4], writes=[bn64], tag=f"f0{ti}")
        P.op("dve", lambda v: v.bn_stats(bn64[:, 1, :], fr4[:, 512:1024]),
             reads=[fr4], writes=[bn64], tag=f"f1{ti}")
        P.op("dve", lambda v: v.bn_aggr(bn24[:, :], bn64[:, :, :]),
             reads=[bn64], writes=[bn24], tag=f"f2{ti}")
        P.op("act", lambda a: a.activation(rstd4[:, :], bn24[:, 1:2],
                                           AF.Ln, bias=epst[:, 0:1]),
             reads=[bn24, epst], writes=[rstd4], tag=f"f3{ti}")
        P.op("act", lambda a: a.activation(rstd4[:, :], rstd4[:, 0:1],
                                           AF.Exp, scale=-0.5),
             reads=[rstd4], writes=[rstd4], tag=f"f4{ti}")
        P.op("dve", lambda v: v.tensor_scalar(
            fr4[:, :], fr4[:, :], bn24[:, 0:1], rstd4[:, 0:1],
            ALU.subtract, ALU.mult),
            reads=[fr4, bn24, rstd4], writes=[fr4], tag=f"f5{ti}")
        P.op("dve", lambda v: v.tensor_add(y2t[:, :], y1l[:, :], fr4[:, :]),
             reads=[y1l, fr4], writes=[y2t], tag=f"f6{ti}")
        dma("sp", d_out[ti * 128:(ti + 1) * 128, :], y2t[:, :],
            reads=[y2t], writes=[d_out], tag=f"so{ti}")

    for half in range(2):
        hc = half * TH
        for c in range(NCH):
            dma("sp", xnTh[c][:, :], d_xnT[c, :, hc:hc + TH],
                reads=[d_xnT], writes=[xnTh[c]], tag=f"lxn{half}_{c}")
        for e in range(NE):
            for pbi in range(4):
                wgu = guw[pbi % 4]
                dma("pool", wgu.ap().rearrange("p (c n) -> p c n", c=NCH),
                    d_guT[e, :, pbi, :].rearrange("(c p) n -> p c n", p=128),
                    writes=[wgu], tag=f"lg{half}_{e}_{pbi}")
            gr = grow[e % 2]
            dma("sp", gr[:, :], gateT[e:e + 1, hc:hc + TH], reads=[gateT],
                writes=[gr], tag=f"gr{half}_{e}")
            for gs in range(2):
                P.op("pe", lambda t, gr=gr, gs=gs: t.matmul(
                    pf[4][:, :], onesb[0:1, :], gr[0:1, gs * 512:(gs + 1) * 512],
                    start=True, stop=True),
                    reads=[onesb, gr], writes=[pf[4]], tag=f"gb{half}_{e}_{gs}")
                P.op("act", lambda a, gs=gs: a.copy(
                    gbt[:, gs * 512:(gs + 1) * 512], pf[4][:, :]),
                    reads=[pf[4]], writes=[gbt], tag=f"gc{half}_{e}_{gs}")
            for c in range(NCH):
                P.op("pool", lambda g_, c=c: g_.tensor_mul(
                    xet[c][:, :], xnTh[c][:, :], gbt[:, :]),
                    reads=[xnTh[c], gbt], writes=[xet[c]],
                    tag=f"x{half}_{e}_{c}")
            if half == 0 and e == 0:
                tap("gbt0", gbt, gbt[:, :])
                tap("xet0", xet[0], xet[0][:, :])
            for pbi in range(16):
                wgu = guw[pbi % 4]
                for gs in range(2):
                    bg = pf[gs * 2]
                    bu = pf[gs * 2 + 1]
                    for k in range(NCH):
                        P.op("pe", lambda t, k=k, wgu=wgu, gs=gs, bg=bg: t.matmul(
                            bg[:, :], wgu[:, k * 256:k * 256 + 128],
                            xet[k][:, gs * 512:(gs + 1) * 512],
                            start=(k == 0), stop=(k == NCH - 1)),
                            reads=[wgu, xet[k]], writes=[bg],
                            tag=f"mg{half}_{e}_{pbi}_{gs}_{k}")
                    for k in range(NCH):
                        P.op("pe", lambda t, k=k, wgu=wgu, gs=gs, bu=bu: t.matmul(
                            bu[:, :], wgu[:, k * 256 + 128:(k + 1) * 256],
                            xet[k][:, gs * 512:(gs + 1) * 512],
                            start=(k == 0), stop=(k == NCH - 1)),
                            reads=[wgu, xet[k]], writes=[bu],
                            tag=f"mu{half}_{e}_{pbi}_{gs}_{k}")
                    st = s1t[gs]
                    P.op("act", lambda a, st=st, bg=bg: a.activation(
                        st[:, :], bg[:, :], AF.Silu),
                        reads=[bg], writes=[st], tag=f"si{half}_{e}_{pbi}_{gs}")
                    P.op("dve", lambda v, st=st, bu=bu, pbi=pbi, gs=gs:
                         v.tensor_mul(hid[pbi][:, gs * 512:(gs + 1) * 512],
                                      st[:, :], bu[:, :]),
                         reads=[st, bu], writes=[hid[pbi]],
                         tag=f"hm{half}_{e}_{pbi}_{gs}")
                if pbi + 4 < 16:
                    nxt = guw[(pbi + 4) % 4]
                    dma("pool", nxt.ap().rearrange("p (c n) -> p c n", c=NCH),
                        d_guT[e, :, pbi + 4, :]
                        .rearrange("(c p) n -> p c n", p=128),
                        writes=[nxt], tag=f"lg{half}_{e}_{pbi + 4}")
            for ob in range(2):
                dma("pool", dnw[ob].ap().rearrange("p (c n) -> p c n", c=16),
                    d_dnT[e, :, ob * 512:(ob + 1) * 512]
                    .rearrange("(c p) n -> p c n", p=128),
                    writes=[dnw[ob]], tag=f"ld{half}_{e}_{ob}")
            for tt in range(8):
                for ob in range(2):
                    bank = pf[4 + ob]
                    for k in range(16):
                        P.op("pe", lambda t, tt=tt, ob=ob, k=k, bank=bank:
                             t.matmul(bank[:, :],
                                      hid[k][:, tt * 128:(tt + 1) * 128],
                                      dnw[ob][:, k * 512:(k + 1) * 512],
                                      start=(k == 0), stop=(k == 15)),
                             reads=[hid[k], dnw[ob]], writes=[bank],
                             tag=f"md{half}_{e}_{tt}_{ob}_{k}")
                    if e == 0:
                        P.op("dve", lambda v, tt=tt, ob=ob, bank=bank:
                             v.tensor_copy(acc[tt][:, ob * 512:(ob + 1) * 512],
                                           bank[:, :]),
                             reads=[bank], writes=[acc[tt]],
                             tag=f"d0{half}_{tt}_{ob}")
                    else:
                        P.op("dve", lambda v, tt=tt, ob=ob, bank=bank:
                             v.tensor_add(acc[tt][:, ob * 512:(ob + 1) * 512],
                                          acc[tt][:, ob * 512:(ob + 1) * 512],
                                          bank[:, :]),
                             reads=[bank, acc[tt]], writes=[acc[tt]],
                             tag=f"da{half}_{e}_{tt}_{ob}")
                if e == NE - 1:
                    fin_tile(half, tt)
        if half == 0:
            tap("hid0", hid[0], hid[0][:, :])
            tap("acc0", acc[0], acc[0][:, :])

    p4.close()

    lane_totals = dict(P.lane_count)
    ccn = P.cc_count

    def make_final(lane_sems, cc_sem):
        def fn(h):
            last = None
            for ln in P.lane_ids:
                last = h.wait_ge(lane_sems[ln], 16 * lane_totals[ln])
            if ccn:
                last = h.wait_ge(cc_sem, ccn)
            return last
        return fn

    sem_ctx = ExitStack()
    s_pe = sem_ctx.enter_context(nc.semaphore("s_pe"))
    s_act = sem_ctx.enter_context(nc.semaphore("s_act"))
    s_dve = sem_ctx.enter_context(nc.semaphore("s_dve"))
    s_pool = sem_ctx.enter_context(nc.semaphore("s_pool"))
    s_sp = sem_ctx.enter_context(nc.semaphore("s_sp"))
    cc_sem = sem_ctx.enter_context(nc.semaphore("ccs"))
    lane_sems = {ln: sem_ctx.enter_context(nc.semaphore(f"lane{i}"))
                 for i, ln in enumerate(P.lane_ids)}
    with nc.Block() as block:
        sems = dict(pe=s_pe, act=s_act, dve=s_dve, pool=s_pool, sp=s_sp)
        fin = make_final(lane_sems, cc_sem)
        P.ops.append(dict(eng="sp", fn=fin, kind="compute", tag="final",
                          deps=set(), idx=len(P.ops), need_inc=False,
                          dma_dep_vals={}))
        P.eng_ops["sp"].append(len(P.ops) - 1)
        P.emit(block, sems, lane_sems, cc_sem)
    sem_ctx.close()

    ctx.close()
    return nc


_NC_CACHE = None


def _get_nc():
    global _NC_CACHE
    if _NC_CACHE is None:
        _NC_CACHE = build_program()
    return _NC_CACHE


def kernel(L0_attn_query, L0_attn_Wq, L0_attn_bq, L0_attn_Wk, L0_attn_bk,
           L0_attn_Wv, L0_attn_bv, L0_attn_Wo, L0_attn_bo, L0_attn_proj,
           L0_ffn_ln_gamma, L0_ffn_ln_beta, L0_ffn_router_w,
           L0_ffn_gate_up_w, L0_ffn_down_w):
    x = np.ascontiguousarray(np.asarray(L0_attn_query, dtype=np.float32)
                             .reshape(N_TOK, D))
    wqT = np.ascontiguousarray(np.asarray(L0_attn_Wq, np.float32).T)
    wkT = np.ascontiguousarray(np.asarray(L0_attn_Wk, np.float32).T)
    wvT = np.ascontiguousarray(np.asarray(L0_attn_Wv, np.float32).T)
    woT = np.ascontiguousarray(np.asarray(L0_attn_Wo, np.float32).T)
    proj = np.asarray(L0_attn_proj, np.float32)
    projbd = np.zeros((NCH, 128, 128), np.float32)
    for c in range(NCH):
        projbd[c, 0:64, 0:64] = proj[2 * c].T
        projbd[c, 64:128, 64:128] = proj[2 * c + 1].T
    rwT = np.ascontiguousarray(np.asarray(L0_ffn_router_w, np.float32).T)
    _gu = np.swapaxes(np.asarray(L0_ffn_gate_up_w, np.float32), 1, 2)
    _g = _gu[:, :, 0:DFF].reshape(NE, D, 16, 128)
    _u = _gu[:, :, DFF:].reshape(NE, D, 16, 128)
    guT = np.ascontiguousarray(np.concatenate([_g, _u], axis=3))
    dnT = np.ascontiguousarray(np.swapaxes(
        np.asarray(L0_ffn_down_w, np.float32), 1, 2))
    idf = np.eye(128, dtype=np.float32)
    riota = np.broadcast_to(np.tile(np.arange(NE, 0, -1, dtype=np.float32), 4),
                            (128, 4 * NE)).copy()
    e2 = np.zeros((NCH, 128, H), np.float32)
    ebc = np.zeros((NCH, H, 128), np.float32)
    for c in range(NCH):
        e2[c, 0:64, 2 * c] = 1.0
        e2[c, 64:128, 2 * c + 1] = 1.0
        ebc[c, 2 * c, 0:64] = 1.0
        ebc[c, 2 * c + 1, 64:128] = 1.0

    shared = dict(
        wqT=wqT, wkT=wkT, wvT=wvT, woT=woT,
        bq=np.asarray(L0_attn_bq, np.float32),
        bk=np.asarray(L0_attn_bk, np.float32),
        bv=np.asarray(L0_attn_bv, np.float32),
        bo=np.asarray(L0_attn_bo, np.float32),
        projbd=projbd,
        gamma=np.asarray(L0_ffn_ln_gamma, np.float32),
        beta=np.asarray(L0_ffn_ln_beta, np.float32),
        rwT=rwT, guT=guT, dnT=dnT, idf=idf, riota=riota, e2=e2, ebc=ebc,
    )
    in_maps = []
    for c in range(NCORES):
        m = dict(shared)
        m["xq"] = np.ascontiguousarray(x[c * T:(c + 1) * T, :])
        in_maps.append(m)

    nc = _get_nc()
    res = run_bass_kernel_spmd(nc, in_maps, core_ids=list(range(NCORES)))
    out = np.concatenate([res.results[c]["out"] for c in range(NCORES)], axis=0)
    return out.reshape(1, N_TOK, D)


# version 17
# speedup vs baseline: 1.0529x; 1.0529x over previous
import numpy as np
import concourse.bass as bass
import concourse.mybir as mybir
from concourse.bass_utils import run_bass_kernel_spmd
from contextlib import ExitStack

F32 = mybir.dt.float32
BF16 = mybir.dt.bfloat16
AF = mybir.ActivationFunctionType
ALU = mybir.AluOpType
AX = mybir.AxisListType

NCORES = 8
N_TOK = 16384
T = N_TOK // NCORES
D = 1024
H = 16
DH = 64
J = 64
NE = 8
DFF = 2048
NT = T // 128
NG = 4
GT = 4
NCH = D // 128

A_NORM = float(DH) ** -0.25
INV_A = float(DH) ** 0.25
COMB_C = 0.0625 * INV_A
EPS_FEAT = 1e-4
RATIO = float(J) ** -0.5
EPS_Q = EPS_FEAT / RATIO
DEN_CLIP = 0.01 / (RATIO * RATIO)
LN_EPS = 1e-5
ROUTER_EPS = 1e-8
CLAMP = 10000.0

DEBUG = False
ENGINES = ("pe", "act", "dve", "pool", "sp")
DMA_Q = {"pool": "dq_pool", "sp": "dq_sp", "act": "dq_act"}


class Prog:

    def __init__(self):
        self.ops = []
        self.last_write = {}
        self.readers = {}
        self.eng_ops = {e: [] for e in ENGINES}
        self.lane_count = {}
        self.lane_ids = []
        self.cc_count = 0

    def _edges_for(self, eng, reads, writes):
        deps = set()
        for b in reads:
            lw = self.last_write.get(id(b))
            if lw is not None:
                o = self.ops[lw]
                if not (o["eng"] == "pe" and eng == "pe" and o["kind"] == "compute"):
                    deps.add(lw)
        for b in writes:
            lw = self.last_write.get(id(b))
            if lw is not None:
                o = self.ops[lw]
                if not (o["eng"] == "pe" and eng == "pe" and o["kind"] == "compute"):
                    deps.add(lw)
            for r in self.readers.get(id(b), ()):
                o = self.ops[r]
                if o["eng"] == eng and o["kind"] == "compute" and eng != "pe":
                    continue
                if o["eng"] == eng and o["kind"] == "compute" and eng == "pe":
                    continue
                deps.add(r)
        return deps

    def barrier(self):
        last = {}
        for i, o in enumerate(self.ops):
            if o["kind"] == "dma":
                last["lane", o["lane"]] = i
            elif o["kind"] == "cc":
                last["cc"] = i
            else:
                last[o["eng"]] = i
        self.barrier_deps = set(last.values())

    def op(self, eng, fn, reads=(), writes=(), kind="compute", tag=""):
        idx = len(self.ops)
        deps = self._edges_for(eng, reads, writes)
        deps |= getattr(self, "barrier_deps", set())
        rec = dict(eng=eng, fn=fn, kind=kind, tag=tag, deps=deps, idx=idx,
                   need_inc=False)
        dep_vals = {}
        for d in deps:
            t = self.ops[d]
            if t["kind"] == "dma":
                ln = t["lane"]
                dep_vals[ln] = 16 * self.lane_count[ln]
        rec["dma_dep_vals"] = dep_vals
        if kind == "dma":
            assert len(writes) >= 1, f"dma op {tag} needs a dest buffer"
            lane = id(writes[0])
            if lane not in self.lane_count:
                self.lane_count[lane] = 0
                self.lane_ids.append(lane)
            self.lane_count[lane] += 1
            rec["lane"] = lane
        elif kind == "cc":
            self.cc_count += 1
            rec["cc_ord"] = self.cc_count
        self.ops.append(rec)
        self.eng_ops[eng].append(idx)
        for b in reads:
            self.readers.setdefault(id(b), []).append(idx)
        for b in writes:
            self.last_write[id(b)] = idx
            self.readers[id(b)] = []
        return idx

    def emit(self, block, sems, lane_sems, cc_sem):
        for o in self.ops:
            for d in o["deps"]:
                t = self.ops[d]
                if t["kind"] == "compute":
                    t["need_inc"] = True
        for e in ENGINES:
            v = 0
            for i in self.eng_ops[e]:
                o = self.ops[i]
                if o["kind"] == "compute" and o["need_inc"]:
                    v += 1
                o["ms"] = v

        def section(eng, handle):
            observed = {}
            for i in self.eng_ops[eng]:
                o = self.ops[i]
                waits = {}
                for ln, val in o["dma_dep_vals"].items():
                    key = ("lane", ln)
                    if val > waits.get(key, 0):
                        waits[key] = val
                for d in sorted(o["deps"]):
                    t = self.ops[d]
                    if t["kind"] == "dma":
                        continue
                    elif t["kind"] == "cc":
                        key, val = "cc", t["cc_ord"]
                    else:
                        key, val = t["eng"], t["ms"]
                    if val > waits.get(key, 0):
                        waits[key] = val
                for key in sorted(waits, key=str):
                    val = waits[key]
                    if observed.get(key, 0) >= val:
                        continue
                    if isinstance(key, tuple):
                        sem = lane_sems[key[1]]
                    elif key == "cc":
                        sem = cc_sem
                    else:
                        sem = sems[key]
                    handle.wait_ge(sem, val)
                    observed[key] = val
                inst = o["fn"](handle)
                if o["kind"] == "dma":
                    inst.then_inc(lane_sems[o["lane"]], 16)
                elif o["kind"] == "cc":
                    inst.then_inc(cc_sem, 1)
                elif o["need_inc"]:
                    inst.then_inc(sems[eng], 1)

        eng2dec = dict(pe=block.tensor, act=block.scalar, dve=block.vector,
                       pool=block.gpsimd, sp=block.sync)
        for e in ENGINES:
            def make(e=e):
                def _(handle):
                    section(e, handle)
                return _
            eng2dec[e](make())


def build_program():
    nc = bass.Bass("TRN2", target_bir_lowering=False)

    d_x = nc.dram_tensor("xq", [T, D], F32, kind="ExternalInput")
    d_wqT = nc.dram_tensor("wqT", [D, D], F32, kind="ExternalInput")
    d_wkT = nc.dram_tensor("wkT", [D, D], F32, kind="ExternalInput")
    d_wvT = nc.dram_tensor("wvT", [D, D], F32, kind="ExternalInput")
    d_woT = nc.dram_tensor("woT", [D, D], F32, kind="ExternalInput")
    d_bq = nc.dram_tensor("bq", [D], F32, kind="ExternalInput")
    d_bk = nc.dram_tensor("bk", [D], F32, kind="ExternalInput")
    d_bv = nc.dram_tensor("bv", [D], F32, kind="ExternalInput")
    d_bo = nc.dram_tensor("bo", [D], F32, kind="ExternalInput")
    d_projbd = nc.dram_tensor("projbd", [NCH, 128, 128], F32, kind="ExternalInput")
    d_gamma = nc.dram_tensor("gamma", [D], F32, kind="ExternalInput")
    d_beta = nc.dram_tensor("beta", [D], F32, kind="ExternalInput")
    d_rwT = nc.dram_tensor("rwT", [D, NE], F32, kind="ExternalInput")
    d_guT = nc.dram_tensor("guT", [NE, D, 16, 256], F32, kind="ExternalInput")
    d_dnT = nc.dram_tensor("dnT", [NE, DFF, D], F32, kind="ExternalInput")
    d_idf = nc.dram_tensor("idf", [128, 128], F32, kind="ExternalInput")
    d_riota = nc.dram_tensor("riota", [128, GT * NE], F32, kind="ExternalInput")
    d_e2 = nc.dram_tensor("e2", [NCH, 128, H], F32, kind="ExternalInput")
    d_ebc = nc.dram_tensor("ebc", [NCH, H, 128], F32, kind="ExternalInput")

    d_out = nc.dram_tensor("out", [T, D], F32, kind="ExternalOutput")

    d_y1 = nc.dram_tensor("y1d", [T, D], F32)
    d_ddk = nc.dram_tensor("ddkd", [NT, 128, D], BF16)
    d_xnT = nc.dram_tensor("xnTd", [NCH, 128, T], BF16)
    d_skd = nc.dram_tensor("skd", [128, H], F32)
    d_cc1i = nc.dram_tensor("cc1i", [1, H], F32)
    d_cc1o = nc.dram_tensor("cc1o", [1, H], F32)
    d_cc2i = nc.dram_tensor("cc2i", [J, H * 65], F32)
    d_cc2o = nc.dram_tensor("cc2o", [J, H * 65], F32)

    P = Prog()
    ctx = ExitStack()
    taps = {}

    def tap(name, buf, ap_in):
        if not DEBUG:
            return
        shp = list(ap_in.shape)
        dt_ = ap_in.dtype
        dtap = nc.dram_tensor("tap_" + name, shp, dt_, kind="ExternalOutput")
        taps[name] = dtap

        def fn(h):
            return h.dma_start(out=dtap.ap(), in_=ap_in)
        P.op("sp", fn, reads=[buf], writes=[dtap], kind="dma", tag="tap_" + name)

    def sbt(st, name, shape, dtype):
        return st.enter_context(nc.sbuf_tensor(name, shape, dtype))

    idf = sbt(ctx, "s_idf", [128, 128], F32)
    idb = sbt(ctx, "s_idb", [128, 128], BF16)
    riota = sbt(ctx, "s_riota", [128, GT * NE], F32)
    epst = sbt(ctx, "epst", [128, 1], F32)
    onesb = sbt(ctx, "onesb", [128, 128], F32)
    gateT = sbt(ctx, "gateT", [NE, T], F32)
    mu1 = sbt(ctx, "mu1", [128, NT], F32)
    rs1 = sbt(ctx, "rs1", [128, NT], F32)
    stabgb = sbt(ctx, "stabgb", [128, H], F32)
    pf = [ctx.enter_context(nc.psum_tensor(f"pf{i}", [128, 512], F32))
          for i in range(6)]
    pb = [ctx.enter_context(nc.psum_tensor(f"pb{i}", [128, 1024], BF16))
          for i in range(2)]

    stabR = sbt(ctx, "stabR", [1, H], F32)
    sA = ExitStack()
    sK = ExitStack()
    sB = ExitStack()
    qpT = [sbt(sA, f"qpT{c}", [128, T], BF16) for c in range(NCH)]
    kvbd = [sbt(sK, f"kvbd{c}", [128, 128], BF16) for c in range(NCH)]
    ksl = [sbt(sK, f"ksl{c}", [128, H], BF16) for c in range(NCH)]
    dgk = [sbt(sB, f"dgk{t}", [128, H], F32) for t in range(NT)]
    vtk = [sbt(sB, f"vtk{t}", [128, H * 65], BF16) for t in range(NT)]

    def pe_fence(bufs, tag):
        P.op("pe", lambda t: t.matmul(pf[5][0:1, 496:497], onesb[0:1, 0:1],
                                      onesb[0:1, 0:1], start=True, stop=True),
             reads=[onesb], writes=list(bufs), tag=f"fence_{tag}")

    def dma(eng, out_ap, in_ap, reads=(), writes=(), tag="", slow=False):
        def fn(h):
            if slow:
                with nc.allow_non_contiguous_dma(reason="small"):
                    return h.dma_start(out=out_ap, in_=in_ap)
            return h.dma_start(out=out_ap, in_=in_ap)
        return P.op(eng, fn, reads=reads, writes=writes, kind="dma", tag=tag)

    dma("sp", idf[:, :], d_idf.ap(), writes=[idf], tag="idf")
    dma("pool", idb[:, :], d_idf.ap(), writes=[idb], tag="idb")
    dma("sp", riota[:, :], d_riota.ap(), writes=[riota], tag="riota")
    P.op("dve", lambda v: v.memset(epst[:, :], LN_EPS), writes=[epst], tag="eps")
    P.op("dve", lambda v: v.memset(onesb[:, :], 1.0), writes=[onesb], tag="ones")

    def bcast_row(row, dst, tag):
        for hf in range(2):
            s = hf * 512
            P.op("pe", lambda t, s=s, row=row: t.matmul(
                pf[5][:, :], onesb[0:1, :], row[0:1, s:s + 512],
                start=True, stop=True), reads=[onesb, row], writes=[pf[5]],
                tag=f"bc_{tag}{hf}")
            P.op("dve", lambda v, s=s, dst=dst: v.tensor_copy(
                dst[:, s:s + 512], pf[5][:, :]), reads=[pf[5]], writes=[dst],
                tag=f"bce_{tag}{hf}")


    p1 = ExitStack()
    e2c = sbt(p1, "e2c", [128, NCH * H], F32)
    bq_sb = sbt(p1, "bq_sb", [128, NCH], F32)
    bk_sb = sbt(p1, "bk_sb", [128, NCH], F32)
    row_bv = sbt(p1, "row_bv", [1, D], F32)
    bvb = sbt(p1, "bvb", [128, D], F32)
    dma("sp", e2c.ap().rearrange("p (c h) -> p c h", c=NCH),
        d_e2.ap().rearrange("c p h -> p c h"), writes=[e2c], tag="e2")
    dma("sp", bq_sb[:, :], d_bq.ap().rearrange("(c p) -> p c", p=128),
        writes=[bq_sb], tag="bqc", slow=True)
    dma("sp", bk_sb[:, :], d_bk.ap().rearrange("(c p) -> p c", p=128),
        writes=[bk_sb], tag="bkc", slow=True)
    dma("sp", row_bv[:, :], d_bv.ap().unsqueeze(0), writes=[row_bv], tag="rbv")
    bcast_row(row_bv, bvb, "bv")
    wq = sbt(p1, "wq", [128, NCH * D], BF16)
    wk = sbt(p1, "wk", [128, NCH * D], BF16)
    wv = sbt(p1, "wv", [128, NCH * D], BF16)
    pjb = sbt(p1, "pjb", [128, NCH * 128], BF16)
    xinT = [sbt(p1, f"xinT{c}", [128, 512], BF16) for c in range(NCH)]
    xg = [sbt(p1, f"xg{i}", [128, D], F32) for i in range(2)]
    xtk = [sbt(p1, f"xtk{i}", [128, D], BF16) for i in range(2)]
    bn6 = [sbt(p1, f"bn6_{i}", [128, 2, 6], F32) for i in range(2)]
    bn2 = [sbt(p1, f"bn2_{i}", [128, 2], F32) for i in range(2)]
    rstd = [sbt(p1, f"rstd{i}", [128, 1], F32) for i in range(2)]
    pT = [sbt(p1, f"pT{c}", [128, 512], BF16) for c in range(NCH)]
    kT = [sbt(p1, f"kT{c}", [128, 512], BF16) for c in range(NCH)]
    sqt = [sbt(p1, f"sqt{i}", [128, 128], F32) for i in range(2)]
    dg2 = sbt(p1, "dg2", [128, 2 * H], F32)
    stq2 = sbt(p1, "stq2", [128, 2 * H], F32)
    stk2 = sbt(p1, "stk2", [128, 2 * H], F32)
    comb2 = sbt(p1, "comb2", [128, 2 * H], F32)
    subf2 = sbt(p1, "subf2", [128, 2 * D], F32)
    qptk2 = sbt(p1, "qptk2", [128, 2 * D], BF16)
    qdd = sbt(p1, "qdd", [128, 2 * D], BF16)
    stk_part = [sbt(p1, f"stk_part{i}", [128, H], F32) for i in range(2)]
    ddst = sbt(p1, "ddst", [128, 2 * D], BF16)
    stk_run = sbt(p1, "stk_run", [128, H], F32)

    def w_load(dst, src, tag):
        dma("pool", dst.ap().rearrange("p (c n) -> p c n", c=NCH),
            src.ap().rearrange("(c p) n -> p c n", p=128),
            writes=[dst], tag=tag)

    w_load(wq, d_wqT, "wq")
    w_load(wk, d_wkT, "wk")
    w_load(wv, d_wvT, "wv")
    dma("pool", pjb.ap().rearrange("p (c n) -> p c n", c=NCH),
        d_projbd.ap().rearrange("c p n -> p c n"), writes=[pjb], tag="pjb")

    def ln_tile(ti, src_buf, dst_tok, dst_col, tag):
        par = ti % 2
        b6, b2, rs = bn6[par], bn2[par], rstd[par]
        P.op("dve", lambda v: v.bn_stats(b6[:, 0, :], src_buf[:, 0:512]),
             reads=[src_buf], writes=[b6], tag=f"{tag}s0_{ti}")
        P.op("dve", lambda v: v.bn_stats(b6[:, 1, :], src_buf[:, 512:1024]),
             reads=[src_buf], writes=[b6], tag=f"{tag}s1_{ti}")
        P.op("dve", lambda v: v.bn_aggr(b2[:, :], b6[:, :, :]),
             reads=[b6], writes=[b2], tag=f"{tag}ag_{ti}")
        P.op("act", lambda a: a.activation(rs[:, :], b2[:, 1:2], AF.Ln,
                                           bias=epst[:, 0:1]),
             reads=[b2, epst], writes=[rs], tag=f"{tag}sq_{ti}")
        P.op("act", lambda a: a.activation(rs[:, :], rs[:, 0:1], AF.Exp,
                                           scale=-0.5),
             reads=[rs], writes=[rs], tag=f"{tag}rc_{ti}")
        P.op("dve", lambda v: v.tensor_scalar(
            dst_tok[:, :], src_buf[:, :], b2[:, 0:1], rs[:, 0:1],
            ALU.subtract, ALU.mult),
            reads=[src_buf, b2, rs], writes=[dst_tok], tag=f"{tag}nm_{ti}")
        for c in range(NCH):
            P.op("pe", lambda t, c=c: t.transpose(
                pb[0][:, c * 128:(c + 1) * 128],
                dst_tok[:, c * 128:(c + 1) * 128], idb[:, :]),
                reads=[dst_tok, idb], writes=[pb[0]], tag=f"{tag}tr_{ti}_{c}")
        for c in range(NCH):
            P.op("act", lambda a, c=c: a.copy(
                xinT[c][:, dst_col:dst_col + 128],
                pb[0][:, c * 128:(c + 1) * 128]),
                reads=[pb[0]], writes=[xinT[c]], tag=f"{tag}te_{ti}_{c}")

    def proj_fmajor(wsb, bsb, dstT, tag):
        for m in range(NCH):
            bank = pf[m % 2]
            for k in range(NCH):
                P.op("pe", lambda t, m=m, k=k, bank=bank: t.matmul(
                    bank[:, :], wsb[:, k * D + m * 128: k * D + (m + 1) * 128],
                    xinT[k][:, :], start=(k == 0), stop=(k == NCH - 1)),
                    reads=[wsb, xinT[k]], writes=[bank], tag=f"{tag}m{m}_{k}")
            P.op("act", lambda a, m=m, bank=bank: a.activation(
                dstT[m][:, :], bank[:, :], AF.Identity, bias=bsb[:, m:m + 1]),
                reads=[bank, bsb], writes=[dstT[m]], tag=f"{tag}e{m}")

    def proj_v(g, tag):
        for ts in range(GT):
            ti = g * GT + ts
            for ob in range(2):
                bank = pf[(ts * 2 + ob) % 2]
                for k in range(NCH):
                    P.op("pe", lambda t, ts=ts, ob=ob, k=k, bank=bank: t.matmul(
                        bank[:, :], xinT[k][:, ts * 128:(ts + 1) * 128],
                        wv[:, k * D + ob * 512: k * D + (ob + 1) * 512],
                        start=(k == 0), stop=(k == NCH - 1)),
                        reads=[wv, xinT[k]], writes=[bank],
                        tag=f"{tag}m{ts}_{ob}_{k}")
                P.op("dve", lambda v, ti=ti, ob=ob, bank=bank: v.tensor_add(
                    vtk[ti].ap().rearrange("p (h j) -> p h j", j=65)
                    [:, ob * 8:(ob + 1) * 8, 0:64],
                    bank.ap().rearrange("p (h j) -> p h j", j=64),
                    bvb.ap().rearrange("p (h j) -> p h j", j=64)
                    [:, ob * 8:(ob + 1) * 8, :]),
                    reads=[bank, bvb], writes=[vtk[ti]], tag=f"{tag}e{ts}_{ob}")
            P.op("dve", lambda v, ti=ti: v.memset(
                vtk[ti].ap().rearrange("p (h j) -> p h j", j=65)[:, :, 64:65],
                1.0), writes=[vtk[ti]], tag=f"{tag}o{ti}")

    def features(side, g, tag):
        srcT = pT if side == "q" else kT
        for pair in range(GT // 2):
            ts0 = pair * 2
            banks = [pf[2], pf[3], pf[4], pf[5]]
            ddsb = qdd if side == "q" else ddst
            for tt in range(2):
                ts = ts0 + tt
                for c in range(NCH):
                    bank = banks[tt * 2 + c // 4]
                    col = (c % 4) * 128
                    P.op("pe", lambda t, c=c, ts=ts, bank=bank, col=col: t.matmul(
                        bank[:, col:col + 128],
                        srcT[c][:, ts * 128:(ts + 1) * 128],
                        pjb[:, c * 128:(c + 1) * 128], start=True, stop=True),
                        reads=[srcT[c], pjb], writes=[bank],
                        tag=f"{tag}d{ts}_{c}")
            for q4 in range(4):
                P.op("act", lambda a, q4=q4: a.copy(
                    ddsb[:, q4 * 512:(q4 + 1) * 512], banks[q4][:, 0:512]),
                    reads=[banks[q4]], writes=[ddsb], tag=f"{tag}w{ts0}_{q4}")
            for tt in range(2):
                ts = ts0 + tt
                for c in range(NCH):
                    sq = sqt[c % 2]
                    P.op("act", lambda a, c=c, ts=ts, sq=sq: a.activation(
                        sq[:, :], srcT[c][:, ts * 128:(ts + 1) * 128],
                        AF.Square),
                        reads=[srcT[c]], writes=[sq], tag=f"{tag}q{ts}_{c}")
                    P.op("pe", lambda t, c=c, sq=sq, tt=tt: t.matmul(
                        pf[2][:, tt * H:(tt + 1) * H], sq[:, :],
                        e2c[:, c * H:(c + 1) * H],
                        start=(c == 0), stop=(c == NCH - 1)),
                        reads=[e2c, sq], writes=[pf[2]], tag=f"{tag}g{ts}_{c}")
            P.op("dve", lambda v: v.tensor_copy(dg2[:, :], pf[2][:, 0:2 * H]),
                 reads=[pf[2]], writes=[dg2], tag=f"{tag}g2{ts0}")
            st2 = stq2 if side == "q" else stk2
            P.op("dve", lambda v, st2=st2, ddsb=ddsb: v.reduce_max(
                st2[:, :], ddsb.ap().rearrange("p (h j) -> p h j", j=64),
                axis=AX.X),
                reads=[ddsb], writes=[st2], tag=f"{tag}s{ts0}")
            if side == "q":
                P.op("dve", lambda v: v.scalar_tensor_tensor(
                    comb2[:, :], dg2[:, :], COMB_C, stq2[:, :],
                    ALU.mult, ALU.add),
                    reads=[dg2, stq2], writes=[comb2], tag=f"{tag}c{ts0}")
                P.op("dve", lambda v, ddsb=ddsb: v.tensor_tensor(
                    subf2.ap().rearrange("p (h j) -> p h j", j=64),
                    ddsb.ap().rearrange("p (h j) -> p h j", j=64),
                    comb2.ap().unsqueeze(2).broadcast_to([128, 2 * H, 64]),
                    ALU.subtract),
                    reads=[ddsb, comb2], writes=[subf2], tag=f"{tag}u{ts0}")
                P.op("act", lambda a: a.activation(qptk2[:, :], subf2[:, :],
                                                   AF.Exp, scale=A_NORM),
                     reads=[subf2], writes=[qptk2], tag=f"{tag}x{ts0}")
                P.op("dve", lambda v: v.tensor_scalar_add(
                    qptk2[:, :], qptk2[:, :], EPS_Q),
                    reads=[qptk2], writes=[qptk2], tag=f"{tag}p{ts0}")
                for tt in range(2):
                    ts = ts0 + tt
                    ti = g * GT + ts
                    pbt = pb[tt]
                    for c in range(NCH):
                        P.op("pe", lambda t, c=c, tt=tt, pbt=pbt: t.transpose(
                            pbt[:, c * 128:(c + 1) * 128],
                            qptk2[:, tt * D + c * 128:tt * D + (c + 1) * 128],
                            idb[:, :]),
                            reads=[qptk2, idb], writes=[pbt],
                            tag=f"{tag}t{ts}_{c}")
                    for c in range(NCH):
                        P.op("act", lambda a, c=c, ti=ti, pbt=pbt: a.copy(
                            qpT[c][:, ti * 128:(ti + 1) * 128],
                            pbt[:, c * 128:(c + 1) * 128]),
                            reads=[pbt], writes=[qpT[c]], tag=f"{tag}y{ts}_{c}")
            else:
                for tt in range(2):
                    ts = ts0 + tt
                    ti = g * GT + ts
                    dma("sp", d_ddk[ti, :, :], ddsb[:, tt * D:(tt + 1) * D],
                        reads=[ddsb], writes=[d_ddk], tag=f"{tag}wd{ts}")
                    P.op("dve", lambda v, ti=ti, tt=tt: v.tensor_copy(
                        dgk[ti][:, :], dg2[:, tt * H:(tt + 1) * H]),
                        reads=[dg2], writes=[dgk[ti]], tag=f"{tag}dg{ts}")
                for tt in range(2):
                    P.op("dve", lambda v, tt=tt: v.tensor_max(
                        stk_run[:, :], stk_run[:, :],
                        stk2[:, tt * H:(tt + 1) * H]),
                        reads=[stk_run, stk2], writes=[stk_run],
                        tag=f"{tag}tm{ts0}_{tt}")

    P.op("dve", lambda v: v.memset(stk_run[:, :], -1e30), writes=[stk_run],
         tag="skz")

    def ln_group(g):
        for ts in range(GT):
            ti = g * GT + ts
            xb, xt = xg[ts % 2], xtk[ts % 2]
            dma("sp", xb[:, :], d_x[ti * 128:(ti + 1) * 128, :],
                writes=[xb], tag=f"ldx{ti}")
            ln_tile(ti, xb, xt, ts * 128, "ln")

    for g in range(NG):
        if g == 0:
            ln_group(0)
        proj_fmajor(wq, bq_sb, pT, f"pq{g}")
        proj_fmajor(wk, bk_sb, kT, f"pk{g}")
        proj_v(g, f"pv{g}")
        if g == 0:
            tap("qT0", pT[0], pT[0][:, :])
            tap("vtk0", vtk[0], vtk[0][:, :])
        if g + 1 < NG:
            ln_group(g + 1)
        features("q", g, f"fq{g}")
        features("k", g, f"fk{g}")
        if g == 0:
            tap("dgk0", dgk[0], dgk[0][:, :])

    skrow = sbt(p1, "skrow", [1, 64 * H], F32)
    stab2 = sbt(p1, "stab2", [1, H], F32)
    dma("sp", d_skd.ap(), stk_run[:, :], reads=[stk_run], writes=[d_skd],
        tag="skst")
    for hf in range(2):
        dma("sp", skrow.ap(),
            d_skd[hf * 64:(hf + 1) * 64, :].rearrange("t h -> (t h)").unsqueeze(0),
            reads=[d_skd], writes=[skrow], tag=f"skfl{hf}")
        dst = stabR if hf == 0 else stab2
        P.op("dve", lambda v, dst=dst: v.reduce_max(
            dst[:, :], skrow.ap().rearrange("one (t h) -> one h t", t=64),
            axis=AX.X), reads=[skrow], writes=[dst], tag=f"sktr{hf}")
    P.op("dve", lambda v: v.tensor_max(stabR[:, :], stabR[:, :], stab2[:, :]),
         reads=[stabR, stab2], writes=[stabR], tag="sktm2")
    tap("stk_run", stk_run, stk_run[:, :])
    tap("stabL", stabR, stabR[:, :])
    dma("sp", d_cc1i.ap(), stabR[:, :], reads=[stabR], writes=[d_cc1i],
        tag="cc1w")
    P.op("pool", lambda gp: gp.collective_compute(
        "AllReduce", ALU.max, replica_groups=[list(range(NCORES))],
        ins=[d_cc1i.ap()], outs=[d_cc1o.ap()]),
        reads=[d_cc1i], writes=[d_cc1o], kind="cc", tag="AR1")
    row_st = sbt(p1, "row_st", [1, H], F32)
    dma("sp", row_st[:, :], d_cc1o.ap(), reads=[d_cc1o], writes=[row_st],
        tag="cc1r")
    P.op("pe", lambda t: t.matmul(pf[5][:, 0:H], onesb[0:1, :], row_st[0:1, :],
                                  start=True, stop=True),
         reads=[onesb, row_st], writes=[pf[5]], tag="stbm")
    P.op("dve", lambda v: v.tensor_copy(stabgb[:, :], pf[5][:, 0:H]),
         reads=[pf[5]], writes=[stabgb], tag="stbe")
    tap("stabgb", stabgb, stabgb[0:1, :])

    p1.close()
    P.barrier()

    p2 = ExitStack()
    combk = [sbt(p2, f"combk{i}", [128, H], F32) for i in range(2)]
    kddl = [sbt(p2, f"kddl{i}", [128, D], BF16) for i in range(2)]
    e1 = [sbt(p2, f"e1_{i}", [128, D], F32) for i in range(2)]
    e2t = [sbt(p2, f"e2t{i}", [128, D], BF16) for i in range(2)]
    kpt = [sbt(p2, f"kpt{t}", [128, D], BF16) for t in range(NT)]
    ar2st = sbt(p2, "ar2st", [J, H * 65], F32)
    kvs = sbt(p2, "kvs", [J, H * 65], F32)

    for ti in range(NT):
        par = ti % 2
        kdd = kddl[par]
        dma("sp", kdd[:, :], d_ddk[ti, :, :], reads=[d_ddk], writes=[kdd],
            tag=f"kld{ti}")
        P.op("dve", lambda v, ti=ti, par=par: v.scalar_tensor_tensor(
            combk[par][:, :], dgk[ti][:, :], COMB_C, stabgb[:, :],
            ALU.mult, ALU.add),
            reads=[dgk[ti], stabgb], writes=[combk[par]], tag=f"kc{ti}")
        P.op("dve", lambda v, par=par, kdd=kdd: v.tensor_tensor(
            e1[par].ap().rearrange("p (h j) -> p h j", j=64),
            kdd.ap().rearrange("p (h j) -> p h j", j=64),
            combk[par].ap().unsqueeze(2).broadcast_to([128, H, 64]),
            ALU.subtract),
            reads=[kdd, combk[par]], writes=[e1[par]], tag=f"ku{ti}")
        P.op("act", lambda a, par=par: a.activation(e2t[par][:, :],
                                                    e1[par][:, :], AF.Exp,
                                                    scale=A_NORM),
             reads=[e1[par]], writes=[e2t[par]], tag=f"kx{ti}")
        P.op("dve", lambda v, ti=ti, par=par: v.tensor_scalar_add(
            kpt[ti][:, :], e2t[par][:, :], EPS_Q),
            reads=[e2t[par]], writes=[kpt[ti]], tag=f"kp{ti}")

    def ubank(h):
        if h < 12:
            return pf[h // 2], (h % 2) * 130
        return pf[h - 12], 260

    for h in range(H):
        bank, col = ubank(h)
        for ti in range(NT):
            P.op("pe", lambda t, ti=ti, h=h, bank=bank, col=col: t.matmul(
                bank[0:J, col:col + 65], kpt[ti][:, h * 64:(h + 1) * 64],
                vtk[ti][:, h * 65:(h + 1) * 65],
                start=(ti == 0), stop=(ti == NT - 1)),
                reads=[kpt[ti], vtk[ti]], writes=[bank], tag=f"U{h}_{ti}")
    for h in range(H):
        bank, col = ubank(h)
        P.op("dve", lambda v, h=h, bank=bank, col=col: v.tensor_copy(
            ar2st[:, h * 65:(h + 1) * 65], bank[0:J, col:col + 65]),
            reads=[bank], writes=[ar2st], tag=f"Ue{h}")
    dma("sp", d_cc2i.ap(), ar2st[:, :], reads=[ar2st], writes=[d_cc2i],
        tag="cc2w")
    P.op("pool", lambda gp: gp.collective_compute(
        "AllReduce", ALU.add, replica_groups=[list(range(NCORES))],
        ins=[d_cc2i.ap()], outs=[d_cc2o.ap()]),
        reads=[d_cc2i], writes=[d_cc2o], kind="cc", tag="AR2")
    dma("sp", kvs[:, :], d_cc2o.ap(), reads=[d_cc2o], writes=[kvs], tag="cc2r")
    tap("kpt0", kpt[0], kpt[0][:, :])
    tap("ar2st", ar2st, ar2st[:, :])
    tap("kvs", kvs, kvs[:, :])

    for c in range(NCH):
        P.op("dve", lambda v, c=c: v.memset(kvbd[c][:, :], 0.0),
             writes=[kvbd[c]], tag=f"kz{c}")
        P.op("dve", lambda v, c=c: v.memset(ksl[c][:, :], 0.0),
             writes=[ksl[c]], tag=f"sz{c}")
        for s in range(2):
            h = 2 * c + s
            P.op("dve", lambda v, c=c, s=s, h=h: v.tensor_copy(
                kvbd[c][s * 64:(s + 1) * 64, s * 64:(s + 1) * 64],
                kvs[:, h * 65:h * 65 + 64]),
                reads=[kvs], writes=[kvbd[c]], tag=f"kc{c}_{s}")
            P.op("dve", lambda v, c=c, s=s, h=h: v.tensor_copy(
                ksl[c][s * 64:(s + 1) * 64, h:h + 1],
                kvs[:, h * 65 + 64:(h + 1) * 65]),
                reads=[kvs], writes=[ksl[c]], tag=f"sc{c}_{s}")

    p2.close()
    P.barrier()
    sB.close()
    P.barrier()

    p3 = ExitStack()
    ebcc = sbt(p3, "ebcc", [16, NCH * 128], F32)
    row_bo = sbt(p3, "row_bo", [1, D], F32)
    row_g = sbt(p3, "row_g", [1, D], F32)
    row_b = sbt(p3, "row_b", [1, D], F32)
    bob = sbt(p3, "bob", [128, D], F32)
    gmb = sbt(p3, "gmb", [128, D], F32)
    btb = sbt(p3, "btb", [128, D], F32)
    dma("sp", ebcc.ap().rearrange("p (c n) -> p c n", c=NCH),
        d_ebc.ap().rearrange("c p n -> p c n"), writes=[ebcc], tag="ebc")
    dma("sp", row_bo[:, :], d_bo.ap().unsqueeze(0), writes=[row_bo], tag="rbo")
    dma("sp", row_g[:, :], d_gamma.ap().unsqueeze(0), writes=[row_g], tag="rg")
    dma("sp", row_b[:, :], d_beta.ap().unsqueeze(0), writes=[row_b], tag="rb")
    bcast_row(row_bo, bob, "bo")
    bcast_row(row_g, gmb, "g")
    bcast_row(row_b, btb, "b")
    wo = sbt(p3, "wo", [128, NCH * D], BF16)
    w_load(wo, d_woT, "wo")
    rw = sbt(p3, "rw", [128, NCH * NE], BF16)
    dma("pool", rw.ap().rearrange("p (c n) -> p c n", c=NCH),
        d_rwT.ap().rearrange("(c p) n -> p c n", p=128), writes=[rw], tag="rw")

    for c in range(NCH):
        tap(f"ksl{c}", ksl[c], ksl[c][:, :])
    tap("kvbd0", kvbd[0], kvbd[0][:, :])
    tap("kvbd1", kvbd[1], kvbd[1][:, :])
    attnT = [sbt(p3, f"attnT{c}", [128, 512], BF16) for c in range(NCH)]
    xnTg = [sbt(p3, f"xnTg{c}", [128, 512], BF16) for c in range(NCH)]
    aog = [sbt(p3, f"aog{i}", [128, D], F32) for i in range(GT)]
    y1g = [sbt(p3, f"y1g{i}", [128, D], F32) for i in range(GT)]
    bn2g = sbt(p3, "bn2g", [128, GT, 2], F32)
    bn2h = sbt(p3, "bn2h", [128, GT, 2], F32)
    rstdg = sbt(p3, "rstdg", [128, GT], F32)
    rs1g = sbt(p3, "rs1g", [128, GT], F32)
    v2g = sbt(p3, "v2g", [128, GT], F32)
    rstd2g = sbt(p3, "rstd2g", [128, GT], F32)
    rp = sbt(p3, "rp", [128, GT * NE], F32)
    m1g = sbt(p3, "m1g", [128, GT], F32)
    sumg = sbt(p3, "sumg", [128, GT], F32)
    eqg = sbt(p3, "eqg", [128, GT * NE], F32)
    eqg2 = sbt(p3, "eqg2", [128, GT * NE], F32)
    sc1g = sbt(p3, "sc1g", [128, GT * NE], F32)
    s1mg = sbt(p3, "s1mg", [128, GT], F32)
    f1g = sbt(p3, "f1g", [128, GT * NE], F32)
    mskg = sbt(p3, "mskg", [128, GT * NE], F32)
    gvg = sbt(p3, "gvg", [128, GT * NE], F32)
    rnorm = sbt(p3, "rnorm", [16, 512], F32)
    rbf = sbt(p3, "rbf", [128, 512], F32)
    zt = [sbt(p3, f"zt{i}", [128, D], F32) for i in range(GT)]
    z2junk = sbt(p3, "z2junk", [128, D], F32)
    xntk = [sbt(p3, f"xntk{i}", [128, D], BF16) for i in range(2)]
    xtmp = [sbt(p3, f"xtmp{i}", [128, D], F32) for i in range(2)]
    bn6b = [sbt(p3, f"bn6b{i}", [128, 2, 6], F32) for i in range(2)]
    xg3 = [sbt(p3, f"xg3_{i}", [128, D], F32) for i in range(GT)]

    for g in range(NG):
        for c in range(NCH):
            P.op("pe", lambda t, c=c, g=g: t.matmul(
                pf[2][0:16, :], ksl[c][:, :], qpT[c][:, g * 512:(g + 1) * 512],
                start=(c == 0), stop=(c == NCH - 1)),
                reads=[ksl[c], qpT[c]], writes=[pf[2]], tag=f"n{g}_{c}")
        P.op("dve", lambda v: v.tensor_scalar_max(rnorm[:, :], pf[2][0:16, :],
                                                  DEN_CLIP),
             reads=[pf[2]], writes=[rnorm], tag=f"nx{g}")
        if g == 0:
            tap("normraw", rnorm, rnorm[:, :])
        P.op("dve", lambda v: v.reciprocal(rnorm[:, :], rnorm[:, :]),
             reads=[rnorm], writes=[rnorm], tag=f"nr{g}")
        for c in range(NCH):
            P.op("pe", lambda t, c=c, g=g: t.matmul(
                pf[3][:, :], kvbd[c][:, :], qpT[c][:, g * 512:(g + 1) * 512],
                start=True, stop=True),
                reads=[kvbd[c], qpT[c]], writes=[pf[3]], tag=f"qk{g}_{c}")
            P.op("pe", lambda t, c=c: t.matmul(
                pf[4][:, :], ebcc[:, c * 128:(c + 1) * 128], rnorm[0:16, :],
                start=True, stop=True),
                reads=[ebcc, rnorm], writes=[pf[4]], tag=f"rb{g}_{c}")
            P.op("dve", lambda v: v.tensor_copy(rbf[:, :], pf[4][:, :]),
                 reads=[pf[4]], writes=[rbf], tag=f"re{g}_{c}")
            P.op("dve", lambda v, c=c: v.tensor_mul(attnT[c][:, :], pf[3][:, :],
                                                    rbf[:, :]),
                 reads=[pf[3], rbf], writes=[attnT[c]], tag=f"at{g}_{c}")
        if g == 0:
            tap("rnorm", rnorm, rnorm[:, :])
            tap("attnT0", attnT[0], attnT[0][:, :])
            tap("attnT7", attnT[7], attnT[7][:, :])
        for ts in range(GT):
            ti = g * GT + ts
            for ob in range(2):
                bank = pf[(ts % 2) * 2 + ob]
                for k in range(NCH):
                    P.op("pe", lambda t, ts=ts, ob=ob, k=k, bank=bank: t.matmul(
                        bank[:, :], attnT[k][:, ts * 128:(ts + 1) * 128],
                        wo[:, k * D + ob * 512: k * D + (ob + 1) * 512],
                        start=(k == 0), stop=(k == NCH - 1)),
                        reads=[attnT[k], wo], writes=[bank],
                        tag=f"w{ti}_{ob}_{k}")
                P.op("dve", lambda v, ts=ts, ob=ob, bank=bank: v.tensor_add(
                    aog[ts][:, ob * 512:(ob + 1) * 512], bank[:, :],
                    bob[:, ob * 512:(ob + 1) * 512]),
                    reads=[bank, bob], writes=[aog[ts]], tag=f"we{ti}_{ob}")
            dma("sp", xg3[ts][:, :], d_x[ti * 128:(ti + 1) * 128, :],
                writes=[xg3[ts]], tag=f"lx3{ti}")
        for ts in range(GT):
            b6 = bn6b[ts % 2]
            P.op("dve", lambda v, ts=ts, b6=b6: v.bn_stats(
                b6[:, 0, :], aog[ts][:, 0:512]),
                reads=[aog[ts]], writes=[b6], tag=f"l1a{g}_{ts}")
            P.op("dve", lambda v, ts=ts, b6=b6: v.bn_stats(
                b6[:, 1, :], aog[ts][:, 512:1024]),
                reads=[aog[ts]], writes=[b6], tag=f"l1b{g}_{ts}")
            P.op("dve", lambda v, ts=ts, b6=b6: v.bn_aggr(
                bn2g[:, ts, :], b6[:, :, :]),
                reads=[b6], writes=[bn2g], tag=f"l1c{g}_{ts}")
        P.op("act", lambda a: a.activation(
            rstdg[:, :], bn2g[:, :, 1], AF.Ln, bias=epst[:, 0:1]),
            reads=[bn2g, epst], writes=[rstdg], tag=f"l1d{g}")
        P.op("act", lambda a: a.activation(rstdg[:, :], rstdg[:, :], AF.Exp,
                                           scale=-0.5),
             reads=[rstdg], writes=[rstdg], tag=f"l1e{g}")
        for ts in range(GT):
            ti = g * GT + ts
            P.op("dve", lambda v, ts=ts: v.tensor_scalar(
                y1g[ts][:, :], aog[ts][:, :], bn2g[:, ts, 0:1],
                rstdg[:, ts:ts + 1], ALU.subtract, ALU.mult),
                reads=[aog[ts], bn2g, rstdg], writes=[y1g[ts]],
                tag=f"l1n{ti}")
            P.op("dve", lambda v, ts=ts: v.tensor_add(
                y1g[ts][:, :], y1g[ts][:, :], xg3[ts][:, :]),
                reads=[y1g[ts], xg3[ts]], writes=[y1g[ts]], tag=f"y1{ti}")
            dma("sp", d_y1[ti * 128:(ti + 1) * 128, :], y1g[ts][:, :],
                reads=[y1g[ts]], writes=[d_y1], tag=f"sy1{ti}")
        for ts in range(GT):
            b6 = bn6b[ts % 2]
            P.op("dve", lambda v, ts=ts, b6=b6: v.bn_stats(
                b6[:, 0, :], y1g[ts][:, 0:512]),
                reads=[y1g[ts]], writes=[b6], tag=f"l2a{g}_{ts}")
            P.op("dve", lambda v, ts=ts, b6=b6: v.bn_stats(
                b6[:, 1, :], y1g[ts][:, 512:1024]),
                reads=[y1g[ts]], writes=[b6], tag=f"l2b{g}_{ts}")
            P.op("dve", lambda v, ts=ts, b6=b6: v.bn_aggr(
                bn2h[:, ts, :], b6[:, :, :]),
                reads=[b6], writes=[bn2h], tag=f"l2c{g}_{ts}")
        P.op("act", lambda a: a.activation(
            rs1g[:, :], bn2h[:, :, 1], AF.Ln, bias=epst[:, 0:1]),
            reads=[bn2h, epst], writes=[rs1g], tag=f"l2d{g}")
        P.op("act", lambda a: a.activation(rs1g[:, :], rs1g[:, :], AF.Exp,
                                           scale=-0.5),
             reads=[rs1g], writes=[rs1g], tag=f"l2e{g}")
        P.op("dve", lambda v, g=g: v.tensor_copy(
            mu1[:, g * GT:(g + 1) * GT], bn2h[:, :, 0]),
            reads=[bn2h], writes=[mu1], tag=f"m1s{g}")
        P.op("dve", lambda v, g=g: v.tensor_copy(
            rs1[:, g * GT:(g + 1) * GT], rs1g[:, :]),
            reads=[rs1g], writes=[rs1], tag=f"r1s{g}")
        for ts in range(GT):
            ti = g * GT + ts
            par = ts % 2
            P.op("dve", lambda v, ts=ts: v.tensor_scalar(
                zt[ts][:, :], y1g[ts][:, :], bn2h[:, ts, 0:1],
                rs1g[:, ts:ts + 1], ALU.subtract, ALU.mult),
                reads=[y1g[ts], bn2h, rs1g], writes=[zt[ts]], tag=f"z{ti}")
            P.op("act", lambda a, ts=ts: a.activation(
                z2junk[:, :], zt[ts][:, :], AF.Square,
                accum_out=v2g[:, ts:ts + 1]),
                reads=[zt[ts]], writes=[z2junk, v2g], tag=f"v2{ti}")
        P.op("act", lambda a: a.activation(rstd2g[:, :], v2g[:, :], AF.Ln,
                                           bias=epst[:, 0:1], scale=1.0 / D),
             reads=[v2g, epst], writes=[rstd2g], tag=f"r2{g}")
        P.op("act", lambda a: a.activation(rstd2g[:, :], rstd2g[:, :], AF.Exp,
                                           scale=-0.5),
             reads=[rstd2g], writes=[rstd2g], tag=f"r2r{g}")
        for ts in range(GT):
            ti = g * GT + ts
            par = ts % 2
            P.op("dve", lambda v, ts=ts, par=par: v.scalar_tensor_tensor(
                xtmp[par][:, :], zt[ts][:, :], rstd2g[:, ts:ts + 1], gmb[:, :],
                ALU.mult, ALU.mult),
                reads=[zt[ts], rstd2g, gmb], writes=[xtmp[par]], tag=f"xa{ti}")
            P.op("dve", lambda v, par=par: v.tensor_add(
                xntk[par][:, :], xtmp[par][:, :], btb[:, :]),
                reads=[xtmp[par], btb], writes=[xntk[par]], tag=f"xb{ti}")
            pbt = pb[par]
            for c in range(NCH):
                P.op("pe", lambda t, c=c, par=par, pbt=pbt: t.transpose(
                    pbt[:, c * 128:(c + 1) * 128],
                    xntk[par][:, c * 128:(c + 1) * 128], idb[:, :]),
                    reads=[xntk[par], idb], writes=[pbt], tag=f"xt{ti}_{c}")
            for c in range(NCH):
                P.op("act", lambda a, c=c, ts=ts, pbt=pbt: a.copy(
                    xnTg[c][:, ts * 128:(ts + 1) * 128],
                    pbt[:, c * 128:(c + 1) * 128]),
                    reads=[pbt], writes=[xnTg[c]], tag=f"xe{ti}_{c}")
            for c in range(NCH):
                P.op("pe", lambda t, c=c, ts=ts: t.matmul(
                    pf[5][:, ts * NE:(ts + 1) * NE],
                    xnTg[c][:, ts * 128:(ts + 1) * 128],
                    rw[:, c * NE:(c + 1) * NE],
                    start=(c == 0), stop=(c == NCH - 1)),
                    reads=[xnTg[c], rw], writes=[pf[5]], tag=f"rt{ti}_{c}")
        rp3 = rp.ap().rearrange("p (t e) -> p t e", e=NE)
        P.op("dve", lambda v: v.tensor_scalar(
            rp[:, :], pf[5][:, 0:GT * NE], CLAMP, -CLAMP, ALU.min, ALU.max),
            reads=[pf[5]], writes=[rp], tag=f"lg{g}")
        P.op("dve", lambda v: v.reduce_max(m1g[:, :], rp3, axis=AX.X),
             reads=[rp], writes=[m1g], tag=f"lm{g}")
        P.op("dve", lambda v: v.tensor_tensor(
            rp3, rp3, m1g.ap().unsqueeze(2).broadcast_to([128, GT, NE]),
            ALU.subtract),
            reads=[rp, m1g], writes=[rp], tag=f"ls{g}")
        P.op("act", lambda a: a.activation(rp[:, :], rp[:, :], AF.Exp),
             reads=[rp], writes=[rp], tag=f"px{g}")
        P.op("dve", lambda v: v.reduce_sum(sumg[:, :], rp3, axis=AX.X),
             reads=[rp], writes=[sumg], tag=f"p0{g}")
        P.op("dve", lambda v: v.tensor_scalar_add(sumg[:, :], sumg[:, :],
                                                  ROUTER_EPS),
             reads=[sumg], writes=[sumg], tag=f"p1{g}")
        P.op("dve", lambda v: v.reciprocal(sumg[:, :], sumg[:, :]),
             reads=[sumg], writes=[sumg], tag=f"p2{g}")
        P.op("dve", lambda v: v.tensor_tensor(
            rp3, rp3, sumg.ap().unsqueeze(2).broadcast_to([128, GT, NE]),
            ALU.mult),
            reads=[rp, sumg], writes=[rp], tag=f"p3{g}")
        P.op("dve", lambda v: v.tensor_scalar(
            rp[:, :], rp[:, :], 1.0, ROUTER_EPS, ALU.min, ALU.max),
            reads=[rp], writes=[rp], tag=f"p4{g}")

        def top2_sel(dst_mask, srcbuf, tag2):
            s3 = srcbuf.ap().rearrange("p (t e) -> p t e", e=NE)
            P.op("dve", lambda v: v.reduce_max(m1g[:, :], s3, axis=AX.X),
                 reads=[srcbuf], writes=[m1g], tag=f"{tag2}a")
            P.op("dve", lambda v: v.tensor_tensor(
                eqg.ap().rearrange("p (t e) -> p t e", e=NE), s3,
                m1g.ap().unsqueeze(2).broadcast_to([128, GT, NE]),
                ALU.is_equal),
                reads=[srcbuf, m1g], writes=[eqg], tag=f"{tag2}b")
            P.op("dve", lambda v: v.tensor_mul(sc1g[:, :], eqg[:, :],
                                               riota[:, :]),
                 reads=[eqg, riota], writes=[sc1g], tag=f"{tag2}c")
            P.op("dve", lambda v: v.reduce_max(
                s1mg[:, :], sc1g.ap().rearrange("p (t e) -> p t e", e=NE),
                axis=AX.X),
                reads=[sc1g], writes=[s1mg], tag=f"{tag2}d")
            P.op("dve", lambda v: v.tensor_tensor(
                dst_mask.ap().rearrange("p (t e) -> p t e", e=NE),
                sc1g.ap().rearrange("p (t e) -> p t e", e=NE),
                s1mg.ap().unsqueeze(2).broadcast_to([128, GT, NE]),
                ALU.is_equal),
                reads=[sc1g, s1mg], writes=[dst_mask], tag=f"{tag2}e")

        top2_sel(f1g, rp, f"a{g}_")
        P.op("dve", lambda v: v.scalar_tensor_tensor(
            mskg[:, :], f1g[:, :], -1e30, rp[:, :], ALU.mult, ALU.add),
            reads=[f1g, rp], writes=[mskg], tag=f"mk{g}")
        top2_sel(eqg2, mskg, f"b{g}_")
        P.op("dve", lambda v: v.tensor_add(f1g[:, :], f1g[:, :], eqg2[:, :]),
             reads=[f1g, eqg2], writes=[f1g], tag=f"fm{g}")
        P.op("dve", lambda v: v.tensor_mul(gvg[:, :], rp[:, :], f1g[:, :]),
             reads=[rp, f1g], writes=[gvg], tag=f"gv{g}")
        if g == 0:
            tap("gvals0", gvg, gvg[:, 0:NE])
            tap("aog0", aog[0], aog[0][:, :])
            tap("y1g0", y1g[0], y1g[0][:, :])
            tap("bn2g", bn2g, bn2g[:, :, :])
            tap("rstdg", rstdg, rstdg[:, :])
            tap("v2g", v2g, v2g[:, :])
            tap("rp", rp, rp[:, :])
            tap("xnTg0", xnTg[0], xnTg[0][:, :])
        for ts in range(GT):
            ti = g * GT + ts
            P.op("pe", lambda t, ts=ts: t.transpose(
                pf[5][0:NE, 256:256 + 128], gvg[:, ts * NE:(ts + 1) * NE],
                idf[:, :]),
                reads=[gvg, idf], writes=[pf[5]], tag=f"gt{ti}")
            P.op("act", lambda a, ti=ti: a.copy(
                gateT[:, ti * 128:(ti + 1) * 128], pf[5][0:NE, 256:256 + 128]),
                reads=[pf[5]], writes=[gateT], tag=f"ge{ti}")
        for c in range(NCH):
            dma("sp", d_xnT[c, :, g * 512:(g + 1) * 512], xnTg[c][:, :],
                reads=[xnTg[c]], writes=[d_xnT], tag=f"sxn{g}_{c}")

    p3.close()
    sK.close()
    sA.close()
    P.barrier()

    p4 = ExitStack()
    TH = T // 2
    guw = [sbt(p4, f"guw{i}", [128, NCH * 256], BF16) for i in range(4)]
    xnTh = [sbt(p4, f"xnTh{c}", [128, T // 2], BF16) for c in range(NCH)]
    dnw = [sbt(p4, f"dnw{i}", [128, 16 * 512], BF16) for i in range(2)]
    grow = [sbt(p4, f"grow{i}", [1, TH], F32) for i in range(2)]
    gbt = sbt(p4, "gbt", [128, TH], F32)
    xet = [sbt(p4, f"xet{c}", [128, TH], BF16) for c in range(NCH)]
    hid = [sbt(p4, f"hid{c}", [128, TH], BF16) for c in range(16)]
    acc = [sbt(p4, f"acc{i}", [128, D], F32) for i in range(8)]
    s1t = [sbt(p4, f"s1t{i}", [128, 512], BF16) for i in range(2)]
    y1l = sbt(p4, "y1l", [128, D], F32)
    zt4 = sbt(p4, "zt4", [128, D], F32)
    fr4 = sbt(p4, "fr4", [128, D], F32)
    bn64 = sbt(p4, "bn64", [128, 2, 6], F32)
    bn24 = sbt(p4, "bn24", [128, 2], F32)
    rstd4 = sbt(p4, "rstd4", [128, 1], F32)
    y2t = sbt(p4, "y2t", [128, D], F32)

    def fin_tile(half, tt):
        ti = half * 8 + tt
        dma("sp", y1l[:, :], d_y1[ti * 128:(ti + 1) * 128, :],
            reads=[d_y1], writes=[y1l], tag=f"ly{ti}")
        P.op("dve", lambda v, ti=ti: v.tensor_scalar(
            zt4[:, :], y1l[:, :], mu1[:, ti:ti + 1], rs1[:, ti:ti + 1],
            ALU.subtract, ALU.mult),
            reads=[y1l, mu1, rs1], writes=[zt4], tag=f"fz{ti}")
        P.op("dve", lambda v, tt=tt: v.tensor_add(fr4[:, :], zt4[:, :],
                                                  acc[tt][:, :]),
             reads=[zt4, acc[tt]], writes=[fr4], tag=f"ff{ti}")
        P.op("dve", lambda v: v.bn_stats(bn64[:, 0, :], fr4[:, 0:512]),
             reads=[fr4], writes=[bn64], tag=f"f0{ti}")
        P.op("dve", lambda v: v.bn_stats(bn64[:, 1, :], fr4[:, 512:1024]),
             reads=[fr4], writes=[bn64], tag=f"f1{ti}")
        P.op("dve", lambda v: v.bn_aggr(bn24[:, :], bn64[:, :, :]),
             reads=[bn64], writes=[bn24], tag=f"f2{ti}")
        P.op("act", lambda a: a.activation(rstd4[:, :], bn24[:, 1:2],
                                           AF.Ln, bias=epst[:, 0:1]),
             reads=[bn24, epst], writes=[rstd4], tag=f"f3{ti}")
        P.op("act", lambda a: a.activation(rstd4[:, :], rstd4[:, 0:1],
                                           AF.Exp, scale=-0.5),
             reads=[rstd4], writes=[rstd4], tag=f"f4{ti}")
        P.op("dve", lambda v: v.tensor_scalar(
            fr4[:, :], fr4[:, :], bn24[:, 0:1], rstd4[:, 0:1],
            ALU.subtract, ALU.mult),
            reads=[fr4, bn24, rstd4], writes=[fr4], tag=f"f5{ti}")
        P.op("dve", lambda v: v.tensor_add(y2t[:, :], y1l[:, :], fr4[:, :]),
             reads=[y1l, fr4], writes=[y2t], tag=f"f6{ti}")
        dma("sp", d_out[ti * 128:(ti + 1) * 128, :], y2t[:, :],
            reads=[y2t], writes=[d_out], tag=f"so{ti}")

    for half in range(2):
        hc = half * TH
        for c in range(NCH):
            dma("sp", xnTh[c][:, :], d_xnT[c, :, hc:hc + TH],
                reads=[d_xnT], writes=[xnTh[c]], tag=f"lxn{half}_{c}")
        for e in range(NE):
            for pbi in range(4):
                wgu = guw[pbi % 4]
                dma("pool", wgu.ap().rearrange("p (c n) -> p c n", c=NCH),
                    d_guT[e, :, pbi, :].rearrange("(c p) n -> p c n", p=128),
                    writes=[wgu], tag=f"lg{half}_{e}_{pbi}")
            gr = grow[e % 2]
            dma("sp", gr[:, :], gateT[e:e + 1, hc:hc + TH], reads=[gateT],
                writes=[gr], tag=f"gr{half}_{e}")
            for gs in range(2):
                P.op("pe", lambda t, gr=gr, gs=gs: t.matmul(
                    pf[4][:, :], onesb[0:1, :], gr[0:1, gs * 512:(gs + 1) * 512],
                    start=True, stop=True),
                    reads=[onesb, gr], writes=[pf[4]], tag=f"gb{half}_{e}_{gs}")
                P.op("act", lambda a, gs=gs: a.copy(
                    gbt[:, gs * 512:(gs + 1) * 512], pf[4][:, :]),
                    reads=[pf[4]], writes=[gbt], tag=f"gc{half}_{e}_{gs}")
            for c in range(NCH):
                P.op("pool", lambda g_, c=c: g_.tensor_mul(
                    xet[c][:, :], xnTh[c][:, :], gbt[:, :]),
                    reads=[xnTh[c], gbt], writes=[xet[c]],
                    tag=f"x{half}_{e}_{c}")
            if half == 0 and e == 0:
                tap("gbt0", gbt, gbt[:, :])
                tap("xet0", xet[0], xet[0][:, :])
            for pbi in range(16):
                wgu = guw[pbi % 4]
                for gs in range(2):
                    bg = pf[gs * 2]
                    bu = pf[gs * 2 + 1]
                    for k in range(NCH):
                        P.op("pe", lambda t, k=k, wgu=wgu, gs=gs, bg=bg: t.matmul(
                            bg[:, :], wgu[:, k * 256:k * 256 + 128],
                            xet[k][:, gs * 512:(gs + 1) * 512],
                            start=(k == 0), stop=(k == NCH - 1)),
                            reads=[wgu, xet[k]], writes=[bg],
                            tag=f"mg{half}_{e}_{pbi}_{gs}_{k}")
                    for k in range(NCH):
                        P.op("pe", lambda t, k=k, wgu=wgu, gs=gs, bu=bu: t.matmul(
                            bu[:, :], wgu[:, k * 256 + 128:(k + 1) * 256],
                            xet[k][:, gs * 512:(gs + 1) * 512],
                            start=(k == 0), stop=(k == NCH - 1)),
                            reads=[wgu, xet[k]], writes=[bu],
                            tag=f"mu{half}_{e}_{pbi}_{gs}_{k}")
                    st = s1t[gs]
                    P.op("act", lambda a, st=st, bg=bg: a.activation(
                        st[:, :], bg[:, :], AF.Silu),
                        reads=[bg], writes=[st], tag=f"si{half}_{e}_{pbi}_{gs}")
                    P.op("dve", lambda v, st=st, bu=bu, pbi=pbi, gs=gs:
                         v.tensor_mul(hid[pbi][:, gs * 512:(gs + 1) * 512],
                                      st[:, :], bu[:, :]),
                         reads=[st, bu], writes=[hid[pbi]],
                         tag=f"hm{half}_{e}_{pbi}_{gs}")
                if pbi + 4 < 16:
                    nxt = guw[(pbi + 4) % 4]
                    dma("pool", nxt.ap().rearrange("p (c n) -> p c n", c=NCH),
                        d_guT[e, :, pbi + 4, :]
                        .rearrange("(c p) n -> p c n", p=128),
                        writes=[nxt], tag=f"lg{half}_{e}_{pbi + 4}")
            for ob in range(2):
                dma("pool", dnw[ob].ap().rearrange("p (c n) -> p c n", c=16),
                    d_dnT[e, :, ob * 512:(ob + 1) * 512]
                    .rearrange("(c p) n -> p c n", p=128),
                    writes=[dnw[ob]], tag=f"ld{half}_{e}_{ob}")
            for tt in range(8):
                for ob in range(2):
                    bank = pf[4 + ob]
                    for k in range(16):
                        P.op("pe", lambda t, tt=tt, ob=ob, k=k, bank=bank:
                             t.matmul(bank[:, :],
                                      hid[k][:, tt * 128:(tt + 1) * 128],
                                      dnw[ob][:, k * 512:(k + 1) * 512],
                                      start=(k == 0), stop=(k == 15)),
                             reads=[hid[k], dnw[ob]], writes=[bank],
                             tag=f"md{half}_{e}_{tt}_{ob}_{k}")
                    if e == 0:
                        P.op("dve", lambda v, tt=tt, ob=ob, bank=bank:
                             v.tensor_copy(acc[tt][:, ob * 512:(ob + 1) * 512],
                                           bank[:, :]),
                             reads=[bank], writes=[acc[tt]],
                             tag=f"d0{half}_{tt}_{ob}")
                    else:
                        P.op("dve", lambda v, tt=tt, ob=ob, bank=bank:
                             v.tensor_add(acc[tt][:, ob * 512:(ob + 1) * 512],
                                          acc[tt][:, ob * 512:(ob + 1) * 512],
                                          bank[:, :]),
                             reads=[bank, acc[tt]], writes=[acc[tt]],
                             tag=f"da{half}_{e}_{tt}_{ob}")
                if e == NE - 1:
                    fin_tile(half, tt)
        if half == 0:
            tap("hid0", hid[0], hid[0][:, :])
            tap("acc0", acc[0], acc[0][:, :])

    p4.close()

    lane_totals = dict(P.lane_count)
    ccn = P.cc_count

    def make_final(lane_sems, cc_sem):
        def fn(h):
            last = None
            for ln in P.lane_ids:
                last = h.wait_ge(lane_sems[ln], 16 * lane_totals[ln])
            if ccn:
                last = h.wait_ge(cc_sem, ccn)
            return last
        return fn

    sem_ctx = ExitStack()
    s_pe = sem_ctx.enter_context(nc.semaphore("s_pe"))
    s_act = sem_ctx.enter_context(nc.semaphore("s_act"))
    s_dve = sem_ctx.enter_context(nc.semaphore("s_dve"))
    s_pool = sem_ctx.enter_context(nc.semaphore("s_pool"))
    s_sp = sem_ctx.enter_context(nc.semaphore("s_sp"))
    cc_sem = sem_ctx.enter_context(nc.semaphore("ccs"))
    lane_sems = {ln: sem_ctx.enter_context(nc.semaphore(f"lane{i}"))
                 for i, ln in enumerate(P.lane_ids)}
    with nc.Block() as block:
        sems = dict(pe=s_pe, act=s_act, dve=s_dve, pool=s_pool, sp=s_sp)
        fin = make_final(lane_sems, cc_sem)
        P.ops.append(dict(eng="sp", fn=fin, kind="compute", tag="final",
                          deps=set(), idx=len(P.ops), need_inc=False,
                          dma_dep_vals={}))
        P.eng_ops["sp"].append(len(P.ops) - 1)
        P.emit(block, sems, lane_sems, cc_sem)
    sem_ctx.close()

    ctx.close()
    return nc


_NC_CACHE = None


def _get_nc():
    global _NC_CACHE
    if _NC_CACHE is None:
        _NC_CACHE = build_program()
    return _NC_CACHE


def kernel(L0_attn_query, L0_attn_Wq, L0_attn_bq, L0_attn_Wk, L0_attn_bk,
           L0_attn_Wv, L0_attn_bv, L0_attn_Wo, L0_attn_bo, L0_attn_proj,
           L0_ffn_ln_gamma, L0_ffn_ln_beta, L0_ffn_router_w,
           L0_ffn_gate_up_w, L0_ffn_down_w):
    x = np.ascontiguousarray(np.asarray(L0_attn_query, dtype=np.float32)
                             .reshape(N_TOK, D))
    wqT = np.ascontiguousarray(np.asarray(L0_attn_Wq, np.float32).T)
    wkT = np.ascontiguousarray(np.asarray(L0_attn_Wk, np.float32).T)
    wvT = np.ascontiguousarray(np.asarray(L0_attn_Wv, np.float32).T)
    woT = np.ascontiguousarray(np.asarray(L0_attn_Wo, np.float32).T)
    proj = np.asarray(L0_attn_proj, np.float32)
    projbd = np.zeros((NCH, 128, 128), np.float32)
    for c in range(NCH):
        projbd[c, 0:64, 0:64] = proj[2 * c].T
        projbd[c, 64:128, 64:128] = proj[2 * c + 1].T
    rwT = np.ascontiguousarray(np.asarray(L0_ffn_router_w, np.float32).T)
    _gu = np.swapaxes(np.asarray(L0_ffn_gate_up_w, np.float32), 1, 2)
    _g = _gu[:, :, 0:DFF].reshape(NE, D, 16, 128)
    _u = _gu[:, :, DFF:].reshape(NE, D, 16, 128)
    guT = np.ascontiguousarray(np.concatenate([_g, _u], axis=3))
    dnT = np.ascontiguousarray(np.swapaxes(
        np.asarray(L0_ffn_down_w, np.float32), 1, 2))
    idf = np.eye(128, dtype=np.float32)
    riota = np.broadcast_to(np.tile(np.arange(NE, 0, -1, dtype=np.float32), 4),
                            (128, 4 * NE)).copy()
    e2 = np.zeros((NCH, 128, H), np.float32)
    ebc = np.zeros((NCH, H, 128), np.float32)
    for c in range(NCH):
        e2[c, 0:64, 2 * c] = 1.0
        e2[c, 64:128, 2 * c + 1] = 1.0
        ebc[c, 2 * c, 0:64] = 1.0
        ebc[c, 2 * c + 1, 64:128] = 1.0

    shared = dict(
        wqT=wqT, wkT=wkT, wvT=wvT, woT=woT,
        bq=np.asarray(L0_attn_bq, np.float32),
        bk=np.asarray(L0_attn_bk, np.float32),
        bv=np.asarray(L0_attn_bv, np.float32),
        bo=np.asarray(L0_attn_bo, np.float32),
        projbd=projbd,
        gamma=np.asarray(L0_ffn_ln_gamma, np.float32),
        beta=np.asarray(L0_ffn_ln_beta, np.float32),
        rwT=rwT, guT=guT, dnT=dnT, idf=idf, riota=riota, e2=e2, ebc=ebc,
    )
    in_maps = []
    for c in range(NCORES):
        m = dict(shared)
        m["xq"] = np.ascontiguousarray(x[c * T:(c + 1) * T, :])
        in_maps.append(m)

    nc = _get_nc()
    res = run_bass_kernel_spmd(nc, in_maps, core_ids=list(range(NCORES)))
    out = np.concatenate([res.results[c]["out"] for c in range(NCORES)], axis=0)
    return out.reshape(1, N_TOK, D)


# version 19
# speedup vs baseline: 1.0611x; 1.0078x over previous
import numpy as np
import concourse.bass as bass
import concourse.mybir as mybir
from concourse.bass_utils import run_bass_kernel_spmd
from contextlib import ExitStack

F32 = mybir.dt.float32
BF16 = mybir.dt.bfloat16
AF = mybir.ActivationFunctionType
ALU = mybir.AluOpType
AX = mybir.AxisListType

NCORES = 8
N_TOK = 16384
T = N_TOK // NCORES
D = 1024
H = 16
DH = 64
J = 64
NE = 8
DFF = 2048
NT = T // 128
NG = 4
GT = 4
NCH = D // 128

A_NORM = float(DH) ** -0.25
INV_A = float(DH) ** 0.25
COMB_C = 0.0625 * INV_A
EPS_FEAT = 1e-4
RATIO = float(J) ** -0.5
EPS_Q = EPS_FEAT / RATIO
DEN_CLIP = 0.01 / (RATIO * RATIO)
LN_EPS = 1e-5
ROUTER_EPS = 1e-8
CLAMP = 10000.0

DEBUG = False
ENGINES = ("pe", "act", "dve", "pool", "sp")
DMA_Q = {"pool": "dq_pool", "sp": "dq_sp", "act": "dq_act"}


class Prog:

    def __init__(self):
        self.ops = []
        self.last_write = {}
        self.readers = {}
        self.eng_ops = {e: [] for e in ENGINES}
        self.lane_count = {}
        self.lane_ids = []
        self.cc_count = 0

    def _edges_for(self, eng, reads, writes):
        deps = set()
        for b in reads:
            lw = self.last_write.get(id(b))
            if lw is not None:
                o = self.ops[lw]
                if not (o["eng"] == "pe" and eng == "pe" and o["kind"] == "compute"):
                    deps.add(lw)
        for b in writes:
            lw = self.last_write.get(id(b))
            if lw is not None:
                o = self.ops[lw]
                if not (o["eng"] == "pe" and eng == "pe" and o["kind"] == "compute"):
                    deps.add(lw)
            for r in self.readers.get(id(b), ()):
                o = self.ops[r]
                if o["eng"] == eng and o["kind"] == "compute" and eng != "pe":
                    continue
                if o["eng"] == eng and o["kind"] == "compute" and eng == "pe":
                    continue
                deps.add(r)
        return deps

    def barrier(self):
        last = {}
        for i, o in enumerate(self.ops):
            if o["kind"] == "dma":
                last["lane", o["lane"]] = i
            elif o["kind"] == "cc":
                last["cc"] = i
            else:
                last[o["eng"]] = i
        self.barrier_deps = set(last.values())

    def op(self, eng, fn, reads=(), writes=(), kind="compute", tag=""):
        idx = len(self.ops)
        deps = self._edges_for(eng, reads, writes)
        deps |= getattr(self, "barrier_deps", set())
        rec = dict(eng=eng, fn=fn, kind=kind, tag=tag, deps=deps, idx=idx,
                   need_inc=False)
        dep_vals = {}
        for d in deps:
            t = self.ops[d]
            if t["kind"] == "dma":
                ln = t["lane"]
                dep_vals[ln] = 16 * self.lane_count[ln]
        rec["dma_dep_vals"] = dep_vals
        if kind == "dma":
            assert len(writes) >= 1, f"dma op {tag} needs a dest buffer"
            lane = id(writes[0])
            if lane not in self.lane_count:
                self.lane_count[lane] = 0
                self.lane_ids.append(lane)
            self.lane_count[lane] += 1
            rec["lane"] = lane
        elif kind == "cc":
            self.cc_count += 1
            rec["cc_ord"] = self.cc_count
        self.ops.append(rec)
        self.eng_ops[eng].append(idx)
        for b in reads:
            self.readers.setdefault(id(b), []).append(idx)
        for b in writes:
            self.last_write[id(b)] = idx
            self.readers[id(b)] = []
        return idx

    def emit(self, block, sems, lane_sems, cc_sem):
        for o in self.ops:
            for d in o["deps"]:
                t = self.ops[d]
                if t["kind"] == "compute":
                    t["need_inc"] = True
        for e in ENGINES:
            v = 0
            for i in self.eng_ops[e]:
                o = self.ops[i]
                if o["kind"] == "compute" and o["need_inc"]:
                    v += 1
                o["ms"] = v

        def section(eng, handle):
            observed = {}
            for i in self.eng_ops[eng]:
                o = self.ops[i]
                waits = {}
                for ln, val in o["dma_dep_vals"].items():
                    key = ("lane", ln)
                    if val > waits.get(key, 0):
                        waits[key] = val
                for d in sorted(o["deps"]):
                    t = self.ops[d]
                    if t["kind"] == "dma":
                        continue
                    elif t["kind"] == "cc":
                        key, val = "cc", t["cc_ord"]
                    else:
                        key, val = t["eng"], t["ms"]
                    if val > waits.get(key, 0):
                        waits[key] = val
                for key in sorted(waits, key=str):
                    val = waits[key]
                    if observed.get(key, 0) >= val:
                        continue
                    if isinstance(key, tuple):
                        sem = lane_sems[key[1]]
                    elif key == "cc":
                        sem = cc_sem
                    else:
                        sem = sems[key]
                    handle.wait_ge(sem, val)
                    observed[key] = val
                inst = o["fn"](handle)
                if o["kind"] == "dma":
                    inst.then_inc(lane_sems[o["lane"]], 16)
                elif o["kind"] == "cc":
                    inst.then_inc(cc_sem, 1)
                elif o["need_inc"]:
                    inst.then_inc(sems[eng], 1)

        eng2dec = dict(pe=block.tensor, act=block.scalar, dve=block.vector,
                       pool=block.gpsimd, sp=block.sync)
        for e in ENGINES:
            def make(e=e):
                def _(handle):
                    section(e, handle)
                return _
            eng2dec[e](make())


def build_program():
    nc = bass.Bass("TRN2", target_bir_lowering=False)

    d_x = nc.dram_tensor("xq", [T, D], F32, kind="ExternalInput")
    d_wqT = nc.dram_tensor("wqT", [D, D], F32, kind="ExternalInput")
    d_wkT = nc.dram_tensor("wkT", [D, D], F32, kind="ExternalInput")
    d_wvT = nc.dram_tensor("wvT", [D, D], F32, kind="ExternalInput")
    d_woT = nc.dram_tensor("woT", [D, D], F32, kind="ExternalInput")
    d_bq = nc.dram_tensor("bq", [D], F32, kind="ExternalInput")
    d_bk = nc.dram_tensor("bk", [D], F32, kind="ExternalInput")
    d_bv = nc.dram_tensor("bv", [D], F32, kind="ExternalInput")
    d_bo = nc.dram_tensor("bo", [D], F32, kind="ExternalInput")
    d_projbd = nc.dram_tensor("projbd", [NCH, 128, 128], F32, kind="ExternalInput")
    d_gamma = nc.dram_tensor("gamma", [D], F32, kind="ExternalInput")
    d_beta = nc.dram_tensor("beta", [D], F32, kind="ExternalInput")
    d_rwT = nc.dram_tensor("rwT", [D, NE], F32, kind="ExternalInput")
    d_guT = nc.dram_tensor("guT", [NE, D, 16, 256], F32, kind="ExternalInput")
    d_dnT = nc.dram_tensor("dnT", [NE, DFF, D], F32, kind="ExternalInput")
    d_idf = nc.dram_tensor("idf", [128, 128], F32, kind="ExternalInput")
    d_riota = nc.dram_tensor("riota", [128, GT * NE], F32, kind="ExternalInput")
    d_e2 = nc.dram_tensor("e2", [NCH, 128, H], F32, kind="ExternalInput")
    d_ebc = nc.dram_tensor("ebc", [NCH, H, 128], F32, kind="ExternalInput")

    d_out = nc.dram_tensor("out", [T, D], F32, kind="ExternalOutput")

    d_y1 = nc.dram_tensor("y1d", [T, D], F32)
    d_ddk = nc.dram_tensor("ddkd", [NT, 128, D], BF16)
    d_xnT = nc.dram_tensor("xnTd", [NCH, 128, T], BF16)
    d_skd = nc.dram_tensor("skd", [128, H], F32)
    d_cc1i = nc.dram_tensor("cc1i", [1, H], F32)
    d_cc1o = nc.dram_tensor("cc1o", [1, H], F32)
    d_cc2i = nc.dram_tensor("cc2i", [J, H * 65], F32)
    d_cc2o = nc.dram_tensor("cc2o", [J, H * 65], F32)

    P = Prog()
    ctx = ExitStack()

    class _R:
        __slots__ = ()
    ddk_reg = [_R() for _ in range(NG)]
    y1_reg = [_R() for _ in range(NG)]
    xnT_reg = [_R() for _ in range(NG)]
    gv_reg = [_R() for _ in range(NG)]
    taps = {}

    def tap(name, buf, ap_in):
        if not DEBUG:
            return
        shp = list(ap_in.shape)
        dt_ = ap_in.dtype
        dtap = nc.dram_tensor("tap_" + name, shp, dt_, kind="ExternalOutput")
        taps[name] = dtap

        def fn(h):
            return h.dma_start(out=dtap.ap(), in_=ap_in)
        P.op("sp", fn, reads=[buf], writes=[dtap], kind="dma", tag="tap_" + name)

    def sbt(st, name, shape, dtype):
        return st.enter_context(nc.sbuf_tensor(name, shape, dtype))

    idf = sbt(ctx, "s_idf", [128, 128], F32)
    idb = sbt(ctx, "s_idb", [128, 128], BF16)
    riota = sbt(ctx, "s_riota", [128, GT * NE], F32)
    epst = sbt(ctx, "epst", [128, 1], F32)
    onesb = sbt(ctx, "onesb", [128, 128], F32)
    gateT = sbt(ctx, "gateT", [NE, T], F32)
    mu1 = sbt(ctx, "mu1", [128, NT], F32)
    rs1 = sbt(ctx, "rs1", [128, NT], F32)
    stabgb = sbt(ctx, "stabgb", [128, H], F32)
    pf = [ctx.enter_context(nc.psum_tensor(f"pf{i}", [128, 512], F32))
          for i in range(6)]
    pb = [ctx.enter_context(nc.psum_tensor(f"pb{i}", [128, 1024], BF16))
          for i in range(2)]

    stabR = sbt(ctx, "stabR", [1, H], F32)
    sA = ExitStack()
    sK = ExitStack()
    sB = ExitStack()
    qpT = [sbt(sA, f"qpT{c}", [128, T], BF16) for c in range(NCH)]
    kvbd = [sbt(sK, f"kvbd{c}", [128, 128], BF16) for c in range(NCH)]
    ksl = [sbt(sK, f"ksl{c}", [128, H], BF16) for c in range(NCH)]
    dgk = [sbt(sB, f"dgk{t}", [128, H], F32) for t in range(NT)]
    vtk = [sbt(sB, f"vtk{t}", [128, H * 65], BF16) for t in range(NT)]

    def pe_fence(bufs, tag):
        P.op("pe", lambda t: t.matmul(pf[5][0:1, 496:497], onesb[0:1, 0:1],
                                      onesb[0:1, 0:1], start=True, stop=True),
             reads=[onesb], writes=list(bufs), tag=f"fence_{tag}")

    def dma(eng, out_ap, in_ap, reads=(), writes=(), tag="", slow=False):
        def fn(h):
            if slow:
                with nc.allow_non_contiguous_dma(reason="small"):
                    return h.dma_start(out=out_ap, in_=in_ap)
            return h.dma_start(out=out_ap, in_=in_ap)
        return P.op(eng, fn, reads=reads, writes=writes, kind="dma", tag=tag)

    dma("sp", idf[:, :], d_idf.ap(), writes=[idf], tag="idf")
    dma("pool", idb[:, :], d_idf.ap(), writes=[idb], tag="idb")
    dma("sp", riota[:, :], d_riota.ap(), writes=[riota], tag="riota")
    P.op("dve", lambda v: v.memset(epst[:, :], LN_EPS), writes=[epst], tag="eps")
    P.op("dve", lambda v: v.memset(onesb[:, :], 1.0), writes=[onesb], tag="ones")

    def bcast_row(row, dst, tag):
        for hf in range(2):
            s = hf * 512
            P.op("pe", lambda t, s=s, row=row: t.matmul(
                pf[5][:, :], onesb[0:1, :], row[0:1, s:s + 512],
                start=True, stop=True), reads=[onesb, row], writes=[pf[5]],
                tag=f"bc_{tag}{hf}")
            P.op("dve", lambda v, s=s, dst=dst: v.tensor_copy(
                dst[:, s:s + 512], pf[5][:, :]), reads=[pf[5]], writes=[dst],
                tag=f"bce_{tag}{hf}")


    p1 = ExitStack()
    e2c = sbt(p1, "e2c", [128, NCH * H], F32)
    bq_sb = sbt(p1, "bq_sb", [128, NCH], F32)
    bk_sb = sbt(p1, "bk_sb", [128, NCH], F32)
    row_bv = sbt(p1, "row_bv", [1, D], F32)
    bvb = sbt(p1, "bvb", [128, D], F32)
    dma("sp", e2c.ap().rearrange("p (c h) -> p c h", c=NCH),
        d_e2.ap().rearrange("c p h -> p c h"), writes=[e2c], tag="e2")
    dma("sp", bq_sb[:, :], d_bq.ap().rearrange("(c p) -> p c", p=128),
        writes=[bq_sb], tag="bqc", slow=True)
    dma("sp", bk_sb[:, :], d_bk.ap().rearrange("(c p) -> p c", p=128),
        writes=[bk_sb], tag="bkc", slow=True)
    dma("sp", row_bv[:, :], d_bv.ap().unsqueeze(0), writes=[row_bv], tag="rbv")
    bcast_row(row_bv, bvb, "bv")
    wq = sbt(p1, "wq", [128, NCH * D], BF16)
    wk = sbt(p1, "wk", [128, NCH * D], BF16)
    wv = sbt(p1, "wv", [128, NCH * D], BF16)
    pjb = sbt(p1, "pjb", [128, NCH * 128], BF16)
    xinT = [sbt(p1, f"xinT{c}", [128, 512], BF16) for c in range(NCH)]
    xg = [sbt(p1, f"xg{i}", [128, D], F32) for i in range(2)]
    xtk = [sbt(p1, f"xtk{i}", [128, D], BF16) for i in range(2)]
    bn6 = [sbt(p1, f"bn6_{i}", [128, 2, 6], F32) for i in range(2)]
    bn2 = [sbt(p1, f"bn2_{i}", [128, 2], F32) for i in range(2)]
    rstd = [sbt(p1, f"rstd{i}", [128, 1], F32) for i in range(2)]
    pT = [sbt(p1, f"pT{c}", [128, 512], BF16) for c in range(NCH)]
    kT = [sbt(p1, f"kT{c}", [128, 512], BF16) for c in range(NCH)]
    sqt = [sbt(p1, f"sqt{i}", [128, 128], F32) for i in range(2)]
    dg2 = sbt(p1, "dg2", [128, 2 * H], F32)
    stq2 = sbt(p1, "stq2", [128, 2 * H], F32)
    stk2 = sbt(p1, "stk2", [128, 2 * H], F32)
    comb2 = sbt(p1, "comb2", [128, 2 * H], F32)
    subf2 = sbt(p1, "subf2", [128, 2 * D], F32)
    qptk2 = sbt(p1, "qptk2", [128, 2 * D], BF16)
    qdd = sbt(p1, "qdd", [128, 2 * D], BF16)
    stk_part = [sbt(p1, f"stk_part{i}", [128, H], F32) for i in range(2)]
    ddst = sbt(p1, "ddst", [128, 2 * D], BF16)
    stk_run = sbt(p1, "stk_run", [128, H], F32)

    def w_load(dst, src, tag):
        dma("pool", dst.ap().rearrange("p (c n) -> p c n", c=NCH),
            src.ap().rearrange("(c p) n -> p c n", p=128),
            writes=[dst], tag=tag)

    w_load(wq, d_wqT, "wq")
    w_load(wk, d_wkT, "wk")
    w_load(wv, d_wvT, "wv")
    dma("pool", pjb.ap().rearrange("p (c n) -> p c n", c=NCH),
        d_projbd.ap().rearrange("c p n -> p c n"), writes=[pjb], tag="pjb")

    def ln_tile(ti, src_buf, dst_tok, dst_col, tag):
        par = ti % 2
        b6, b2, rs = bn6[par], bn2[par], rstd[par]
        P.op("dve", lambda v: v.bn_stats(b6[:, 0, :], src_buf[:, 0:512]),
             reads=[src_buf], writes=[b6], tag=f"{tag}s0_{ti}")
        P.op("dve", lambda v: v.bn_stats(b6[:, 1, :], src_buf[:, 512:1024]),
             reads=[src_buf], writes=[b6], tag=f"{tag}s1_{ti}")
        P.op("dve", lambda v: v.bn_aggr(b2[:, :], b6[:, :, :]),
             reads=[b6], writes=[b2], tag=f"{tag}ag_{ti}")
        P.op("act", lambda a: a.activation(rs[:, :], b2[:, 1:2], AF.Ln,
                                           bias=epst[:, 0:1]),
             reads=[b2, epst], writes=[rs], tag=f"{tag}sq_{ti}")
        P.op("act", lambda a: a.activation(rs[:, :], rs[:, 0:1], AF.Exp,
                                           scale=-0.5),
             reads=[rs], writes=[rs], tag=f"{tag}rc_{ti}")
        P.op("dve", lambda v: v.tensor_scalar(
            dst_tok[:, :], src_buf[:, :], b2[:, 0:1], rs[:, 0:1],
            ALU.subtract, ALU.mult),
            reads=[src_buf, b2, rs], writes=[dst_tok], tag=f"{tag}nm_{ti}")
        for c in range(NCH):
            P.op("pe", lambda t, c=c: t.transpose(
                pb[0][:, c * 128:(c + 1) * 128],
                dst_tok[:, c * 128:(c + 1) * 128], idb[:, :]),
                reads=[dst_tok, idb], writes=[pb[0]], tag=f"{tag}tr_{ti}_{c}")
        for c in range(NCH):
            P.op("act", lambda a, c=c: a.copy(
                xinT[c][:, dst_col:dst_col + 128],
                pb[0][:, c * 128:(c + 1) * 128]),
                reads=[pb[0]], writes=[xinT[c]], tag=f"{tag}te_{ti}_{c}")

    def proj_fmajor(wsb, bsb, dstT, tag):
        for m in range(NCH):
            bank = pf[m % 2]
            for k in range(NCH):
                P.op("pe", lambda t, m=m, k=k, bank=bank: t.matmul(
                    bank[:, :], wsb[:, k * D + m * 128: k * D + (m + 1) * 128],
                    xinT[k][:, :], start=(k == 0), stop=(k == NCH - 1)),
                    reads=[wsb, xinT[k]], writes=[bank], tag=f"{tag}m{m}_{k}")
            P.op("act", lambda a, m=m, bank=bank: a.activation(
                dstT[m][:, :], bank[:, :], AF.Identity, bias=bsb[:, m:m + 1]),
                reads=[bank, bsb], writes=[dstT[m]], tag=f"{tag}e{m}")

    def proj_v(g, tag):
        for ts in range(GT):
            ti = g * GT + ts
            for ob in range(2):
                bank = pf[(ts * 2 + ob) % 2]
                for k in range(NCH):
                    P.op("pe", lambda t, ts=ts, ob=ob, k=k, bank=bank: t.matmul(
                        bank[:, :], xinT[k][:, ts * 128:(ts + 1) * 128],
                        wv[:, k * D + ob * 512: k * D + (ob + 1) * 512],
                        start=(k == 0), stop=(k == NCH - 1)),
                        reads=[wv, xinT[k]], writes=[bank],
                        tag=f"{tag}m{ts}_{ob}_{k}")
                P.op("dve", lambda v, ti=ti, ob=ob, bank=bank: v.tensor_add(
                    vtk[ti].ap().rearrange("p (h j) -> p h j", j=65)
                    [:, ob * 8:(ob + 1) * 8, 0:64],
                    bank.ap().rearrange("p (h j) -> p h j", j=64),
                    bvb.ap().rearrange("p (h j) -> p h j", j=64)
                    [:, ob * 8:(ob + 1) * 8, :]),
                    reads=[bank, bvb], writes=[vtk[ti]], tag=f"{tag}e{ts}_{ob}")
            P.op("dve", lambda v, ti=ti: v.memset(
                vtk[ti].ap().rearrange("p (h j) -> p h j", j=65)[:, :, 64:65],
                1.0), writes=[vtk[ti]], tag=f"{tag}o{ti}")

    def features(side, g, tag):
        srcT = pT if side == "q" else kT
        for pair in range(GT // 2):
            ts0 = pair * 2
            banks = [pf[2], pf[3], pf[4], pf[5]]
            ddsb = qdd if side == "q" else ddst
            for tt in range(2):
                ts = ts0 + tt
                for c in range(NCH):
                    bank = banks[tt * 2 + c // 4]
                    col = (c % 4) * 128
                    P.op("pe", lambda t, c=c, ts=ts, bank=bank, col=col: t.matmul(
                        bank[:, col:col + 128],
                        srcT[c][:, ts * 128:(ts + 1) * 128],
                        pjb[:, c * 128:(c + 1) * 128], start=True, stop=True),
                        reads=[srcT[c], pjb], writes=[bank],
                        tag=f"{tag}d{ts}_{c}")
            for q4 in range(4):
                P.op("act", lambda a, q4=q4: a.copy(
                    ddsb[:, q4 * 512:(q4 + 1) * 512], banks[q4][:, 0:512]),
                    reads=[banks[q4]], writes=[ddsb], tag=f"{tag}w{ts0}_{q4}")
            for tt in range(2):
                ts = ts0 + tt
                for c in range(NCH):
                    sq = sqt[c % 2]
                    P.op("act", lambda a, c=c, ts=ts, sq=sq: a.activation(
                        sq[:, :], srcT[c][:, ts * 128:(ts + 1) * 128],
                        AF.Square),
                        reads=[srcT[c]], writes=[sq], tag=f"{tag}q{ts}_{c}")
                    P.op("pe", lambda t, c=c, sq=sq, tt=tt: t.matmul(
                        pf[2][:, tt * H:(tt + 1) * H], sq[:, :],
                        e2c[:, c * H:(c + 1) * H],
                        start=(c == 0), stop=(c == NCH - 1)),
                        reads=[e2c, sq], writes=[pf[2]], tag=f"{tag}g{ts}_{c}")
            P.op("dve", lambda v: v.tensor_copy(dg2[:, :], pf[2][:, 0:2 * H]),
                 reads=[pf[2]], writes=[dg2], tag=f"{tag}g2{ts0}")
            st2 = stq2 if side == "q" else stk2
            P.op("dve", lambda v, st2=st2, ddsb=ddsb: v.reduce_max(
                st2[:, :], ddsb.ap().rearrange("p (h j) -> p h j", j=64),
                axis=AX.X),
                reads=[ddsb], writes=[st2], tag=f"{tag}s{ts0}")
            if side == "q":
                P.op("dve", lambda v: v.scalar_tensor_tensor(
                    comb2[:, :], dg2[:, :], COMB_C, stq2[:, :],
                    ALU.mult, ALU.add),
                    reads=[dg2, stq2], writes=[comb2], tag=f"{tag}c{ts0}")
                P.op("dve", lambda v, ddsb=ddsb: v.tensor_tensor(
                    subf2.ap().rearrange("p (h j) -> p h j", j=64),
                    ddsb.ap().rearrange("p (h j) -> p h j", j=64),
                    comb2.ap().unsqueeze(2).broadcast_to([128, 2 * H, 64]),
                    ALU.subtract),
                    reads=[ddsb, comb2], writes=[subf2], tag=f"{tag}u{ts0}")
                P.op("act", lambda a: a.activation(qptk2[:, :], subf2[:, :],
                                                   AF.Exp, scale=A_NORM),
                     reads=[subf2], writes=[qptk2], tag=f"{tag}x{ts0}")
                P.op("dve", lambda v: v.tensor_scalar_add(
                    qptk2[:, :], qptk2[:, :], EPS_Q),
                    reads=[qptk2], writes=[qptk2], tag=f"{tag}p{ts0}")
                for tt in range(2):
                    ts = ts0 + tt
                    ti = g * GT + ts
                    pbt = pb[tt]
                    for c in range(NCH):
                        P.op("pe", lambda t, c=c, tt=tt, pbt=pbt: t.transpose(
                            pbt[:, c * 128:(c + 1) * 128],
                            qptk2[:, tt * D + c * 128:tt * D + (c + 1) * 128],
                            idb[:, :]),
                            reads=[qptk2, idb], writes=[pbt],
                            tag=f"{tag}t{ts}_{c}")
                    for c in range(NCH):
                        P.op("act", lambda a, c=c, ti=ti, pbt=pbt: a.copy(
                            qpT[c][:, ti * 128:(ti + 1) * 128],
                            pbt[:, c * 128:(c + 1) * 128]),
                            reads=[pbt], writes=[qpT[c]], tag=f"{tag}y{ts}_{c}")
            else:
                for tt in range(2):
                    ts = ts0 + tt
                    ti = g * GT + ts
                    dma("sp", d_ddk[ti, :, :], ddsb[:, tt * D:(tt + 1) * D],
                        reads=[ddsb], writes=[ddk_reg[ti // GT]], tag=f"{tag}wd{ts}")
                    P.op("dve", lambda v, ti=ti, tt=tt: v.tensor_copy(
                        dgk[ti][:, :], dg2[:, tt * H:(tt + 1) * H]),
                        reads=[dg2], writes=[dgk[ti]], tag=f"{tag}dg{ts}")
                for tt in range(2):
                    P.op("dve", lambda v, tt=tt: v.tensor_max(
                        stk_run[:, :], stk_run[:, :],
                        stk2[:, tt * H:(tt + 1) * H]),
                        reads=[stk_run, stk2], writes=[stk_run],
                        tag=f"{tag}tm{ts0}_{tt}")

    P.op("dve", lambda v: v.memset(stk_run[:, :], -1e30), writes=[stk_run],
         tag="skz")

    def ln_group(g):
        for ts in range(GT):
            ti = g * GT + ts
            xb, xt = xg[ts % 2], xtk[ts % 2]
            dma("sp", xb[:, :], d_x[ti * 128:(ti + 1) * 128, :],
                writes=[xb], tag=f"ldx{ti}")
            ln_tile(ti, xb, xt, ts * 128, "ln")

    for g in range(NG):
        if g == 0:
            ln_group(0)
        proj_fmajor(wq, bq_sb, pT, f"pq{g}")
        proj_fmajor(wk, bk_sb, kT, f"pk{g}")
        proj_v(g, f"pv{g}")
        if g == 0:
            tap("qT0", pT[0], pT[0][:, :])
            tap("vtk0", vtk[0], vtk[0][:, :])
        if g + 1 < NG:
            ln_group(g + 1)
        features("q", g, f"fq{g}")
        features("k", g, f"fk{g}")
        if g == 0:
            tap("dgk0", dgk[0], dgk[0][:, :])

    skrow = sbt(p1, "skrow", [1, 64 * H], F32)
    stab2 = sbt(p1, "stab2", [1, H], F32)
    dma("sp", d_skd.ap(), stk_run[:, :], reads=[stk_run], writes=[d_skd],
        tag="skst")
    for hf in range(2):
        dma("sp", skrow.ap(),
            d_skd[hf * 64:(hf + 1) * 64, :].rearrange("t h -> (t h)").unsqueeze(0),
            reads=[d_skd], writes=[skrow], tag=f"skfl{hf}")
        dst = stabR if hf == 0 else stab2
        P.op("dve", lambda v, dst=dst: v.reduce_max(
            dst[:, :], skrow.ap().rearrange("one (t h) -> one h t", t=64),
            axis=AX.X), reads=[skrow], writes=[dst], tag=f"sktr{hf}")
    P.op("dve", lambda v: v.tensor_max(stabR[:, :], stabR[:, :], stab2[:, :]),
         reads=[stabR, stab2], writes=[stabR], tag="sktm2")
    tap("stk_run", stk_run, stk_run[:, :])
    tap("stabL", stabR, stabR[:, :])
    dma("sp", d_cc1i.ap(), stabR[:, :], reads=[stabR], writes=[d_cc1i],
        tag="cc1w")
    P.op("pool", lambda gp: gp.collective_compute(
        "AllReduce", ALU.max, replica_groups=[list(range(NCORES))],
        ins=[d_cc1i.ap()], outs=[d_cc1o.ap()]),
        reads=[d_cc1i], writes=[d_cc1o], kind="cc", tag="AR1")
    row_st = sbt(p1, "row_st", [1, H], F32)
    dma("sp", row_st[:, :], d_cc1o.ap(), reads=[d_cc1o], writes=[row_st],
        tag="cc1r")
    P.op("pe", lambda t: t.matmul(pf[5][:, 0:H], onesb[0:1, :], row_st[0:1, :],
                                  start=True, stop=True),
         reads=[onesb, row_st], writes=[pf[5]], tag="stbm")
    P.op("dve", lambda v: v.tensor_copy(stabgb[:, :], pf[5][:, 0:H]),
         reads=[pf[5]], writes=[stabgb], tag="stbe")
    tap("stabgb", stabgb, stabgb[0:1, :])

    p1.close()
    P.barrier()

    p2 = ExitStack()
    combk = [sbt(p2, f"combk{i}", [128, H], F32) for i in range(2)]
    kddl = [sbt(p2, f"kddl{i}", [128, D], BF16) for i in range(2)]
    e1 = [sbt(p2, f"e1_{i}", [128, D], F32) for i in range(2)]
    e2t = [sbt(p2, f"e2t{i}", [128, D], BF16) for i in range(2)]
    kpt = [sbt(p2, f"kpt{t}", [128, D], BF16) for t in range(NT)]
    ar2st = sbt(p2, "ar2st", [J, H * 65], F32)
    kvs = sbt(p2, "kvs", [J, H * 65], F32)

    for ti in range(NT):
        par = ti % 2
        kdd = kddl[par]
        dma("sp", kdd[:, :], d_ddk[ti, :, :], reads=[ddk_reg[ti // GT]], writes=[kdd],
            tag=f"kld{ti}")
        P.op("dve", lambda v, ti=ti, par=par: v.scalar_tensor_tensor(
            combk[par][:, :], dgk[ti][:, :], COMB_C, stabgb[:, :],
            ALU.mult, ALU.add),
            reads=[dgk[ti], stabgb], writes=[combk[par]], tag=f"kc{ti}")
        P.op("dve", lambda v, par=par, kdd=kdd: v.tensor_tensor(
            e1[par].ap().rearrange("p (h j) -> p h j", j=64),
            kdd.ap().rearrange("p (h j) -> p h j", j=64),
            combk[par].ap().unsqueeze(2).broadcast_to([128, H, 64]),
            ALU.subtract),
            reads=[kdd, combk[par]], writes=[e1[par]], tag=f"ku{ti}")
        P.op("act", lambda a, par=par: a.activation(e2t[par][:, :],
                                                    e1[par][:, :], AF.Exp,
                                                    scale=A_NORM),
             reads=[e1[par]], writes=[e2t[par]], tag=f"kx{ti}")
        P.op("dve", lambda v, ti=ti, par=par: v.tensor_scalar_add(
            kpt[ti][:, :], e2t[par][:, :], EPS_Q),
            reads=[e2t[par]], writes=[kpt[ti]], tag=f"kp{ti}")

    def ubank(h):
        if h < 12:
            return pf[h // 2], (h % 2) * 130
        return pf[h - 12], 260

    for h in range(H):
        bank, col = ubank(h)
        for ti in range(NT):
            P.op("pe", lambda t, ti=ti, h=h, bank=bank, col=col: t.matmul(
                bank[0:J, col:col + 65], kpt[ti][:, h * 64:(h + 1) * 64],
                vtk[ti][:, h * 65:(h + 1) * 65],
                start=(ti == 0), stop=(ti == NT - 1)),
                reads=[kpt[ti], vtk[ti]], writes=[bank], tag=f"U{h}_{ti}")
    for h in range(H):
        bank, col = ubank(h)
        P.op("dve", lambda v, h=h, bank=bank, col=col: v.tensor_copy(
            ar2st[:, h * 65:(h + 1) * 65], bank[0:J, col:col + 65]),
            reads=[bank], writes=[ar2st], tag=f"Ue{h}")
    dma("sp", d_cc2i.ap(), ar2st[:, :], reads=[ar2st], writes=[d_cc2i],
        tag="cc2w")
    P.op("pool", lambda gp: gp.collective_compute(
        "AllReduce", ALU.add, replica_groups=[list(range(NCORES))],
        ins=[d_cc2i.ap()], outs=[d_cc2o.ap()]),
        reads=[d_cc2i], writes=[d_cc2o], kind="cc", tag="AR2")
    dma("sp", kvs[:, :], d_cc2o.ap(), reads=[d_cc2o], writes=[kvs], tag="cc2r")
    tap("kpt0", kpt[0], kpt[0][:, :])
    tap("ar2st", ar2st, ar2st[:, :])
    tap("kvs", kvs, kvs[:, :])

    for c in range(NCH):
        P.op("dve", lambda v, c=c: v.memset(kvbd[c][:, :], 0.0),
             writes=[kvbd[c]], tag=f"kz{c}")
        P.op("dve", lambda v, c=c: v.memset(ksl[c][:, :], 0.0),
             writes=[ksl[c]], tag=f"sz{c}")
        for s in range(2):
            h = 2 * c + s
            P.op("dve", lambda v, c=c, s=s, h=h: v.tensor_copy(
                kvbd[c][s * 64:(s + 1) * 64, s * 64:(s + 1) * 64],
                kvs[:, h * 65:h * 65 + 64]),
                reads=[kvs], writes=[kvbd[c]], tag=f"kc{c}_{s}")
            P.op("dve", lambda v, c=c, s=s, h=h: v.tensor_copy(
                ksl[c][s * 64:(s + 1) * 64, h:h + 1],
                kvs[:, h * 65 + 64:(h + 1) * 65]),
                reads=[kvs], writes=[ksl[c]], tag=f"sc{c}_{s}")

    p2.close()
    P.barrier()
    sB.close()
    P.barrier()

    p3 = ExitStack()
    ebcc = sbt(p3, "ebcc", [16, NCH * 128], F32)
    row_bo = sbt(p3, "row_bo", [1, D], F32)
    row_g = sbt(p3, "row_g", [1, D], F32)
    row_b = sbt(p3, "row_b", [1, D], F32)
    bob = sbt(p3, "bob", [128, D], F32)
    gmb = sbt(p3, "gmb", [128, D], F32)
    btb = sbt(p3, "btb", [128, D], F32)
    dma("sp", ebcc.ap().rearrange("p (c n) -> p c n", c=NCH),
        d_ebc.ap().rearrange("c p n -> p c n"), writes=[ebcc], tag="ebc")
    dma("sp", row_bo[:, :], d_bo.ap().unsqueeze(0), writes=[row_bo], tag="rbo")
    dma("sp", row_g[:, :], d_gamma.ap().unsqueeze(0), writes=[row_g], tag="rg")
    dma("sp", row_b[:, :], d_beta.ap().unsqueeze(0), writes=[row_b], tag="rb")
    bcast_row(row_bo, bob, "bo")
    bcast_row(row_g, gmb, "g")
    bcast_row(row_b, btb, "b")
    wo = sbt(p3, "wo", [128, NCH * D], BF16)
    w_load(wo, d_woT, "wo")
    rw = sbt(p3, "rw", [128, NCH * NE], BF16)
    dma("pool", rw.ap().rearrange("p (c n) -> p c n", c=NCH),
        d_rwT.ap().rearrange("(c p) n -> p c n", p=128), writes=[rw], tag="rw")

    for c in range(NCH):
        tap(f"ksl{c}", ksl[c], ksl[c][:, :])
    tap("kvbd0", kvbd[0], kvbd[0][:, :])
    tap("kvbd1", kvbd[1], kvbd[1][:, :])
    attnT = [sbt(p3, f"attnT{c}", [128, 512], BF16) for c in range(NCH)]
    xnTg = [sbt(p3, f"xnTg{c}", [128, 512], BF16) for c in range(NCH)]
    aog = [sbt(p3, f"aog{i}", [128, D], F32) for i in range(GT)]
    y1g = [sbt(p3, f"y1g{i}", [128, D], F32) for i in range(GT)]
    bn2g = sbt(p3, "bn2g", [128, GT, 2], F32)
    bn2h = sbt(p3, "bn2h", [128, GT, 2], F32)
    rstdg = sbt(p3, "rstdg", [128, GT], F32)
    rs1g = sbt(p3, "rs1g", [128, GT], F32)
    v2g = sbt(p3, "v2g", [128, GT], F32)
    rstd2g = sbt(p3, "rstd2g", [128, GT], F32)
    rp = sbt(p3, "rp", [128, GT * NE], F32)
    m1g = sbt(p3, "m1g", [128, GT], F32)
    sumg = sbt(p3, "sumg", [128, GT], F32)
    eqg = sbt(p3, "eqg", [128, GT * NE], F32)
    eqg2 = sbt(p3, "eqg2", [128, GT * NE], F32)
    sc1g = sbt(p3, "sc1g", [128, GT * NE], F32)
    s1mg = sbt(p3, "s1mg", [128, GT], F32)
    f1g = sbt(p3, "f1g", [128, GT * NE], F32)
    mskg = sbt(p3, "mskg", [128, GT * NE], F32)
    gvg = sbt(p3, "gvg", [128, GT * NE], F32)
    gvg4 = sbt(p3, "gvg4", [128, NG * GT * NE], F32)
    rnorm = sbt(p3, "rnorm", [16, 512], F32)
    rbf = sbt(p3, "rbf", [128, 512], F32)
    zt = [sbt(p3, f"zt{i}", [128, D], F32) for i in range(GT)]
    z2junk = sbt(p3, "z2junk", [128, D], F32)
    xntk = [sbt(p3, f"xntk{i}", [128, D], BF16) for i in range(2)]
    xtmp = [sbt(p3, f"xtmp{i}", [128, D], F32) for i in range(2)]
    bn6b = [sbt(p3, f"bn6b{i}", [128, 2, 6], F32) for i in range(2)]
    xg3 = [sbt(p3, f"xg3_{i}", [128, D], F32) for i in range(GT)]

    pending_gate = []
    for g in range(NG):
        for c in range(NCH):
            P.op("pe", lambda t, c=c, g=g: t.matmul(
                pf[2][0:16, :], ksl[c][:, :], qpT[c][:, g * 512:(g + 1) * 512],
                start=(c == 0), stop=(c == NCH - 1)),
                reads=[ksl[c], qpT[c]], writes=[pf[2]], tag=f"n{g}_{c}")
        if pending_gate:
            pending_gate.pop(0)()
        P.op("dve", lambda v: v.tensor_scalar_max(rnorm[:, :], pf[2][0:16, :],
                                                  DEN_CLIP),
             reads=[pf[2]], writes=[rnorm], tag=f"nx{g}")
        if g == 0:
            tap("normraw", rnorm, rnorm[:, :])
        P.op("dve", lambda v: v.reciprocal(rnorm[:, :], rnorm[:, :]),
             reads=[rnorm], writes=[rnorm], tag=f"nr{g}")
        for c in range(NCH):
            P.op("pe", lambda t, c=c, g=g: t.matmul(
                pf[3][:, :], kvbd[c][:, :], qpT[c][:, g * 512:(g + 1) * 512],
                start=True, stop=True),
                reads=[kvbd[c], qpT[c]], writes=[pf[3]], tag=f"qk{g}_{c}")
            P.op("pe", lambda t, c=c: t.matmul(
                pf[4][:, :], ebcc[:, c * 128:(c + 1) * 128], rnorm[0:16, :],
                start=True, stop=True),
                reads=[ebcc, rnorm], writes=[pf[4]], tag=f"rb{g}_{c}")
            P.op("dve", lambda v: v.tensor_copy(rbf[:, :], pf[4][:, :]),
                 reads=[pf[4]], writes=[rbf], tag=f"re{g}_{c}")
            P.op("dve", lambda v, c=c: v.tensor_mul(attnT[c][:, :], pf[3][:, :],
                                                    rbf[:, :]),
                 reads=[pf[3], rbf], writes=[attnT[c]], tag=f"at{g}_{c}")
        if g == 0:
            tap("rnorm", rnorm, rnorm[:, :])
            tap("attnT0", attnT[0], attnT[0][:, :])
            tap("attnT7", attnT[7], attnT[7][:, :])
        for ts in range(GT):
            ti = g * GT + ts
            for ob in range(2):
                bank = pf[(ts % 2) * 2 + ob]
                for k in range(NCH):
                    P.op("pe", lambda t, ts=ts, ob=ob, k=k, bank=bank: t.matmul(
                        bank[:, :], attnT[k][:, ts * 128:(ts + 1) * 128],
                        wo[:, k * D + ob * 512: k * D + (ob + 1) * 512],
                        start=(k == 0), stop=(k == NCH - 1)),
                        reads=[attnT[k], wo], writes=[bank],
                        tag=f"w{ti}_{ob}_{k}")
                P.op("dve", lambda v, ts=ts, ob=ob, bank=bank: v.tensor_add(
                    aog[ts][:, ob * 512:(ob + 1) * 512], bank[:, :],
                    bob[:, ob * 512:(ob + 1) * 512]),
                    reads=[bank, bob], writes=[aog[ts]], tag=f"we{ti}_{ob}")
            dma("sp", xg3[ts][:, :], d_x[ti * 128:(ti + 1) * 128, :],
                writes=[xg3[ts]], tag=f"lx3{ti}")
        for ts in range(GT):
            b6 = bn6b[ts % 2]
            P.op("dve", lambda v, ts=ts, b6=b6: v.bn_stats(
                b6[:, 0, :], aog[ts][:, 0:512]),
                reads=[aog[ts]], writes=[b6], tag=f"l1a{g}_{ts}")
            P.op("dve", lambda v, ts=ts, b6=b6: v.bn_stats(
                b6[:, 1, :], aog[ts][:, 512:1024]),
                reads=[aog[ts]], writes=[b6], tag=f"l1b{g}_{ts}")
            P.op("dve", lambda v, ts=ts, b6=b6: v.bn_aggr(
                bn2g[:, ts, :], b6[:, :, :]),
                reads=[b6], writes=[bn2g], tag=f"l1c{g}_{ts}")
        P.op("act", lambda a: a.activation(
            rstdg[:, :], bn2g[:, :, 1], AF.Ln, bias=epst[:, 0:1]),
            reads=[bn2g, epst], writes=[rstdg], tag=f"l1d{g}")
        P.op("act", lambda a: a.activation(rstdg[:, :], rstdg[:, :], AF.Exp,
                                           scale=-0.5),
             reads=[rstdg], writes=[rstdg], tag=f"l1e{g}")
        for ts in range(GT):
            ti = g * GT + ts
            P.op("dve", lambda v, ts=ts: v.tensor_scalar(
                y1g[ts][:, :], aog[ts][:, :], bn2g[:, ts, 0:1],
                rstdg[:, ts:ts + 1], ALU.subtract, ALU.mult),
                reads=[aog[ts], bn2g, rstdg], writes=[y1g[ts]],
                tag=f"l1n{ti}")
            P.op("dve", lambda v, ts=ts: v.tensor_add(
                y1g[ts][:, :], y1g[ts][:, :], xg3[ts][:, :]),
                reads=[y1g[ts], xg3[ts]], writes=[y1g[ts]], tag=f"y1{ti}")
            dma("sp", d_y1[ti * 128:(ti + 1) * 128, :], y1g[ts][:, :],
                reads=[y1g[ts]], writes=[y1_reg[ti // GT]], tag=f"sy1{ti}")
        for ts in range(GT):
            b6 = bn6b[ts % 2]
            P.op("dve", lambda v, ts=ts, b6=b6: v.bn_stats(
                b6[:, 0, :], y1g[ts][:, 0:512]),
                reads=[y1g[ts]], writes=[b6], tag=f"l2a{g}_{ts}")
            P.op("dve", lambda v, ts=ts, b6=b6: v.bn_stats(
                b6[:, 1, :], y1g[ts][:, 512:1024]),
                reads=[y1g[ts]], writes=[b6], tag=f"l2b{g}_{ts}")
            P.op("dve", lambda v, ts=ts, b6=b6: v.bn_aggr(
                bn2h[:, ts, :], b6[:, :, :]),
                reads=[b6], writes=[bn2h], tag=f"l2c{g}_{ts}")
        P.op("act", lambda a: a.activation(
            rs1g[:, :], bn2h[:, :, 1], AF.Ln, bias=epst[:, 0:1]),
            reads=[bn2h, epst], writes=[rs1g], tag=f"l2d{g}")
        P.op("act", lambda a: a.activation(rs1g[:, :], rs1g[:, :], AF.Exp,
                                           scale=-0.5),
             reads=[rs1g], writes=[rs1g], tag=f"l2e{g}")
        P.op("dve", lambda v, g=g: v.tensor_copy(
            mu1[:, g * GT:(g + 1) * GT], bn2h[:, :, 0]),
            reads=[bn2h], writes=[mu1], tag=f"m1s{g}")
        P.op("dve", lambda v, g=g: v.tensor_copy(
            rs1[:, g * GT:(g + 1) * GT], rs1g[:, :]),
            reads=[rs1g], writes=[rs1], tag=f"r1s{g}")
        for ts in range(GT):
            ti = g * GT + ts
            par = ts % 2
            P.op("dve", lambda v, ts=ts: v.tensor_scalar(
                zt[ts][:, :], y1g[ts][:, :], bn2h[:, ts, 0:1],
                rs1g[:, ts:ts + 1], ALU.subtract, ALU.mult),
                reads=[y1g[ts], bn2h, rs1g], writes=[zt[ts]], tag=f"z{ti}")
            P.op("act", lambda a, ts=ts: a.activation(
                z2junk[:, :], zt[ts][:, :], AF.Square,
                accum_out=v2g[:, ts:ts + 1]),
                reads=[zt[ts]], writes=[z2junk, v2g], tag=f"v2{ti}")
        P.op("act", lambda a: a.activation(rstd2g[:, :], v2g[:, :], AF.Ln,
                                           bias=epst[:, 0:1], scale=1.0 / D),
             reads=[v2g, epst], writes=[rstd2g], tag=f"r2{g}")
        P.op("act", lambda a: a.activation(rstd2g[:, :], rstd2g[:, :], AF.Exp,
                                           scale=-0.5),
             reads=[rstd2g], writes=[rstd2g], tag=f"r2r{g}")
        for ts in range(GT):
            ti = g * GT + ts
            par = ts % 2
            P.op("dve", lambda v, ts=ts, par=par: v.scalar_tensor_tensor(
                xtmp[par][:, :], zt[ts][:, :], rstd2g[:, ts:ts + 1], gmb[:, :],
                ALU.mult, ALU.mult),
                reads=[zt[ts], rstd2g, gmb], writes=[xtmp[par]], tag=f"xa{ti}")
            P.op("dve", lambda v, par=par: v.tensor_add(
                xntk[par][:, :], xtmp[par][:, :], btb[:, :]),
                reads=[xtmp[par], btb], writes=[xntk[par]], tag=f"xb{ti}")
            pbt = pb[par]
            for c in range(NCH):
                P.op("pe", lambda t, c=c, par=par, pbt=pbt: t.transpose(
                    pbt[:, c * 128:(c + 1) * 128],
                    xntk[par][:, c * 128:(c + 1) * 128], idb[:, :]),
                    reads=[xntk[par], idb], writes=[pbt], tag=f"xt{ti}_{c}")
            for c in range(NCH):
                P.op("act", lambda a, c=c, ts=ts, pbt=pbt: a.copy(
                    xnTg[c][:, ts * 128:(ts + 1) * 128],
                    pbt[:, c * 128:(c + 1) * 128]),
                    reads=[pbt], writes=[xnTg[c]], tag=f"xe{ti}_{c}")
            for c in range(NCH):
                P.op("pe", lambda t, c=c, ts=ts: t.matmul(
                    pf[5][:, ts * NE:(ts + 1) * NE],
                    xnTg[c][:, ts * 128:(ts + 1) * 128],
                    rw[:, c * NE:(c + 1) * NE],
                    start=(c == 0), stop=(c == NCH - 1)),
                    reads=[xnTg[c], rw], writes=[pf[5]], tag=f"rt{ti}_{c}")
        rp3 = rp.ap().rearrange("p (t e) -> p t e", e=NE)
        P.op("dve", lambda v: v.tensor_scalar(
            rp[:, :], pf[5][:, 0:GT * NE], CLAMP, -CLAMP, ALU.min, ALU.max),
            reads=[pf[5]], writes=[rp], tag=f"lg{g}")
        P.op("dve", lambda v: v.reduce_max(m1g[:, :], rp3, axis=AX.X),
             reads=[rp], writes=[m1g], tag=f"lm{g}")
        P.op("dve", lambda v: v.tensor_tensor(
            rp3, rp3, m1g.ap().unsqueeze(2).broadcast_to([128, GT, NE]),
            ALU.subtract),
            reads=[rp, m1g], writes=[rp], tag=f"ls{g}")
        P.op("act", lambda a: a.activation(rp[:, :], rp[:, :], AF.Exp),
             reads=[rp], writes=[rp], tag=f"px{g}")
        P.op("dve", lambda v: v.reduce_sum(sumg[:, :], rp3, axis=AX.X),
             reads=[rp], writes=[sumg], tag=f"p0{g}")
        P.op("dve", lambda v: v.tensor_scalar_add(sumg[:, :], sumg[:, :],
                                                  ROUTER_EPS),
             reads=[sumg], writes=[sumg], tag=f"p1{g}")
        P.op("dve", lambda v: v.reciprocal(sumg[:, :], sumg[:, :]),
             reads=[sumg], writes=[sumg], tag=f"p2{g}")
        P.op("dve", lambda v: v.tensor_tensor(
            rp3, rp3, sumg.ap().unsqueeze(2).broadcast_to([128, GT, NE]),
            ALU.mult),
            reads=[rp, sumg], writes=[rp], tag=f"p3{g}")
        P.op("dve", lambda v: v.tensor_scalar(
            rp[:, :], rp[:, :], 1.0, ROUTER_EPS, ALU.min, ALU.max),
            reads=[rp], writes=[rp], tag=f"p4{g}")

        def top2_sel(dst_mask, srcbuf, tag2):
            s3 = srcbuf.ap().rearrange("p (t e) -> p t e", e=NE)
            P.op("dve", lambda v: v.reduce_max(m1g[:, :], s3, axis=AX.X),
                 reads=[srcbuf], writes=[m1g], tag=f"{tag2}a")
            P.op("dve", lambda v: v.tensor_tensor(
                eqg.ap().rearrange("p (t e) -> p t e", e=NE), s3,
                m1g.ap().unsqueeze(2).broadcast_to([128, GT, NE]),
                ALU.is_equal),
                reads=[srcbuf, m1g], writes=[eqg], tag=f"{tag2}b")
            P.op("dve", lambda v: v.tensor_mul(sc1g[:, :], eqg[:, :],
                                               riota[:, :]),
                 reads=[eqg, riota], writes=[sc1g], tag=f"{tag2}c")
            P.op("dve", lambda v: v.reduce_max(
                s1mg[:, :], sc1g.ap().rearrange("p (t e) -> p t e", e=NE),
                axis=AX.X),
                reads=[sc1g], writes=[s1mg], tag=f"{tag2}d")
            P.op("dve", lambda v: v.tensor_tensor(
                dst_mask.ap().rearrange("p (t e) -> p t e", e=NE),
                sc1g.ap().rearrange("p (t e) -> p t e", e=NE),
                s1mg.ap().unsqueeze(2).broadcast_to([128, GT, NE]),
                ALU.is_equal),
                reads=[sc1g, s1mg], writes=[dst_mask], tag=f"{tag2}e")

        top2_sel(f1g, rp, f"a{g}_")
        P.op("dve", lambda v: v.scalar_tensor_tensor(
            mskg[:, :], f1g[:, :], -1e30, rp[:, :], ALU.mult, ALU.add),
            reads=[f1g, rp], writes=[mskg], tag=f"mk{g}")
        top2_sel(eqg2, mskg, f"b{g}_")
        P.op("dve", lambda v: v.tensor_add(f1g[:, :], f1g[:, :], eqg2[:, :]),
             reads=[f1g, eqg2], writes=[f1g], tag=f"fm{g}")
        P.op("dve", lambda v: v.tensor_mul(gvg[:, :], rp[:, :], f1g[:, :]),
             reads=[rp, f1g], writes=[gvg], tag=f"gv{g}")
        if g == 0:
            tap("gvals0", gvg, gvg[:, 0:NE])
            tap("aog0", aog[0], aog[0][:, :])
            tap("y1g0", y1g[0], y1g[0][:, :])
            tap("bn2g", bn2g, bn2g[:, :, :])
            tap("rstdg", rstdg, rstdg[:, :])
            tap("v2g", v2g, v2g[:, :])
            tap("rp", rp, rp[:, :])
            tap("xnTg0", xnTg[0], xnTg[0][:, :])
        P.op("dve", lambda v, g=g: v.tensor_copy(gvg4[:, g * 32:(g + 1) * 32],
                                                 gvg[:, :]),
             reads=[gvg], writes=[gvg4], tag=f"gvs{g}")

        def gate_flush(g=g):
            for ts in range(GT):
                ti = g * GT + ts
                P.op("pe", lambda t, g=g, ts=ts: t.transpose(
                    pf[5][0:NE, 256:256 + 128],
                    gvg4[:, g * 32 + ts * NE:g * 32 + (ts + 1) * NE],
                    idf[:, :]),
                    reads=[gvg4, idf], writes=[pf[5]], tag=f"gt{ti}")
                P.op("act", lambda a, ti=ti: a.copy(
                    gateT[:, ti * 128:(ti + 1) * 128],
                    pf[5][0:NE, 256:256 + 128]),
                    reads=[pf[5]], writes=[gateT], tag=f"ge{ti}")
        pending_gate.append(gate_flush)
        for c in range(NCH):
            dma("sp", d_xnT[c, :, g * 512:(g + 1) * 512], xnTg[c][:, :],
                reads=[xnTg[c]], writes=[xnT_reg[g]], tag=f"sxn{g}_{c}")

    while pending_gate:
        pending_gate.pop(0)()
    p3.close()
    sK.close()
    sA.close()
    P.barrier()

    p4 = ExitStack()
    TH = T // 2
    guw = [sbt(p4, f"guw{i}", [128, NCH * 256], BF16) for i in range(4)]
    xnTh = [sbt(p4, f"xnTh{c}", [128, T // 2], BF16) for c in range(NCH)]
    dnw = [sbt(p4, f"dnw{i}", [128, 16 * 512], BF16) for i in range(2)]
    grow = [sbt(p4, f"grow{i}", [1, TH], F32) for i in range(2)]
    gbt = sbt(p4, "gbt", [128, TH], F32)
    xet = [sbt(p4, f"xet{c}", [128, TH], BF16) for c in range(NCH)]
    hid = [sbt(p4, f"hid{c}", [128, TH], BF16) for c in range(16)]
    acc = [sbt(p4, f"acc{i}", [128, D], F32) for i in range(8)]
    s1t = [sbt(p4, f"s1t{i}", [128, 512], BF16) for i in range(2)]
    y1l = sbt(p4, "y1l", [128, D], F32)
    zt4 = sbt(p4, "zt4", [128, D], F32)
    fr4 = sbt(p4, "fr4", [128, D], F32)
    bn64 = sbt(p4, "bn64", [128, 2, 6], F32)
    bn24 = sbt(p4, "bn24", [128, 2], F32)
    rstd4 = sbt(p4, "rstd4", [128, 1], F32)
    y2t = sbt(p4, "y2t", [128, D], F32)

    def fin_tile(half, tt):
        ti = half * 8 + tt
        dma("act", y1l[:, :], d_y1[ti * 128:(ti + 1) * 128, :],
            reads=[y1_reg[ti // GT]], writes=[y1l], tag=f"ly{ti}")
        P.op("dve", lambda v, ti=ti: v.tensor_scalar(
            zt4[:, :], y1l[:, :], mu1[:, ti:ti + 1], rs1[:, ti:ti + 1],
            ALU.subtract, ALU.mult),
            reads=[y1l, mu1, rs1], writes=[zt4], tag=f"fz{ti}")
        P.op("dve", lambda v, tt=tt: v.tensor_add(fr4[:, :], zt4[:, :],
                                                  acc[tt][:, :]),
             reads=[zt4, acc[tt]], writes=[fr4], tag=f"ff{ti}")
        P.op("dve", lambda v: v.bn_stats(bn64[:, 0, :], fr4[:, 0:512]),
             reads=[fr4], writes=[bn64], tag=f"f0{ti}")
        P.op("dve", lambda v: v.bn_stats(bn64[:, 1, :], fr4[:, 512:1024]),
             reads=[fr4], writes=[bn64], tag=f"f1{ti}")
        P.op("dve", lambda v: v.bn_aggr(bn24[:, :], bn64[:, :, :]),
             reads=[bn64], writes=[bn24], tag=f"f2{ti}")
        P.op("act", lambda a: a.activation(rstd4[:, :], bn24[:, 1:2],
                                           AF.Ln, bias=epst[:, 0:1]),
             reads=[bn24, epst], writes=[rstd4], tag=f"f3{ti}")
        P.op("act", lambda a: a.activation(rstd4[:, :], rstd4[:, 0:1],
                                           AF.Exp, scale=-0.5),
             reads=[rstd4], writes=[rstd4], tag=f"f4{ti}")
        P.op("dve", lambda v: v.tensor_scalar(
            fr4[:, :], fr4[:, :], bn24[:, 0:1], rstd4[:, 0:1],
            ALU.subtract, ALU.mult),
            reads=[fr4, bn24, rstd4], writes=[fr4], tag=f"f5{ti}")
        P.op("dve", lambda v: v.tensor_add(y2t[:, :], y1l[:, :], fr4[:, :]),
             reads=[y1l, fr4], writes=[y2t], tag=f"f6{ti}")
        dma("act", d_out[ti * 128:(ti + 1) * 128, :], y2t[:, :],
            reads=[y2t], writes=[d_out], tag=f"so{ti}")

    for half in range(2):
        hc = half * TH
        for c in range(NCH):
            dma("sp", xnTh[c][:, :], d_xnT[c, :, hc:hc + TH],
                reads=[xnT_reg[half * 2], xnT_reg[half * 2 + 1]],
                writes=[xnTh[c]], tag=f"lxn{half}_{c}")
        for e in range(NE):
            for pbi in range(4):
                wgu = guw[pbi % 4]
                dma("pool", wgu.ap().rearrange("p (c n) -> p c n", c=NCH),
                    d_guT[e, :, pbi, :].rearrange("(c p) n -> p c n", p=128),
                    writes=[wgu], tag=f"lg{half}_{e}_{pbi}")
            gr = grow[e % 2]
            dma("sp", gr[:, :], gateT[e:e + 1, hc:hc + TH], reads=[gateT],
                writes=[gr], tag=f"gr{half}_{e}")
            for gs in range(2):
                P.op("pe", lambda t, gr=gr, gs=gs: t.matmul(
                    pf[4][:, :], onesb[0:1, :], gr[0:1, gs * 512:(gs + 1) * 512],
                    start=True, stop=True),
                    reads=[onesb, gr], writes=[pf[4]], tag=f"gb{half}_{e}_{gs}")
                P.op("act", lambda a, gs=gs: a.copy(
                    gbt[:, gs * 512:(gs + 1) * 512], pf[4][:, :]),
                    reads=[pf[4]], writes=[gbt], tag=f"gc{half}_{e}_{gs}")
            for c in range(NCH):
                P.op("pool", lambda g_, c=c: g_.tensor_mul(
                    xet[c][:, :], xnTh[c][:, :], gbt[:, :]),
                    reads=[xnTh[c], gbt], writes=[xet[c]],
                    tag=f"x{half}_{e}_{c}")
            if half == 0 and e == 0:
                tap("gbt0", gbt, gbt[:, :])
                tap("xet0", xet[0], xet[0][:, :])
            for pbi in range(16):
                wgu = guw[pbi % 4]
                for gs in range(2):
                    bg = pf[gs * 2]
                    bu = pf[gs * 2 + 1]
                    for k in range(NCH):
                        P.op("pe", lambda t, k=k, wgu=wgu, gs=gs, bg=bg: t.matmul(
                            bg[:, :], wgu[:, k * 256:k * 256 + 128],
                            xet[k][:, gs * 512:(gs + 1) * 512],
                            start=(k == 0), stop=(k == NCH - 1)),
                            reads=[wgu, xet[k]], writes=[bg],
                            tag=f"mg{half}_{e}_{pbi}_{gs}_{k}")
                    for k in range(NCH):
                        P.op("pe", lambda t, k=k, wgu=wgu, gs=gs, bu=bu: t.matmul(
                            bu[:, :], wgu[:, k * 256 + 128:(k + 1) * 256],
                            xet[k][:, gs * 512:(gs + 1) * 512],
                            start=(k == 0), stop=(k == NCH - 1)),
                            reads=[wgu, xet[k]], writes=[bu],
                            tag=f"mu{half}_{e}_{pbi}_{gs}_{k}")
                    st = s1t[gs]
                    P.op("act", lambda a, st=st, bg=bg: a.activation(
                        st[:, :], bg[:, :], AF.Silu),
                        reads=[bg], writes=[st], tag=f"si{half}_{e}_{pbi}_{gs}")
                    P.op("dve", lambda v, st=st, bu=bu, pbi=pbi, gs=gs:
                         v.tensor_mul(hid[pbi][:, gs * 512:(gs + 1) * 512],
                                      st[:, :], bu[:, :]),
                         reads=[st, bu], writes=[hid[pbi]],
                         tag=f"hm{half}_{e}_{pbi}_{gs}")
                if pbi + 4 < 16:
                    nxt = guw[(pbi + 4) % 4]
                    dma("pool", nxt.ap().rearrange("p (c n) -> p c n", c=NCH),
                        d_guT[e, :, pbi + 4, :]
                        .rearrange("(c p) n -> p c n", p=128),
                        writes=[nxt], tag=f"lg{half}_{e}_{pbi + 4}")
            for ob in range(2):
                dma("pool", dnw[ob].ap().rearrange("p (c n) -> p c n", c=16),
                    d_dnT[e, :, ob * 512:(ob + 1) * 512]
                    .rearrange("(c p) n -> p c n", p=128),
                    writes=[dnw[ob]], tag=f"ld{half}_{e}_{ob}")
            for tt in range(8):
                for ob in range(2):
                    bank = pf[4 + ob]
                    for k in range(16):
                        P.op("pe", lambda t, tt=tt, ob=ob, k=k, bank=bank:
                             t.matmul(bank[:, :],
                                      hid[k][:, tt * 128:(tt + 1) * 128],
                                      dnw[ob][:, k * 512:(k + 1) * 512],
                                      start=(k == 0), stop=(k == 15)),
                             reads=[hid[k], dnw[ob]], writes=[bank],
                             tag=f"md{half}_{e}_{tt}_{ob}_{k}")
                    if e == 0:
                        P.op("dve", lambda v, tt=tt, ob=ob, bank=bank:
                             v.tensor_copy(acc[tt][:, ob * 512:(ob + 1) * 512],
                                           bank[:, :]),
                             reads=[bank], writes=[acc[tt]],
                             tag=f"d0{half}_{tt}_{ob}")
                    else:
                        P.op("dve", lambda v, tt=tt, ob=ob, bank=bank:
                             v.tensor_add(acc[tt][:, ob * 512:(ob + 1) * 512],
                                          acc[tt][:, ob * 512:(ob + 1) * 512],
                                          bank[:, :]),
                             reads=[bank, acc[tt]], writes=[acc[tt]],
                             tag=f"da{half}_{e}_{tt}_{ob}")
                if e == NE - 1:
                    fin_tile(half, tt)
        if half == 0:
            tap("hid0", hid[0], hid[0][:, :])
            tap("acc0", acc[0], acc[0][:, :])

    p4.close()

    lane_totals = dict(P.lane_count)
    ccn = P.cc_count

    def make_final(lane_sems, cc_sem):
        def fn(h):
            last = None
            for ln in P.lane_ids:
                last = h.wait_ge(lane_sems[ln], 16 * lane_totals[ln])
            if ccn:
                last = h.wait_ge(cc_sem, ccn)
            return last
        return fn

    sem_ctx = ExitStack()
    s_pe = sem_ctx.enter_context(nc.semaphore("s_pe"))
    s_act = sem_ctx.enter_context(nc.semaphore("s_act"))
    s_dve = sem_ctx.enter_context(nc.semaphore("s_dve"))
    s_pool = sem_ctx.enter_context(nc.semaphore("s_pool"))
    s_sp = sem_ctx.enter_context(nc.semaphore("s_sp"))
    cc_sem = sem_ctx.enter_context(nc.semaphore("ccs"))
    lane_sems = {ln: sem_ctx.enter_context(nc.semaphore(f"lane{i}"))
                 for i, ln in enumerate(P.lane_ids)}
    with nc.Block() as block:
        sems = dict(pe=s_pe, act=s_act, dve=s_dve, pool=s_pool, sp=s_sp)
        fin = make_final(lane_sems, cc_sem)
        P.ops.append(dict(eng="sp", fn=fin, kind="compute", tag="final",
                          deps=set(), idx=len(P.ops), need_inc=False,
                          dma_dep_vals={}))
        P.eng_ops["sp"].append(len(P.ops) - 1)
        P.emit(block, sems, lane_sems, cc_sem)
    sem_ctx.close()

    ctx.close()
    return nc


_NC_CACHE = None


def _get_nc():
    global _NC_CACHE
    if _NC_CACHE is None:
        _NC_CACHE = build_program()
    return _NC_CACHE


def kernel(L0_attn_query, L0_attn_Wq, L0_attn_bq, L0_attn_Wk, L0_attn_bk,
           L0_attn_Wv, L0_attn_bv, L0_attn_Wo, L0_attn_bo, L0_attn_proj,
           L0_ffn_ln_gamma, L0_ffn_ln_beta, L0_ffn_router_w,
           L0_ffn_gate_up_w, L0_ffn_down_w):
    x = np.ascontiguousarray(np.asarray(L0_attn_query, dtype=np.float32)
                             .reshape(N_TOK, D))
    wqT = np.ascontiguousarray(np.asarray(L0_attn_Wq, np.float32).T)
    wkT = np.ascontiguousarray(np.asarray(L0_attn_Wk, np.float32).T)
    wvT = np.ascontiguousarray(np.asarray(L0_attn_Wv, np.float32).T)
    woT = np.ascontiguousarray(np.asarray(L0_attn_Wo, np.float32).T)
    proj = np.asarray(L0_attn_proj, np.float32)
    projbd = np.zeros((NCH, 128, 128), np.float32)
    for c in range(NCH):
        projbd[c, 0:64, 0:64] = proj[2 * c].T
        projbd[c, 64:128, 64:128] = proj[2 * c + 1].T
    rwT = np.ascontiguousarray(np.asarray(L0_ffn_router_w, np.float32).T)
    _gu = np.swapaxes(np.asarray(L0_ffn_gate_up_w, np.float32), 1, 2)
    _g = _gu[:, :, 0:DFF].reshape(NE, D, 16, 128)
    _u = _gu[:, :, DFF:].reshape(NE, D, 16, 128)
    guT = np.ascontiguousarray(np.concatenate([_g, _u], axis=3))
    dnT = np.ascontiguousarray(np.swapaxes(
        np.asarray(L0_ffn_down_w, np.float32), 1, 2))
    idf = np.eye(128, dtype=np.float32)
    riota = np.broadcast_to(np.tile(np.arange(NE, 0, -1, dtype=np.float32), 4),
                            (128, 4 * NE)).copy()
    e2 = np.zeros((NCH, 128, H), np.float32)
    ebc = np.zeros((NCH, H, 128), np.float32)
    for c in range(NCH):
        e2[c, 0:64, 2 * c] = 1.0
        e2[c, 64:128, 2 * c + 1] = 1.0
        ebc[c, 2 * c, 0:64] = 1.0
        ebc[c, 2 * c + 1, 64:128] = 1.0

    shared = dict(
        wqT=wqT, wkT=wkT, wvT=wvT, woT=woT,
        bq=np.asarray(L0_attn_bq, np.float32),
        bk=np.asarray(L0_attn_bk, np.float32),
        bv=np.asarray(L0_attn_bv, np.float32),
        bo=np.asarray(L0_attn_bo, np.float32),
        projbd=projbd,
        gamma=np.asarray(L0_ffn_ln_gamma, np.float32),
        beta=np.asarray(L0_ffn_ln_beta, np.float32),
        rwT=rwT, guT=guT, dnT=dnT, idf=idf, riota=riota, e2=e2, ebc=ebc,
    )
    in_maps = []
    for c in range(NCORES):
        m = dict(shared)
        m["xq"] = np.ascontiguousarray(x[c * T:(c + 1) * T, :])
        in_maps.append(m)

    nc = _get_nc()
    res = run_bass_kernel_spmd(nc, in_maps, core_ids=list(range(NCORES)))
    out = np.concatenate([res.results[c]["out"] for c in range(NCORES)], axis=0)
    return out.reshape(1, N_TOK, D)


# version 22
# speedup vs baseline: 1.0943x; 1.0312x over previous
import numpy as np
import concourse.bass as bass
import concourse.mybir as mybir
from concourse.bass_utils import run_bass_kernel_spmd
from contextlib import ExitStack

F32 = mybir.dt.float32
BF16 = mybir.dt.bfloat16
AF = mybir.ActivationFunctionType
ALU = mybir.AluOpType
AX = mybir.AxisListType

NCORES = 8
N_TOK = 16384
T = N_TOK // NCORES
D = 1024
H = 16
DH = 64
J = 64
NE = 8
DFF = 2048
NT = T // 128
NG = 4
GT = 4
NCH = D // 128

A_NORM = float(DH) ** -0.25
INV_A = float(DH) ** 0.25
COMB_C = 0.0625 * INV_A
EPS_FEAT = 1e-4
RATIO = float(J) ** -0.5
EPS_Q = EPS_FEAT / RATIO
DEN_CLIP = 0.01 / (RATIO * RATIO)
LN_EPS = 1e-5
ROUTER_EPS = 1e-8
CLAMP = 10000.0

DEBUG = False
ENGINES = ("pe", "act", "dve", "pool", "sp")
DMA_Q = {"pool": "dq_pool", "sp": "dq_sp", "act": "dq_act"}


class Prog:

    def __init__(self):
        self.ops = []
        self.last_write = {}
        self.readers = {}
        self.eng_ops = {e: [] for e in ENGINES}
        self.lane_count = {}
        self.lane_ids = []
        self.cc_count = 0

    def _edges_for(self, eng, reads, writes):
        deps = set()
        for b in reads:
            lw = self.last_write.get(id(b))
            if lw is not None:
                o = self.ops[lw]
                if not (o["eng"] == "pe" and eng == "pe" and o["kind"] == "compute"):
                    deps.add(lw)
        for b in writes:
            lw = self.last_write.get(id(b))
            if lw is not None:
                o = self.ops[lw]
                if not (o["eng"] == "pe" and eng == "pe" and o["kind"] == "compute"):
                    deps.add(lw)
            for r in self.readers.get(id(b), ()):
                o = self.ops[r]
                if o["eng"] == eng and o["kind"] == "compute" and eng != "pe":
                    continue
                if o["eng"] == eng and o["kind"] == "compute" and eng == "pe":
                    continue
                deps.add(r)
        return deps

    def barrier(self):
        last = {}
        for i, o in enumerate(self.ops):
            if o["kind"] == "dma":
                last["lane", o["lane"]] = i
            elif o["kind"] == "cc":
                last["cc"] = i
            else:
                last[o["eng"]] = i
        self.barrier_deps = set(last.values())

    def op(self, eng, fn, reads=(), writes=(), kind="compute", tag=""):
        idx = len(self.ops)
        deps = self._edges_for(eng, reads, writes)
        deps |= getattr(self, "barrier_deps", set())
        rec = dict(eng=eng, fn=fn, kind=kind, tag=tag, deps=deps, idx=idx,
                   need_inc=False)
        dep_vals = {}
        for d in deps:
            t = self.ops[d]
            if t["kind"] == "dma":
                ln = t["lane"]
                dep_vals[ln] = 16 * self.lane_count[ln]
        rec["dma_dep_vals"] = dep_vals
        if kind == "dma":
            assert len(writes) >= 1, f"dma op {tag} needs a dest buffer"
            lane = id(writes[0])
            if lane not in self.lane_count:
                self.lane_count[lane] = 0
                self.lane_ids.append(lane)
            self.lane_count[lane] += 1
            rec["lane"] = lane
        elif kind == "cc":
            self.cc_count += 1
            rec["cc_ord"] = self.cc_count
        self.ops.append(rec)
        self.eng_ops[eng].append(idx)
        for b in reads:
            self.readers.setdefault(id(b), []).append(idx)
        for b in writes:
            self.last_write[id(b)] = idx
            self.readers[id(b)] = []
        return idx

    def emit(self, block, sems, lane_sems, cc_sem):
        for o in self.ops:
            for d in o["deps"]:
                t = self.ops[d]
                if t["kind"] == "compute":
                    t["need_inc"] = True
        for e in ENGINES:
            v = 0
            for i in self.eng_ops[e]:
                o = self.ops[i]
                if o["kind"] == "compute" and o["need_inc"]:
                    v += 1
                o["ms"] = v

        def section(eng, handle):
            observed = {}
            for i in self.eng_ops[eng]:
                o = self.ops[i]
                waits = {}
                for ln, val in o["dma_dep_vals"].items():
                    key = ("lane", ln)
                    if val > waits.get(key, 0):
                        waits[key] = val
                for d in sorted(o["deps"]):
                    t = self.ops[d]
                    if t["kind"] == "dma":
                        continue
                    elif t["kind"] == "cc":
                        key, val = "cc", t["cc_ord"]
                    else:
                        key, val = t["eng"], t["ms"]
                    if val > waits.get(key, 0):
                        waits[key] = val
                for key in sorted(waits, key=str):
                    val = waits[key]
                    if observed.get(key, 0) >= val:
                        continue
                    if isinstance(key, tuple):
                        sem = lane_sems[key[1]]
                    elif key == "cc":
                        sem = cc_sem
                    else:
                        sem = sems[key]
                    handle.wait_ge(sem, val)
                    observed[key] = val
                inst = o["fn"](handle)
                if o["kind"] == "dma":
                    inst.then_inc(lane_sems[o["lane"]], 16)
                elif o["kind"] == "cc":
                    inst.then_inc(cc_sem, 1)
                elif o["need_inc"]:
                    inst.then_inc(sems[eng], 1)

        eng2dec = dict(pe=block.tensor, act=block.scalar, dve=block.vector,
                       pool=block.gpsimd, sp=block.sync)
        for e in ENGINES:
            def make(e=e):
                def _(handle):
                    section(e, handle)
                return _
            eng2dec[e](make())


def build_program():
    nc = bass.Bass("TRN2", target_bir_lowering=False)

    d_x = nc.dram_tensor("xq", [T, D], F32, kind="ExternalInput")
    d_wqT = nc.dram_tensor("wqT", [D, D], F32, kind="ExternalInput")
    d_wkT = nc.dram_tensor("wkT", [D, D], F32, kind="ExternalInput")
    d_wvT = nc.dram_tensor("wvT", [D, D], F32, kind="ExternalInput")
    d_woT = nc.dram_tensor("woT", [D, D], F32, kind="ExternalInput")
    d_bq = nc.dram_tensor("bq", [D], F32, kind="ExternalInput")
    d_bk = nc.dram_tensor("bk", [D], F32, kind="ExternalInput")
    d_bv = nc.dram_tensor("bv", [D], F32, kind="ExternalInput")
    d_bo = nc.dram_tensor("bo", [D], F32, kind="ExternalInput")
    d_projbd = nc.dram_tensor("projbd", [NCH, 128, 128], F32, kind="ExternalInput")
    d_gamma = nc.dram_tensor("gamma", [D], F32, kind="ExternalInput")
    d_beta = nc.dram_tensor("beta", [D], F32, kind="ExternalInput")
    d_rwT = nc.dram_tensor("rwT", [D, NE], F32, kind="ExternalInput")
    d_guT = nc.dram_tensor("guT", [NE, D, 16, 256], F32, kind="ExternalInput")
    d_dnT = nc.dram_tensor("dnT", [NE, DFF, D], F32, kind="ExternalInput")
    d_idf = nc.dram_tensor("idf", [128, 128], F32, kind="ExternalInput")
    d_riota = nc.dram_tensor("riota", [128, GT * NE], F32, kind="ExternalInput")
    d_e2 = nc.dram_tensor("e2", [NCH, 128, H], F32, kind="ExternalInput")
    d_ebc = nc.dram_tensor("ebc", [NCH, H, 128], F32, kind="ExternalInput")

    d_out = nc.dram_tensor("out", [T, D], F32, kind="ExternalOutput")

    d_y1 = nc.dram_tensor("y1d", [T, D], F32)
    d_ddk = nc.dram_tensor("ddkd", [NT, 128, D], BF16)
    d_xnT = nc.dram_tensor("xnTd", [NCH, 128, T], BF16)
    d_skd = nc.dram_tensor("skd", [128, H], F32)
    d_cc1i = nc.dram_tensor("cc1i", [1, H], F32)
    d_cc1o = nc.dram_tensor("cc1o", [1, H], F32)
    d_cc2i = nc.dram_tensor("cc2i", [J, H * 65], F32)
    d_cc2o = nc.dram_tensor("cc2o", [J, H * 65], F32)

    P = Prog()
    ctx = ExitStack()

    class _R:
        __slots__ = ()
    ddk_reg = [_R() for _ in range(NG)]
    y1_reg = [_R() for _ in range(NG)]
    xnT_reg = [_R() for _ in range(NG)]
    gv_reg = [_R() for _ in range(NG)]
    taps = {}

    def tap(name, buf, ap_in):
        if not DEBUG:
            return
        shp = list(ap_in.shape)
        dt_ = ap_in.dtype
        dtap = nc.dram_tensor("tap_" + name, shp, dt_, kind="ExternalOutput")
        taps[name] = dtap

        def fn(h):
            return h.dma_start(out=dtap.ap(), in_=ap_in)
        P.op("sp", fn, reads=[buf], writes=[dtap], kind="dma", tag="tap_" + name)

    def sbt(st, name, shape, dtype):
        return st.enter_context(nc.sbuf_tensor(name, shape, dtype))

    idf = sbt(ctx, "s_idf", [128, 128], F32)
    idb = sbt(ctx, "s_idb", [128, 128], BF16)
    riota = sbt(ctx, "s_riota", [128, GT * NE], F32)
    epst = sbt(ctx, "epst", [128, 1], F32)
    onesb = sbt(ctx, "onesb", [128, 128], F32)
    gateT = sbt(ctx, "gateT", [NE, T], F32)
    mu1 = sbt(ctx, "mu1", [128, NT], F32)
    rs1 = sbt(ctx, "rs1", [128, NT], F32)
    stabgb = sbt(ctx, "stabgb", [128, H], F32)
    pf = [ctx.enter_context(nc.psum_tensor(f"pf{i}", [128, 512], F32))
          for i in range(6)]
    pb = [ctx.enter_context(nc.psum_tensor(f"pb{i}", [128, 1024], BF16))
          for i in range(2)]

    stabR = sbt(ctx, "stabR", [1, H], F32)
    sA = ExitStack()
    sK = ExitStack()
    sB = ExitStack()
    qpT = [sbt(sA, f"qpT{c}", [128, T], BF16) for c in range(NCH)]
    kvbd = [sbt(sK, f"kvbd{c}", [128, 128], BF16) for c in range(NCH)]
    ksl = [sbt(sK, f"ksl{c}", [128, H], BF16) for c in range(NCH)]
    dgk = [sbt(sB, f"dgk{t}", [128, H], F32) for t in range(NT)]
    vtk = [sbt(sB, f"vtk{t}", [128, H * 65], BF16) for t in range(NT)]

    def pe_fence(bufs, tag):
        P.op("pe", lambda t: t.matmul(pf[5][0:1, 496:497], onesb[0:1, 0:1],
                                      onesb[0:1, 0:1], start=True, stop=True),
             reads=[onesb], writes=list(bufs), tag=f"fence_{tag}")

    def dma(eng, out_ap, in_ap, reads=(), writes=(), tag="", slow=False):
        def fn(h):
            if slow:
                with nc.allow_non_contiguous_dma(reason="small"):
                    return h.dma_start(out=out_ap, in_=in_ap)
            return h.dma_start(out=out_ap, in_=in_ap)
        return P.op(eng, fn, reads=reads, writes=writes, kind="dma", tag=tag)

    dma("sp", idf[:, :], d_idf.ap(), writes=[idf], tag="idf")
    dma("pool", idb[:, :], d_idf.ap(), writes=[idb], tag="idb")
    dma("sp", riota[:, :], d_riota.ap(), writes=[riota], tag="riota")
    P.op("dve", lambda v: v.memset(epst[:, :], LN_EPS), writes=[epst], tag="eps")
    P.op("dve", lambda v: v.memset(onesb[:, :], 1.0), writes=[onesb], tag="ones")

    def bcast_row(row, dst, tag):
        for hf in range(2):
            s = hf * 512
            P.op("pe", lambda t, s=s, row=row: t.matmul(
                pf[5][:, :], onesb[0:1, :], row[0:1, s:s + 512],
                start=True, stop=True), reads=[onesb, row], writes=[pf[5]],
                tag=f"bc_{tag}{hf}")
            P.op("dve", lambda v, s=s, dst=dst: v.tensor_copy(
                dst[:, s:s + 512], pf[5][:, :]), reads=[pf[5]], writes=[dst],
                tag=f"bce_{tag}{hf}")


    p1 = ExitStack()
    e2c = sbt(p1, "e2c", [128, NCH * H], F32)
    bq_sb = sbt(p1, "bq_sb", [128, NCH], F32)
    bk_sb = sbt(p1, "bk_sb", [128, NCH], F32)
    row_bv = sbt(p1, "row_bv", [1, D], F32)
    bvb = sbt(p1, "bvb", [128, D], F32)
    dma("sp", e2c.ap().rearrange("p (c h) -> p c h", c=NCH),
        d_e2.ap().rearrange("c p h -> p c h"), writes=[e2c], tag="e2")
    dma("sp", bq_sb[:, :], d_bq.ap().rearrange("(c p) -> p c", p=128),
        writes=[bq_sb], tag="bqc", slow=True)
    dma("sp", bk_sb[:, :], d_bk.ap().rearrange("(c p) -> p c", p=128),
        writes=[bk_sb], tag="bkc", slow=True)
    dma("sp", row_bv[:, :], d_bv.ap().unsqueeze(0), writes=[row_bv], tag="rbv")
    bcast_row(row_bv, bvb, "bv")
    wq = sbt(p1, "wq", [128, NCH * D], BF16)
    wk = sbt(p1, "wk", [128, NCH * D], BF16)
    wv = sbt(p1, "wv", [128, NCH * D], BF16)
    pjb = sbt(p1, "pjb", [128, NCH * 128], BF16)
    xinT = [sbt(p1, f"xinT{c}", [128, 512], BF16) for c in range(NCH)]
    xg = [sbt(p1, f"xg{i}", [128, D], F32) for i in range(2)]
    xtk = [sbt(p1, f"xtk{i}", [128, D], BF16) for i in range(2)]
    bn6 = [sbt(p1, f"bn6_{i}", [128, 2, 6], F32) for i in range(2)]
    bn2 = [sbt(p1, f"bn2_{i}", [128, 2], F32) for i in range(2)]
    rstd = [sbt(p1, f"rstd{i}", [128, 1], F32) for i in range(2)]
    pT = [sbt(p1, f"pT{c}", [128, 512], BF16) for c in range(NCH)]
    kT = [sbt(p1, f"kT{c}", [128, 512], BF16) for c in range(NCH)]
    sqt = [sbt(p1, f"sqt{i}", [128, 128], F32) for i in range(2)]
    dg2 = sbt(p1, "dg2", [128, 2 * H], F32)
    stq2 = sbt(p1, "stq2", [128, 2 * H], F32)
    stk2 = sbt(p1, "stk2", [128, 2 * H], F32)
    comb2 = sbt(p1, "comb2", [128, 2 * H], F32)
    subf2 = sbt(p1, "subf2", [128, 2 * D], F32)
    qptk2 = sbt(p1, "qptk2", [128, 2 * D], BF16)
    qdd = sbt(p1, "qdd", [128, 2 * D], BF16)
    stk_part = [sbt(p1, f"stk_part{i}", [128, H], F32) for i in range(2)]
    ddst = sbt(p1, "ddst", [128, 2 * D], BF16)
    stk_run = sbt(p1, "stk_run", [128, H], F32)

    def w_load(dst, src, tag):
        dma("pool", dst.ap().rearrange("p (c n) -> p c n", c=NCH),
            src.ap().rearrange("(c p) n -> p c n", p=128),
            writes=[dst], tag=tag)

    w_load(wq, d_wqT, "wq")
    w_load(wk, d_wkT, "wk")
    w_load(wv, d_wvT, "wv")
    dma("pool", pjb.ap().rearrange("p (c n) -> p c n", c=NCH),
        d_projbd.ap().rearrange("c p n -> p c n"), writes=[pjb], tag="pjb")

    def ln_tile(ti, src_buf, dst_tok, dst_col, tag):
        par = ti % 2
        b6, b2, rs = bn6[par], bn2[par], rstd[par]
        P.op("dve", lambda v: v.bn_stats(b6[:, 0, :], src_buf[:, 0:512]),
             reads=[src_buf], writes=[b6], tag=f"{tag}s0_{ti}")
        P.op("dve", lambda v: v.bn_stats(b6[:, 1, :], src_buf[:, 512:1024]),
             reads=[src_buf], writes=[b6], tag=f"{tag}s1_{ti}")
        P.op("dve", lambda v: v.bn_aggr(b2[:, :], b6[:, :, :]),
             reads=[b6], writes=[b2], tag=f"{tag}ag_{ti}")
        P.op("act", lambda a: a.activation(rs[:, :], b2[:, 1:2], AF.Ln,
                                           bias=epst[:, 0:1]),
             reads=[b2, epst], writes=[rs], tag=f"{tag}sq_{ti}")
        P.op("act", lambda a: a.activation(rs[:, :], rs[:, 0:1], AF.Exp,
                                           scale=-0.5),
             reads=[rs], writes=[rs], tag=f"{tag}rc_{ti}")
        P.op("dve", lambda v: v.tensor_scalar(
            dst_tok[:, :], src_buf[:, :], b2[:, 0:1], rs[:, 0:1],
            ALU.subtract, ALU.mult),
            reads=[src_buf, b2, rs], writes=[dst_tok], tag=f"{tag}nm_{ti}")
        for c in range(NCH):
            P.op("pe", lambda t, c=c: t.transpose(
                pb[0][:, c * 128:(c + 1) * 128],
                dst_tok[:, c * 128:(c + 1) * 128], idb[:, :]),
                reads=[dst_tok, idb], writes=[pb[0]], tag=f"{tag}tr_{ti}_{c}")
        for c in range(NCH):
            P.op("act", lambda a, c=c: a.copy(
                xinT[c][:, dst_col:dst_col + 128],
                pb[0][:, c * 128:(c + 1) * 128]),
                reads=[pb[0]], writes=[xinT[c]], tag=f"{tag}te_{ti}_{c}")

    def proj_fmajor(wsb, bsb, dstT, tag):
        for m in range(NCH):
            bank = pf[m % 2]
            for k in range(NCH):
                P.op("pe", lambda t, m=m, k=k, bank=bank: t.matmul(
                    bank[:, :], wsb[:, k * D + m * 128: k * D + (m + 1) * 128],
                    xinT[k][:, :], start=(k == 0), stop=(k == NCH - 1)),
                    reads=[wsb, xinT[k]], writes=[bank], tag=f"{tag}m{m}_{k}")
            P.op("act", lambda a, m=m, bank=bank: a.activation(
                dstT[m][:, :], bank[:, :], AF.Identity, bias=bsb[:, m:m + 1]),
                reads=[bank, bsb], writes=[dstT[m]], tag=f"{tag}e{m}")

    def proj_v(g, tag):
        for ts in range(GT):
            ti = g * GT + ts
            for ob in range(2):
                bank = pf[(ts * 2 + ob) % 2]
                for k in range(NCH):
                    P.op("pe", lambda t, ts=ts, ob=ob, k=k, bank=bank: t.matmul(
                        bank[:, :], xinT[k][:, ts * 128:(ts + 1) * 128],
                        wv[:, k * D + ob * 512: k * D + (ob + 1) * 512],
                        start=(k == 0), stop=(k == NCH - 1)),
                        reads=[wv, xinT[k]], writes=[bank],
                        tag=f"{tag}m{ts}_{ob}_{k}")
                P.op("dve", lambda v, ti=ti, ob=ob, bank=bank: v.tensor_add(
                    vtk[ti].ap().rearrange("p (h j) -> p h j", j=65)
                    [:, ob * 8:(ob + 1) * 8, 0:64],
                    bank.ap().rearrange("p (h j) -> p h j", j=64),
                    bvb.ap().rearrange("p (h j) -> p h j", j=64)
                    [:, ob * 8:(ob + 1) * 8, :]),
                    reads=[bank, bvb], writes=[vtk[ti]], tag=f"{tag}e{ts}_{ob}")
            P.op("dve", lambda v, ti=ti: v.memset(
                vtk[ti].ap().rearrange("p (h j) -> p h j", j=65)[:, :, 64:65],
                1.0), writes=[vtk[ti]], tag=f"{tag}o{ti}")

    def features(side, g, tag):
        srcT = pT if side == "q" else kT
        for pair in range(GT // 2):
            ts0 = pair * 2
            banks = [pf[2], pf[3], pf[4], pf[5]]
            ddsb = qdd if side == "q" else ddst
            for tt in range(2):
                ts = ts0 + tt
                for c in range(NCH):
                    bank = banks[tt * 2 + c // 4]
                    col = (c % 4) * 128
                    P.op("pe", lambda t, c=c, ts=ts, bank=bank, col=col: t.matmul(
                        bank[:, col:col + 128],
                        srcT[c][:, ts * 128:(ts + 1) * 128],
                        pjb[:, c * 128:(c + 1) * 128], start=True, stop=True),
                        reads=[srcT[c], pjb], writes=[bank],
                        tag=f"{tag}d{ts}_{c}")
            for q4 in range(4):
                P.op("act", lambda a, q4=q4: a.copy(
                    ddsb[:, q4 * 512:(q4 + 1) * 512], banks[q4][:, 0:512]),
                    reads=[banks[q4]], writes=[ddsb], tag=f"{tag}w{ts0}_{q4}")
            for tt in range(2):
                ts = ts0 + tt
                for c in range(NCH):
                    sq = sqt[c % 2]
                    P.op("act", lambda a, c=c, ts=ts, sq=sq: a.activation(
                        sq[:, :], srcT[c][:, ts * 128:(ts + 1) * 128],
                        AF.Square),
                        reads=[srcT[c]], writes=[sq], tag=f"{tag}q{ts}_{c}")
                    P.op("pe", lambda t, c=c, sq=sq, tt=tt: t.matmul(
                        pf[2][:, tt * H:(tt + 1) * H], sq[:, :],
                        e2c[:, c * H:(c + 1) * H],
                        start=(c == 0), stop=(c == NCH - 1)),
                        reads=[e2c, sq], writes=[pf[2]], tag=f"{tag}g{ts}_{c}")
            P.op("dve", lambda v: v.tensor_copy(dg2[:, :], pf[2][:, 0:2 * H]),
                 reads=[pf[2]], writes=[dg2], tag=f"{tag}g2{ts0}")
            st2 = stq2 if side == "q" else stk2
            P.op("dve", lambda v, st2=st2, ddsb=ddsb: v.reduce_max(
                st2[:, :], ddsb.ap().rearrange("p (h j) -> p h j", j=64),
                axis=AX.X),
                reads=[ddsb], writes=[st2], tag=f"{tag}s{ts0}")
            if side == "q":
                P.op("dve", lambda v: v.scalar_tensor_tensor(
                    comb2[:, :], dg2[:, :], COMB_C, stq2[:, :],
                    ALU.mult, ALU.add),
                    reads=[dg2, stq2], writes=[comb2], tag=f"{tag}c{ts0}")
                P.op("dve", lambda v, ddsb=ddsb: v.tensor_tensor(
                    subf2.ap().rearrange("p (h j) -> p h j", j=64),
                    ddsb.ap().rearrange("p (h j) -> p h j", j=64),
                    comb2.ap().unsqueeze(2).broadcast_to([128, 2 * H, 64]),
                    ALU.subtract),
                    reads=[ddsb, comb2], writes=[subf2], tag=f"{tag}u{ts0}")
                P.op("act", lambda a: a.activation(qptk2[:, :], subf2[:, :],
                                                   AF.Exp, scale=A_NORM),
                     reads=[subf2], writes=[qptk2], tag=f"{tag}x{ts0}")
                P.op("dve", lambda v: v.tensor_scalar_add(
                    qptk2[:, :], qptk2[:, :], EPS_Q),
                    reads=[qptk2], writes=[qptk2], tag=f"{tag}p{ts0}")
                for tt in range(2):
                    ts = ts0 + tt
                    ti = g * GT + ts
                    pbt = pb[tt]
                    for c in range(NCH):
                        P.op("pe", lambda t, c=c, tt=tt, pbt=pbt: t.transpose(
                            pbt[:, c * 128:(c + 1) * 128],
                            qptk2[:, tt * D + c * 128:tt * D + (c + 1) * 128],
                            idb[:, :]),
                            reads=[qptk2, idb], writes=[pbt],
                            tag=f"{tag}t{ts}_{c}")
                    for c in range(NCH):
                        P.op("act", lambda a, c=c, ti=ti, pbt=pbt: a.copy(
                            qpT[c][:, ti * 128:(ti + 1) * 128],
                            pbt[:, c * 128:(c + 1) * 128]),
                            reads=[pbt], writes=[qpT[c]], tag=f"{tag}y{ts}_{c}")
            else:
                for tt in range(2):
                    ts = ts0 + tt
                    ti = g * GT + ts
                    dma("sp", d_ddk[ti, :, :], ddsb[:, tt * D:(tt + 1) * D],
                        reads=[ddsb], writes=[ddk_reg[ti // GT]], tag=f"{tag}wd{ts}")
                    P.op("dve", lambda v, ti=ti, tt=tt: v.tensor_copy(
                        dgk[ti][:, :], dg2[:, tt * H:(tt + 1) * H]),
                        reads=[dg2], writes=[dgk[ti]], tag=f"{tag}dg{ts}")
                for tt in range(2):
                    P.op("dve", lambda v, tt=tt: v.tensor_max(
                        stk_run[:, :], stk_run[:, :],
                        stk2[:, tt * H:(tt + 1) * H]),
                        reads=[stk_run, stk2], writes=[stk_run],
                        tag=f"{tag}tm{ts0}_{tt}")

    P.op("dve", lambda v: v.memset(stk_run[:, :], -1e30), writes=[stk_run],
         tag="skz")

    def ln_group(g):
        for ts in range(GT):
            ti = g * GT + ts
            xb, xt = xg[ts % 2], xtk[ts % 2]
            dma("sp", xb[:, :], d_x[ti * 128:(ti + 1) * 128, :],
                writes=[xb], tag=f"ldx{ti}")
            ln_tile(ti, xb, xt, ts * 128, "ln")

    for g in range(NG):
        if g == 0:
            ln_group(0)
        proj_fmajor(wk, bk_sb, kT, f"pk{g}")
        proj_fmajor(wq, bq_sb, pT, f"pq{g}")
        proj_v(g, f"pv{g}")
        if g + 1 < NG:
            ln_group(g + 1)
        features("k", g, f"fk{g}")
        features("q", g, f"fq{g}")

    skrow = sbt(p1, "skrow", [1, 64 * H], F32)
    stab2 = sbt(p1, "stab2", [1, H], F32)
    dma("sp", d_skd.ap(), stk_run[:, :], reads=[stk_run], writes=[d_skd],
        tag="skst")
    for hf in range(2):
        dma("sp", skrow.ap(),
            d_skd[hf * 64:(hf + 1) * 64, :].rearrange("t h -> (t h)").unsqueeze(0),
            reads=[d_skd], writes=[skrow], tag=f"skfl{hf}")
        dst = stabR if hf == 0 else stab2
        P.op("dve", lambda v, dst=dst: v.reduce_max(
            dst[:, :], skrow.ap().rearrange("one (t h) -> one h t", t=64),
            axis=AX.X), reads=[skrow], writes=[dst], tag=f"sktr{hf}")
    P.op("dve", lambda v: v.tensor_max(stabR[:, :], stabR[:, :], stab2[:, :]),
         reads=[stabR, stab2], writes=[stabR], tag="sktm2")
    tap("stk_run", stk_run, stk_run[:, :])
    tap("stabL", stabR, stabR[:, :])
    dma("sp", d_cc1i.ap(), stabR[:, :], reads=[stabR], writes=[d_cc1i],
        tag="cc1w")
    P.op("pool", lambda gp: gp.collective_compute(
        "AllReduce", ALU.max, replica_groups=[list(range(NCORES))],
        ins=[d_cc1i.ap()], outs=[d_cc1o.ap()]),
        reads=[d_cc1i], writes=[d_cc1o], kind="cc", tag="AR1")
    row_st = sbt(p1, "row_st", [1, H], F32)
    dma("sp", row_st[:, :], d_cc1o.ap(), reads=[d_cc1o], writes=[row_st],
        tag="cc1r")
    P.op("pe", lambda t: t.matmul(pf[5][:, 0:H], onesb[0:1, :], row_st[0:1, :],
                                  start=True, stop=True),
         reads=[onesb, row_st], writes=[pf[5]], tag="stbm")
    P.op("dve", lambda v: v.tensor_copy(stabgb[:, :], pf[5][:, 0:H]),
         reads=[pf[5]], writes=[stabgb], tag="stbe")
    tap("stabgb", stabgb, stabgb[0:1, :])

    p1.close()
    P.barrier()

    p2 = ExitStack()
    combk = [sbt(p2, f"combk{i}", [128, H], F32) for i in range(2)]
    kddl = [sbt(p2, f"kddl{i}", [128, D], BF16) for i in range(2)]
    e1 = [sbt(p2, f"e1_{i}", [128, D], F32) for i in range(2)]
    e2t = [sbt(p2, f"e2t{i}", [128, D], BF16) for i in range(2)]
    kpt = [sbt(p2, f"kpt{t}", [128, D], BF16) for t in range(NT)]
    ar2st = sbt(p2, "ar2st", [J, H * 65], F32)
    kvs = sbt(p2, "kvs", [J, H * 65], F32)

    for ti in range(NT):
        par = ti % 2
        kdd = kddl[par]
        dma("sp", kdd[:, :], d_ddk[ti, :, :], reads=[ddk_reg[ti // GT]], writes=[kdd],
            tag=f"kld{ti}")
        P.op("dve", lambda v, ti=ti, par=par: v.scalar_tensor_tensor(
            combk[par][:, :], dgk[ti][:, :], COMB_C, stabgb[:, :],
            ALU.mult, ALU.add),
            reads=[dgk[ti], stabgb], writes=[combk[par]], tag=f"kc{ti}")
        P.op("dve", lambda v, par=par, kdd=kdd: v.tensor_tensor(
            e1[par].ap().rearrange("p (h j) -> p h j", j=64),
            kdd.ap().rearrange("p (h j) -> p h j", j=64),
            combk[par].ap().unsqueeze(2).broadcast_to([128, H, 64]),
            ALU.subtract),
            reads=[kdd, combk[par]], writes=[e1[par]], tag=f"ku{ti}")
        P.op("act", lambda a, par=par: a.activation(e2t[par][:, :],
                                                    e1[par][:, :], AF.Exp,
                                                    scale=A_NORM),
             reads=[e1[par]], writes=[e2t[par]], tag=f"kx{ti}")
        P.op("dve", lambda v, ti=ti, par=par: v.tensor_scalar_add(
            kpt[ti][:, :], e2t[par][:, :], EPS_Q),
            reads=[e2t[par]], writes=[kpt[ti]], tag=f"kp{ti}")

    def ubank(h):
        if h < 12:
            return pf[h // 2], (h % 2) * 130
        return pf[h - 12], 260

    for h in range(H):
        bank, col = ubank(h)
        for ti in range(NT):
            P.op("pe", lambda t, ti=ti, h=h, bank=bank, col=col: t.matmul(
                bank[0:J, col:col + 65], kpt[ti][:, h * 64:(h + 1) * 64],
                vtk[ti][:, h * 65:(h + 1) * 65],
                start=(ti == 0), stop=(ti == NT - 1)),
                reads=[kpt[ti], vtk[ti]], writes=[bank], tag=f"U{h}_{ti}")
    for h in range(H):
        bank, col = ubank(h)
        P.op("dve", lambda v, h=h, bank=bank, col=col: v.tensor_copy(
            ar2st[:, h * 65:(h + 1) * 65], bank[0:J, col:col + 65]),
            reads=[bank], writes=[ar2st], tag=f"Ue{h}")
    dma("sp", d_cc2i.ap(), ar2st[:, :], reads=[ar2st], writes=[d_cc2i],
        tag="cc2w")
    P.op("pool", lambda gp: gp.collective_compute(
        "AllReduce", ALU.add, replica_groups=[list(range(NCORES))],
        ins=[d_cc2i.ap()], outs=[d_cc2o.ap()]),
        reads=[d_cc2i], writes=[d_cc2o], kind="cc", tag="AR2")
    dma("sp", kvs[:, :], d_cc2o.ap(), reads=[d_cc2o], writes=[kvs], tag="cc2r")
    tap("kpt0", kpt[0], kpt[0][:, :])
    tap("ar2st", ar2st, ar2st[:, :])
    tap("kvs", kvs, kvs[:, :])

    for c in range(NCH):
        P.op("dve", lambda v, c=c: v.memset(kvbd[c][:, :], 0.0),
             writes=[kvbd[c]], tag=f"kz{c}")
        P.op("dve", lambda v, c=c: v.memset(ksl[c][:, :], 0.0),
             writes=[ksl[c]], tag=f"sz{c}")
        for s in range(2):
            h = 2 * c + s
            P.op("dve", lambda v, c=c, s=s, h=h: v.tensor_copy(
                kvbd[c][s * 64:(s + 1) * 64, s * 64:(s + 1) * 64],
                kvs[:, h * 65:h * 65 + 64]),
                reads=[kvs], writes=[kvbd[c]], tag=f"kc{c}_{s}")
            P.op("dve", lambda v, c=c, s=s, h=h: v.tensor_copy(
                ksl[c][s * 64:(s + 1) * 64, h:h + 1],
                kvs[:, h * 65 + 64:(h + 1) * 65]),
                reads=[kvs], writes=[ksl[c]], tag=f"sc{c}_{s}")

    p2.close()
    P.barrier()
    sB.close()
    P.barrier()

    p3 = ExitStack()
    ebcc = sbt(p3, "ebcc", [16, NCH * 128], F32)
    row_bo = sbt(p3, "row_bo", [1, D], F32)
    row_g = sbt(p3, "row_g", [1, D], F32)
    row_b = sbt(p3, "row_b", [1, D], F32)
    bob = sbt(p3, "bob", [128, D], F32)
    gmb = sbt(p3, "gmb", [128, D], F32)
    btb = sbt(p3, "btb", [128, D], F32)
    dma("sp", ebcc.ap().rearrange("p (c n) -> p c n", c=NCH),
        d_ebc.ap().rearrange("c p n -> p c n"), writes=[ebcc], tag="ebc")
    dma("sp", row_bo[:, :], d_bo.ap().unsqueeze(0), writes=[row_bo], tag="rbo")
    dma("sp", row_g[:, :], d_gamma.ap().unsqueeze(0), writes=[row_g], tag="rg")
    dma("sp", row_b[:, :], d_beta.ap().unsqueeze(0), writes=[row_b], tag="rb")
    bcast_row(row_bo, bob, "bo")
    bcast_row(row_g, gmb, "g")
    bcast_row(row_b, btb, "b")
    wo = sbt(p3, "wo", [128, NCH * D], BF16)
    w_load(wo, d_woT, "wo")
    rw = sbt(p3, "rw", [128, NCH * NE], BF16)
    dma("pool", rw.ap().rearrange("p (c n) -> p c n", c=NCH),
        d_rwT.ap().rearrange("(c p) n -> p c n", p=128), writes=[rw], tag="rw")

    for c in range(NCH):
        tap(f"ksl{c}", ksl[c], ksl[c][:, :])
    tap("kvbd0", kvbd[0], kvbd[0][:, :])
    tap("kvbd1", kvbd[1], kvbd[1][:, :])
    attnT = [sbt(p3, f"attnT{c}", [128, 512], BF16) for c in range(NCH)]
    xnTg = [[sbt(p3, f"xnTg{c}_{i}", [128, 512], BF16) for c in range(NCH)]
            for i in range(2)]
    aog = [sbt(p3, f"aog{i}", [128, D], F32) for i in range(GT)]
    y1g = [sbt(p3, f"y1g{i}", [128, D], F32) for i in range(GT)]
    bn2g = sbt(p3, "bn2g", [128, GT, 2], F32)
    bn2h = sbt(p3, "bn2h", [128, GT, 2], F32)
    rstdg = sbt(p3, "rstdg", [128, GT], F32)
    rs1g = sbt(p3, "rs1g", [128, GT], F32)
    v2g = sbt(p3, "v2g", [128, GT], F32)
    rstd2g = sbt(p3, "rstd2g", [128, GT], F32)
    rp = sbt(p3, "rp", [128, GT * NE], F32)
    m1g = sbt(p3, "m1g", [128, GT], F32)
    sumg = sbt(p3, "sumg", [128, GT], F32)
    eqg = sbt(p3, "eqg", [128, GT * NE], F32)
    eqg2 = sbt(p3, "eqg2", [128, GT * NE], F32)
    sc1g = sbt(p3, "sc1g", [128, GT * NE], F32)
    s1mg = sbt(p3, "s1mg", [128, GT], F32)
    f1g = sbt(p3, "f1g", [128, GT * NE], F32)
    mskg = sbt(p3, "mskg", [128, GT * NE], F32)
    gvg = sbt(p3, "gvg", [128, GT * NE], F32)
    gvg4 = sbt(p3, "gvg4", [128, NG * GT * NE], F32)
    rnorm = sbt(p3, "rnorm", [16, 512], F32)
    rbf = sbt(p3, "rbf", [128, 512], F32)
    zt = [sbt(p3, f"zt{i}", [128, D], F32) for i in range(GT)]
    xntk = [[sbt(p3, f"xntk{i}_{j}", [128, D], BF16) for j in range(GT)]
            for i in range(2)]
    xtmp = [sbt(p3, f"xtmp{i}", [128, D], F32) for i in range(2)]
    bn6b = [sbt(p3, f"bn6b{i}", [128, 2, 6], F32) for i in range(2)]
    xg3 = [sbt(p3, f"xg3_{i}", [128, D], F32) for i in range(GT)]

    pending_gate = []
    pending_xn = []
    for g in range(NG):
        for c in range(NCH):
            P.op("pe", lambda t, c=c, g=g: t.matmul(
                pf[2][0:16, :], ksl[c][:, :], qpT[c][:, g * 512:(g + 1) * 512],
                start=(c == 0), stop=(c == NCH - 1)),
                reads=[ksl[c], qpT[c]], writes=[pf[2]], tag=f"n{g}_{c}")
        if pending_xn:
            pending_xn.pop(0)()
        if len(pending_gate) > 1:
            pending_gate.pop(0)()
        P.op("dve", lambda v: v.tensor_scalar_max(rnorm[:, :], pf[2][0:16, :],
                                                  DEN_CLIP),
             reads=[pf[2]], writes=[rnorm], tag=f"nx{g}")
        if g == 0:
            tap("normraw", rnorm, rnorm[:, :])
        P.op("dve", lambda v: v.reciprocal(rnorm[:, :], rnorm[:, :]),
             reads=[rnorm], writes=[rnorm], tag=f"nr{g}")
        for c in range(NCH):
            P.op("pe", lambda t, c=c, g=g: t.matmul(
                pf[3][:, :], kvbd[c][:, :], qpT[c][:, g * 512:(g + 1) * 512],
                start=True, stop=True),
                reads=[kvbd[c], qpT[c]], writes=[pf[3]], tag=f"qk{g}_{c}")
            P.op("pe", lambda t, c=c: t.matmul(
                pf[4][:, :], ebcc[:, c * 128:(c + 1) * 128], rnorm[0:16, :],
                start=True, stop=True),
                reads=[ebcc, rnorm], writes=[pf[4]], tag=f"rb{g}_{c}")
            P.op("dve", lambda v: v.tensor_copy(rbf[:, :], pf[4][:, :]),
                 reads=[pf[4]], writes=[rbf], tag=f"re{g}_{c}")
            P.op("dve", lambda v, c=c: v.tensor_mul(attnT[c][:, :], pf[3][:, :],
                                                    rbf[:, :]),
                 reads=[pf[3], rbf], writes=[attnT[c]], tag=f"at{g}_{c}")
        if g == 0:
            tap("rnorm", rnorm, rnorm[:, :])
            tap("attnT0", attnT[0], attnT[0][:, :])
            tap("attnT7", attnT[7], attnT[7][:, :])
        for ts in range(GT):
            ti = g * GT + ts
            for ob in range(2):
                bank = pf[(ts % 2) * 2 + ob]
                for k in range(NCH):
                    P.op("pe", lambda t, ts=ts, ob=ob, k=k, bank=bank: t.matmul(
                        bank[:, :], attnT[k][:, ts * 128:(ts + 1) * 128],
                        wo[:, k * D + ob * 512: k * D + (ob + 1) * 512],
                        start=(k == 0), stop=(k == NCH - 1)),
                        reads=[attnT[k], wo], writes=[bank],
                        tag=f"w{ti}_{ob}_{k}")
                P.op("dve", lambda v, ts=ts, ob=ob, bank=bank: v.tensor_add(
                    aog[ts][:, ob * 512:(ob + 1) * 512], bank[:, :],
                    bob[:, ob * 512:(ob + 1) * 512]),
                    reads=[bank, bob], writes=[aog[ts]], tag=f"we{ti}_{ob}")
            dma("sp", xg3[ts][:, :], d_x[ti * 128:(ti + 1) * 128, :],
                writes=[xg3[ts]], tag=f"lx3{ti}")
        for ts in range(GT):
            b6 = bn6b[ts % 2]
            P.op("dve", lambda v, ts=ts, b6=b6: v.bn_stats(
                b6[:, 0, :], aog[ts][:, 0:512]),
                reads=[aog[ts]], writes=[b6], tag=f"l1a{g}_{ts}")
            P.op("dve", lambda v, ts=ts, b6=b6: v.bn_stats(
                b6[:, 1, :], aog[ts][:, 512:1024]),
                reads=[aog[ts]], writes=[b6], tag=f"l1b{g}_{ts}")
            P.op("dve", lambda v, ts=ts, b6=b6: v.bn_aggr(
                bn2g[:, ts, :], b6[:, :, :]),
                reads=[b6], writes=[bn2g], tag=f"l1c{g}_{ts}")
        P.op("act", lambda a: a.activation(
            rstdg[:, :], bn2g[:, :, 1], AF.Ln, bias=epst[:, 0:1]),
            reads=[bn2g, epst], writes=[rstdg], tag=f"l1d{g}")
        P.op("act", lambda a: a.activation(rstdg[:, :], rstdg[:, :], AF.Exp,
                                           scale=-0.5),
             reads=[rstdg], writes=[rstdg], tag=f"l1e{g}")
        for ts in range(GT):
            ti = g * GT + ts
            P.op("dve", lambda v, ts=ts: v.tensor_scalar(
                y1g[ts][:, :], aog[ts][:, :], bn2g[:, ts, 0:1],
                rstdg[:, ts:ts + 1], ALU.subtract, ALU.mult),
                reads=[aog[ts], bn2g, rstdg], writes=[y1g[ts]],
                tag=f"l1n{ti}")
            P.op("dve", lambda v, ts=ts: v.tensor_add(
                y1g[ts][:, :], y1g[ts][:, :], xg3[ts][:, :]),
                reads=[y1g[ts], xg3[ts]], writes=[y1g[ts]], tag=f"y1{ti}")
            dma("sp", d_y1[ti * 128:(ti + 1) * 128, :], y1g[ts][:, :],
                reads=[y1g[ts]], writes=[y1_reg[ti // GT]], tag=f"sy1{ti}")
        for ts in range(GT):
            b6 = bn6b[ts % 2]
            P.op("dve", lambda v, ts=ts, b6=b6: v.bn_stats(
                b6[:, 0, :], y1g[ts][:, 0:512]),
                reads=[y1g[ts]], writes=[b6], tag=f"l2a{g}_{ts}")
            P.op("dve", lambda v, ts=ts, b6=b6: v.bn_stats(
                b6[:, 1, :], y1g[ts][:, 512:1024]),
                reads=[y1g[ts]], writes=[b6], tag=f"l2b{g}_{ts}")
            P.op("dve", lambda v, ts=ts, b6=b6: v.bn_aggr(
                bn2h[:, ts, :], b6[:, :, :]),
                reads=[b6], writes=[bn2h], tag=f"l2c{g}_{ts}")
        P.op("act", lambda a: a.activation(
            rs1g[:, :], bn2h[:, :, 1], AF.Ln, bias=epst[:, 0:1]),
            reads=[bn2h, epst], writes=[rs1g], tag=f"l2d{g}")
        P.op("act", lambda a: a.activation(rs1g[:, :], rs1g[:, :], AF.Exp,
                                           scale=-0.5),
             reads=[rs1g], writes=[rs1g], tag=f"l2e{g}")
        P.op("dve", lambda v, g=g: v.tensor_copy(
            mu1[:, g * GT:(g + 1) * GT], bn2h[:, :, 0]),
            reads=[bn2h], writes=[mu1], tag=f"m1s{g}")
        P.op("dve", lambda v, g=g: v.tensor_copy(
            rs1[:, g * GT:(g + 1) * GT], rs1g[:, :]),
            reads=[rs1g], writes=[rs1], tag=f"r1s{g}")
        for ts in range(GT):
            ti = g * GT + ts
            par = ts % 2
            P.op("dve", lambda v, ts=ts: v.tensor_scalar(
                zt[ts][:, :], y1g[ts][:, :], bn2h[:, ts, 0:1],
                rs1g[:, ts:ts + 1], ALU.subtract, ALU.mult),
                reads=[y1g[ts], bn2h, rs1g], writes=[zt[ts]], tag=f"z{ti}")
            P.op("act", lambda a, ts=ts: a.activation(
                aog[ts][:, :], zt[ts][:, :], AF.Square,
                accum_out=v2g[:, ts:ts + 1]),
                reads=[zt[ts]], writes=[aog[ts], v2g], tag=f"v2{ti}")
        P.op("act", lambda a: a.activation(rstd2g[:, :], v2g[:, :], AF.Ln,
                                           bias=epst[:, 0:1], scale=1.0 / D),
             reads=[v2g, epst], writes=[rstd2g], tag=f"r2{g}")
        P.op("act", lambda a: a.activation(rstd2g[:, :], rstd2g[:, :], AF.Exp,
                                           scale=-0.5),
             reads=[rstd2g], writes=[rstd2g], tag=f"r2r{g}")
        for ts in range(GT):
            ti = g * GT + ts
            xnp = xntk[g % 2][ts]
            P.op("dve", lambda v, ts=ts, xnp=xnp: v.scalar_tensor_tensor(
                xtmp[ts % 2][:, :], zt[ts][:, :], rstd2g[:, ts:ts + 1],
                gmb[:, :], ALU.mult, ALU.mult),
                reads=[zt[ts], rstd2g, gmb], writes=[xtmp[ts % 2]],
                tag=f"xa{ti}")
            P.op("dve", lambda v, ts=ts, xnp=xnp: v.tensor_add(
                xnp[:, :], xtmp[ts % 2][:, :], btb[:, :]),
                reads=[xtmp[ts % 2], btb], writes=[xnp], tag=f"xb{ti}")

        def xn_block(g=g):
            xng = xnTg[g % 2]
            for ts in range(GT):
                ti = g * GT + ts
                par = ts % 2
                xnp = xntk[g % 2][ts]
                pbt = pb[par]
                for c in range(NCH):
                    P.op("pe", lambda t, c=c, xnp=xnp, pbt=pbt: t.transpose(
                        pbt[:, c * 128:(c + 1) * 128],
                        xnp[:, c * 128:(c + 1) * 128], idb[:, :]),
                        reads=[xnp, idb], writes=[pbt], tag=f"xt{ti}_{c}")
                for c in range(NCH):
                    P.op("act", lambda a, c=c, ts=ts, pbt=pbt, xng=xng: a.copy(
                        xng[c][:, ts * 128:(ts + 1) * 128],
                        pbt[:, c * 128:(c + 1) * 128]),
                        reads=[pbt], writes=[xng[c]], tag=f"xe{ti}_{c}")
                for c in range(NCH):
                    P.op("pe", lambda t, c=c, ts=ts, xng=xng: t.matmul(
                        pf[5][:, ts * NE:(ts + 1) * NE],
                        xng[c][:, ts * 128:(ts + 1) * 128],
                        rw[:, c * NE:(c + 1) * NE],
                        start=(c == 0), stop=(c == NCH - 1)),
                        reads=[xng[c], rw], writes=[pf[5]], tag=f"rt{ti}_{c}")
            rp3 = rp.ap().rearrange("p (t e) -> p t e", e=NE)
            P.op("dve", lambda v: v.tensor_scalar(
                rp[:, :], pf[5][:, 0:GT * NE], CLAMP, -CLAMP, ALU.min, ALU.max),
                reads=[pf[5]], writes=[rp], tag=f"lg{g}")
            P.op("dve", lambda v: v.reduce_max(m1g[:, :], rp3, axis=AX.X),
                 reads=[rp], writes=[m1g], tag=f"lm{g}")
            P.op("dve", lambda v: v.tensor_tensor(
                rp3, rp3, m1g.ap().unsqueeze(2).broadcast_to([128, GT, NE]),
                ALU.subtract),
                reads=[rp, m1g], writes=[rp], tag=f"ls{g}")
            P.op("act", lambda a: a.activation(rp[:, :], rp[:, :], AF.Exp),
                 reads=[rp], writes=[rp], tag=f"px{g}")
            P.op("dve", lambda v: v.reduce_sum(sumg[:, :], rp3, axis=AX.X),
                 reads=[rp], writes=[sumg], tag=f"p0{g}")
            P.op("dve", lambda v: v.tensor_scalar_add(sumg[:, :], sumg[:, :],
                                                      ROUTER_EPS),
                 reads=[sumg], writes=[sumg], tag=f"p1{g}")
            P.op("dve", lambda v: v.reciprocal(sumg[:, :], sumg[:, :]),
                 reads=[sumg], writes=[sumg], tag=f"p2{g}")
            P.op("dve", lambda v: v.tensor_tensor(
                rp3, rp3, sumg.ap().unsqueeze(2).broadcast_to([128, GT, NE]),
                ALU.mult),
                reads=[rp, sumg], writes=[rp], tag=f"p3{g}")
            P.op("dve", lambda v: v.tensor_scalar(
                rp[:, :], rp[:, :], 1.0, ROUTER_EPS, ALU.min, ALU.max),
                reads=[rp], writes=[rp], tag=f"p4{g}")

            def top2_sel(dst_mask, srcbuf, tag2):
                s3 = srcbuf.ap().rearrange("p (t e) -> p t e", e=NE)
                P.op("dve", lambda v: v.reduce_max(m1g[:, :], s3, axis=AX.X),
                     reads=[srcbuf], writes=[m1g], tag=f"{tag2}a")
                P.op("dve", lambda v: v.tensor_tensor(
                    eqg.ap().rearrange("p (t e) -> p t e", e=NE), s3,
                    m1g.ap().unsqueeze(2).broadcast_to([128, GT, NE]),
                    ALU.is_equal),
                    reads=[srcbuf, m1g], writes=[eqg], tag=f"{tag2}b")
                P.op("dve", lambda v: v.tensor_mul(sc1g[:, :], eqg[:, :],
                                                   riota[:, :]),
                     reads=[eqg, riota], writes=[sc1g], tag=f"{tag2}c")
                P.op("dve", lambda v: v.reduce_max(
                    s1mg[:, :], sc1g.ap().rearrange("p (t e) -> p t e", e=NE),
                    axis=AX.X),
                    reads=[sc1g], writes=[s1mg], tag=f"{tag2}d")
                P.op("dve", lambda v: v.tensor_tensor(
                    dst_mask.ap().rearrange("p (t e) -> p t e", e=NE),
                    sc1g.ap().rearrange("p (t e) -> p t e", e=NE),
                    s1mg.ap().unsqueeze(2).broadcast_to([128, GT, NE]),
                    ALU.is_equal),
                    reads=[sc1g, s1mg], writes=[dst_mask], tag=f"{tag2}e")

            top2_sel(f1g, rp, f"a{g}_")
            P.op("dve", lambda v: v.scalar_tensor_tensor(
                mskg[:, :], f1g[:, :], -1e30, rp[:, :], ALU.mult, ALU.add),
                reads=[f1g, rp], writes=[mskg], tag=f"mk{g}")
            top2_sel(eqg2, mskg, f"b{g}_")
            P.op("dve", lambda v: v.tensor_add(f1g[:, :], f1g[:, :],
                                               eqg2[:, :]),
                 reads=[f1g, eqg2], writes=[f1g], tag=f"fm{g}")
            P.op("dve", lambda v: v.tensor_mul(gvg[:, :], rp[:, :], f1g[:, :]),
                 reads=[rp, f1g], writes=[gvg], tag=f"gv{g}")
            P.op("dve", lambda v, g=g: v.tensor_copy(
                gvg4[:, g * 32:(g + 1) * 32], gvg[:, :]),
                reads=[gvg], writes=[gvg4], tag=f"gvs{g}")
            for c in range(NCH):
                dma("sp", d_xnT[c, :, g * 512:(g + 1) * 512],
                    xnTg[g % 2][c][:, :],
                    reads=[xnTg[g % 2][c]], writes=[xnT_reg[g]],
                    tag=f"sxn{g}_{c}")

        def gate_flush(g=g):
            for ts in range(GT):
                ti = g * GT + ts
                P.op("pe", lambda t, g=g, ts=ts: t.transpose(
                    pf[5][0:NE, 256:256 + 128],
                    gvg4[:, g * 32 + ts * NE:g * 32 + (ts + 1) * NE],
                    idf[:, :]),
                    reads=[gvg4, idf], writes=[pf[5]], tag=f"gt{ti}")
                P.op("act", lambda a, ti=ti: a.copy(
                    gateT[:, ti * 128:(ti + 1) * 128],
                    pf[5][0:NE, 256:256 + 128]),
                    reads=[pf[5]], writes=[gateT], tag=f"ge{ti}")

        pending_xn.append(xn_block)
        pending_gate.append(gate_flush)

    while pending_xn:
        pending_xn.pop(0)()
    while pending_gate:
        pending_gate.pop(0)()
    p3.close()
    sK.close()
    sA.close()
    P.barrier()

    p4 = ExitStack()
    TH = T // 2
    guw = [sbt(p4, f"guw{i}", [128, NCH * 256], BF16) for i in range(4)]
    xnTh = [sbt(p4, f"xnTh{c}", [128, T // 2], BF16) for c in range(NCH)]
    dnw = [sbt(p4, f"dnw{i}", [128, 16 * 512], BF16) for i in range(2)]
    grow = [sbt(p4, f"grow{i}", [1, TH], F32) for i in range(2)]
    gbt = sbt(p4, "gbt", [128, TH], F32)
    xet = [sbt(p4, f"xet{c}", [128, TH], BF16) for c in range(NCH)]
    hid = [sbt(p4, f"hid{c}", [128, TH], BF16) for c in range(16)]
    acc = [sbt(p4, f"acc{i}", [128, D], F32) for i in range(8)]
    s1t = [sbt(p4, f"s1t{i}", [128, 512], BF16) for i in range(2)]
    y1l = sbt(p4, "y1l", [128, D], F32)
    zt4 = sbt(p4, "zt4", [128, D], F32)
    fr4 = sbt(p4, "fr4", [128, D], F32)
    bn64 = sbt(p4, "bn64", [128, 2, 6], F32)
    bn24 = sbt(p4, "bn24", [128, 2], F32)
    rstd4 = sbt(p4, "rstd4", [128, 1], F32)
    y2t = sbt(p4, "y2t", [128, D], F32)

    def fin_tile(half, tt):
        ti = half * 8 + tt
        dma("act", y1l[:, :], d_y1[ti * 128:(ti + 1) * 128, :],
            reads=[y1_reg[ti // GT]], writes=[y1l], tag=f"ly{ti}")
        P.op("dve", lambda v, ti=ti: v.tensor_scalar(
            zt4[:, :], y1l[:, :], mu1[:, ti:ti + 1], rs1[:, ti:ti + 1],
            ALU.subtract, ALU.mult),
            reads=[y1l, mu1, rs1], writes=[zt4], tag=f"fz{ti}")
        P.op("dve", lambda v, tt=tt: v.tensor_add(fr4[:, :], zt4[:, :],
                                                  acc[tt][:, :]),
             reads=[zt4, acc[tt]], writes=[fr4], tag=f"ff{ti}")
        P.op("dve", lambda v: v.bn_stats(bn64[:, 0, :], fr4[:, 0:512]),
             reads=[fr4], writes=[bn64], tag=f"f0{ti}")
        P.op("dve", lambda v: v.bn_stats(bn64[:, 1, :], fr4[:, 512:1024]),
             reads=[fr4], writes=[bn64], tag=f"f1{ti}")
        P.op("dve", lambda v: v.bn_aggr(bn24[:, :], bn64[:, :, :]),
             reads=[bn64], writes=[bn24], tag=f"f2{ti}")
        P.op("act", lambda a: a.activation(rstd4[:, :], bn24[:, 1:2],
                                           AF.Ln, bias=epst[:, 0:1]),
             reads=[bn24, epst], writes=[rstd4], tag=f"f3{ti}")
        P.op("act", lambda a: a.activation(rstd4[:, :], rstd4[:, 0:1],
                                           AF.Exp, scale=-0.5),
             reads=[rstd4], writes=[rstd4], tag=f"f4{ti}")
        P.op("dve", lambda v: v.tensor_scalar(
            fr4[:, :], fr4[:, :], bn24[:, 0:1], rstd4[:, 0:1],
            ALU.subtract, ALU.mult),
            reads=[fr4, bn24, rstd4], writes=[fr4], tag=f"f5{ti}")
        P.op("dve", lambda v: v.tensor_add(y2t[:, :], y1l[:, :], fr4[:, :]),
             reads=[y1l, fr4], writes=[y2t], tag=f"f6{ti}")
        dma("act", d_out[ti * 128:(ti + 1) * 128, :], y2t[:, :],
            reads=[y2t], writes=[d_out], tag=f"so{ti}")

    for half in range(2):
        hc = half * TH
        for c in range(NCH):
            dma("sp", xnTh[c][:, :], d_xnT[c, :, hc:hc + TH],
                reads=[xnT_reg[half * 2], xnT_reg[half * 2 + 1]],
                writes=[xnTh[c]], tag=f"lxn{half}_{c}")
        for e in range(NE):
            for pbi in range(4):
                wgu = guw[pbi % 4]
                dma("pool", wgu.ap().rearrange("p (c n) -> p c n", c=NCH),
                    d_guT[e, :, pbi, :].rearrange("(c p) n -> p c n", p=128),
                    writes=[wgu], tag=f"lg{half}_{e}_{pbi}")
            gr = grow[e % 2]
            dma("sp", gr[:, :], gateT[e:e + 1, hc:hc + TH], reads=[gateT],
                writes=[gr], tag=f"gr{half}_{e}")
            for gs in range(2):
                P.op("pe", lambda t, gr=gr, gs=gs: t.matmul(
                    pf[4][:, :], onesb[0:1, :], gr[0:1, gs * 512:(gs + 1) * 512],
                    start=True, stop=True),
                    reads=[onesb, gr], writes=[pf[4]], tag=f"gb{half}_{e}_{gs}")
                P.op("act", lambda a, gs=gs: a.copy(
                    gbt[:, gs * 512:(gs + 1) * 512], pf[4][:, :]),
                    reads=[pf[4]], writes=[gbt], tag=f"gc{half}_{e}_{gs}")
            for c in range(NCH):
                P.op("pool", lambda g_, c=c: g_.tensor_mul(
                    xet[c][:, :], xnTh[c][:, :], gbt[:, :]),
                    reads=[xnTh[c], gbt], writes=[xet[c]],
                    tag=f"x{half}_{e}_{c}")
            if half == 0 and e == 0:
                tap("gbt0", gbt, gbt[:, :])
                tap("xet0", xet[0], xet[0][:, :])
            for pbi in range(16):
                wgu = guw[pbi % 4]
                for gs in range(2):
                    bg = pf[gs * 2]
                    bu = pf[gs * 2 + 1]
                    for k in range(NCH):
                        P.op("pe", lambda t, k=k, wgu=wgu, gs=gs, bg=bg: t.matmul(
                            bg[:, :], wgu[:, k * 256:k * 256 + 128],
                            xet[k][:, gs * 512:(gs + 1) * 512],
                            start=(k == 0), stop=(k == NCH - 1)),
                            reads=[wgu, xet[k]], writes=[bg],
                            tag=f"mg{half}_{e}_{pbi}_{gs}_{k}")
                    for k in range(NCH):
                        P.op("pe", lambda t, k=k, wgu=wgu, gs=gs, bu=bu: t.matmul(
                            bu[:, :], wgu[:, k * 256 + 128:(k + 1) * 256],
                            xet[k][:, gs * 512:(gs + 1) * 512],
                            start=(k == 0), stop=(k == NCH - 1)),
                            reads=[wgu, xet[k]], writes=[bu],
                            tag=f"mu{half}_{e}_{pbi}_{gs}_{k}")
                    st = s1t[gs]
                    P.op("act", lambda a, st=st, bg=bg: a.activation(
                        st[:, :], bg[:, :], AF.Silu),
                        reads=[bg], writes=[st], tag=f"si{half}_{e}_{pbi}_{gs}")
                    P.op("dve", lambda v, st=st, bu=bu, pbi=pbi, gs=gs:
                         v.tensor_mul(hid[pbi][:, gs * 512:(gs + 1) * 512],
                                      st[:, :], bu[:, :]),
                         reads=[st, bu], writes=[hid[pbi]],
                         tag=f"hm{half}_{e}_{pbi}_{gs}")
                if pbi + 4 < 16:
                    nxt = guw[(pbi + 4) % 4]
                    dma("pool", nxt.ap().rearrange("p (c n) -> p c n", c=NCH),
                        d_guT[e, :, pbi + 4, :]
                        .rearrange("(c p) n -> p c n", p=128),
                        writes=[nxt], tag=f"lg{half}_{e}_{pbi + 4}")
            for ob in range(2):
                dma("pool", dnw[ob].ap().rearrange("p (c n) -> p c n", c=16),
                    d_dnT[e, :, ob * 512:(ob + 1) * 512]
                    .rearrange("(c p) n -> p c n", p=128),
                    writes=[dnw[ob]], tag=f"ld{half}_{e}_{ob}")
            for tt in range(8):
                for ob in range(2):
                    bank = pf[4 + ob]
                    for k in range(16):
                        P.op("pe", lambda t, tt=tt, ob=ob, k=k, bank=bank:
                             t.matmul(bank[:, :],
                                      hid[k][:, tt * 128:(tt + 1) * 128],
                                      dnw[ob][:, k * 512:(k + 1) * 512],
                                      start=(k == 0), stop=(k == 15)),
                             reads=[hid[k], dnw[ob]], writes=[bank],
                             tag=f"md{half}_{e}_{tt}_{ob}_{k}")
                    if e == 0:
                        P.op("dve", lambda v, tt=tt, ob=ob, bank=bank:
                             v.tensor_copy(acc[tt][:, ob * 512:(ob + 1) * 512],
                                           bank[:, :]),
                             reads=[bank], writes=[acc[tt]],
                             tag=f"d0{half}_{tt}_{ob}")
                    else:
                        P.op("dve", lambda v, tt=tt, ob=ob, bank=bank:
                             v.tensor_add(acc[tt][:, ob * 512:(ob + 1) * 512],
                                          acc[tt][:, ob * 512:(ob + 1) * 512],
                                          bank[:, :]),
                             reads=[bank, acc[tt]], writes=[acc[tt]],
                             tag=f"da{half}_{e}_{tt}_{ob}")
                if e == NE - 1:
                    fin_tile(half, tt)
        if half == 0:
            tap("hid0", hid[0], hid[0][:, :])
            tap("acc0", acc[0], acc[0][:, :])

    p4.close()

    lane_totals = dict(P.lane_count)
    ccn = P.cc_count

    def make_final(lane_sems, cc_sem):
        def fn(h):
            last = None
            for ln in P.lane_ids:
                last = h.wait_ge(lane_sems[ln], 16 * lane_totals[ln])
            if ccn:
                last = h.wait_ge(cc_sem, ccn)
            return last
        return fn

    sem_ctx = ExitStack()
    s_pe = sem_ctx.enter_context(nc.semaphore("s_pe"))
    s_act = sem_ctx.enter_context(nc.semaphore("s_act"))
    s_dve = sem_ctx.enter_context(nc.semaphore("s_dve"))
    s_pool = sem_ctx.enter_context(nc.semaphore("s_pool"))
    s_sp = sem_ctx.enter_context(nc.semaphore("s_sp"))
    cc_sem = sem_ctx.enter_context(nc.semaphore("ccs"))
    lane_sems = {ln: sem_ctx.enter_context(nc.semaphore(f"lane{i}"))
                 for i, ln in enumerate(P.lane_ids)}
    with nc.Block() as block:
        sems = dict(pe=s_pe, act=s_act, dve=s_dve, pool=s_pool, sp=s_sp)
        fin = make_final(lane_sems, cc_sem)
        P.ops.append(dict(eng="sp", fn=fin, kind="compute", tag="final",
                          deps=set(), idx=len(P.ops), need_inc=False,
                          dma_dep_vals={}))
        P.eng_ops["sp"].append(len(P.ops) - 1)
        P.emit(block, sems, lane_sems, cc_sem)
    sem_ctx.close()

    ctx.close()
    return nc


_NC_CACHE = None


def _get_nc():
    global _NC_CACHE
    if _NC_CACHE is None:
        _NC_CACHE = build_program()
    return _NC_CACHE


def kernel(L0_attn_query, L0_attn_Wq, L0_attn_bq, L0_attn_Wk, L0_attn_bk,
           L0_attn_Wv, L0_attn_bv, L0_attn_Wo, L0_attn_bo, L0_attn_proj,
           L0_ffn_ln_gamma, L0_ffn_ln_beta, L0_ffn_router_w,
           L0_ffn_gate_up_w, L0_ffn_down_w):
    x = np.ascontiguousarray(np.asarray(L0_attn_query, dtype=np.float32)
                             .reshape(N_TOK, D))
    wqT = np.ascontiguousarray(np.asarray(L0_attn_Wq, np.float32).T)
    wkT = np.ascontiguousarray(np.asarray(L0_attn_Wk, np.float32).T)
    wvT = np.ascontiguousarray(np.asarray(L0_attn_Wv, np.float32).T)
    woT = np.ascontiguousarray(np.asarray(L0_attn_Wo, np.float32).T)
    proj = np.asarray(L0_attn_proj, np.float32)
    projbd = np.zeros((NCH, 128, 128), np.float32)
    for c in range(NCH):
        projbd[c, 0:64, 0:64] = proj[2 * c].T
        projbd[c, 64:128, 64:128] = proj[2 * c + 1].T
    rwT = np.ascontiguousarray(np.asarray(L0_ffn_router_w, np.float32).T)
    _gu = np.swapaxes(np.asarray(L0_ffn_gate_up_w, np.float32), 1, 2)
    _g = _gu[:, :, 0:DFF].reshape(NE, D, 16, 128)
    _u = _gu[:, :, DFF:].reshape(NE, D, 16, 128)
    guT = np.ascontiguousarray(np.concatenate([_g, _u], axis=3))
    dnT = np.ascontiguousarray(np.swapaxes(
        np.asarray(L0_ffn_down_w, np.float32), 1, 2))
    idf = np.eye(128, dtype=np.float32)
    riota = np.broadcast_to(np.tile(np.arange(NE, 0, -1, dtype=np.float32), 4),
                            (128, 4 * NE)).copy()
    e2 = np.zeros((NCH, 128, H), np.float32)
    ebc = np.zeros((NCH, H, 128), np.float32)
    for c in range(NCH):
        e2[c, 0:64, 2 * c] = 1.0
        e2[c, 64:128, 2 * c + 1] = 1.0
        ebc[c, 2 * c, 0:64] = 1.0
        ebc[c, 2 * c + 1, 64:128] = 1.0

    shared = dict(
        wqT=wqT, wkT=wkT, wvT=wvT, woT=woT,
        bq=np.asarray(L0_attn_bq, np.float32),
        bk=np.asarray(L0_attn_bk, np.float32),
        bv=np.asarray(L0_attn_bv, np.float32),
        bo=np.asarray(L0_attn_bo, np.float32),
        projbd=projbd,
        gamma=np.asarray(L0_ffn_ln_gamma, np.float32),
        beta=np.asarray(L0_ffn_ln_beta, np.float32),
        rwT=rwT, guT=guT, dnT=dnT, idf=idf, riota=riota, e2=e2, ebc=ebc,
    )
    in_maps = []
    for c in range(NCORES):
        m = dict(shared)
        m["xq"] = np.ascontiguousarray(x[c * T:(c + 1) * T, :])
        in_maps.append(m)

    nc = _get_nc()
    res = run_bass_kernel_spmd(nc, in_maps, core_ids=list(range(NCORES)))
    out = np.concatenate([res.results[c]["out"] for c in range(NCORES)], axis=0)
    return out.reshape(1, N_TOK, D)
